# version 18
# baseline (speedup 1.0000x reference)
import numpy as np
import ml_dtypes
import concourse.bass as bass
import concourse.mybir as mybir
from concourse.bass_utils import run_bass_kernel_spmd
from contextlib import ExitStack

F32 = mybir.dt.float32
BF16 = mybir.dt.bfloat16
AF = mybir.ActivationFunctionType
ALU = mybir.AluOpType
AX = mybir.AxisListType

NCORES = 8
T = 1024
NT = 8
D = 2048
NCH = 16
EPS0 = 1e-5
EPSF = 1e-6
TH_SL = 8192 // NCORES
AG_SL = D // NCORES
TORD = [7, 0, 1, 2, 3, 4, 5, 6]

STAGE = 3
DEBUG = 0


class Sched:
    ENGS = ("tensor", "vector", "scalar", "gpsimd", "sync")

    def __init__(self):
        self.items = {e: [] for e in self.ENGS}
        self.cnt = {}

    def op(self, eng, fn, sem=None, inc=0):
        self.items[eng].append(("op", fn, sem, inc))
        if sem is not None and inc:
            self.cnt[sem] = self.cnt.get(sem, 0) + inc
        return self.cnt.get(sem, 0) if sem else 0

    def wait(self, eng, sem, val):
        assert val <= self.cnt.get(sem, 0), f"wait {sem}>={val} unreachable ({self.cnt.get(sem, 0)})"
        self.items[eng].append(("wait", sem, val))

    def waitm(self, eng, mark):
        self.wait(eng, mark[0], mark[1])

    def mark(self, sem):
        return self.cnt.get(sem, 0)

    def emit(self, eng_name, eng, sems):
        for it in self.items[eng_name]:
            if it[0] == "wait":
                eng.wait_ge(sems[it[1]], it[2])
            else:
                inst = it[1](eng)
                if it[2] is not None and it[3]:
                    inst.then_inc(sems[it[2]], it[3])


def build(stage=3, debug=0):
    nc = bass.Bass("TRN2", target_bir_lowering=False)

    I = lambda n, s, d=F32: nc.dram_tensor(n, s, d, kind="ExternalInput").ap()
    x0h = I("x0", [T, D])
    wqh, wkh, wvh = I("wq", [D, D]), I("wk", [D, D]), I("wv", [D, D])
    woh, wgh = I("wo", [D, D]), I("wg", [D, D])
    d1wh, d2wh = I("d1w", [D, TH_SL]), I("d2w", [TH_SL, D])
    aggwh, bcwh = I("aggw", [D, AG_SL]), I("bcw", [D, AG_SL])
    d1bh = I("d1b", [1, TH_SL])
    d2bh = I("d2b", [1, D])
    aggbh, bcbh = I("aggb", [2, AG_SL]), I("bcb2", [2, AG_SL])
    gatebh = I("gateb", [1, D])
    gambh, betbh = I("gamb", [128, D]), I("betb", [128, D])
    pngth, pnbt2h = I("pngt", [128, NCH]), I("pnbt2", [128, 2 * NCH])
    ingth, inbt2h = I("ingt", [128, NCH]), I("inbt2", [128, 2 * NCH])
    idenh = I("iden", [128, 128], BF16)
    maskh = I("maskc", [128, 128])
    ones2h = I("ones2", [1, 2], BF16)
    ones128h = I("ones128", [1, 128], BF16)
    sel2h = I("sel2", [2, 128])

    xouth = nc.dram_tensor("xout", [T, D], F32, kind="ExternalOutput").ap()
    dbgbc = nc.dram_tensor("dbg_bc", [2, D], F32, kind="ExternalOutput").ap() if debug >= 2 else None

    x1scr = nc.dram_tensor("x1scr", [T, D], F32).ap()
    ag1_in = nc.dram_tensor("ag1_in", [1, D], F32).ap()
    ag1_out = nc.dram_tensor("ag1_out", [NCORES, D], F32, addr_space="Shared").ap()
    agg_in = nc.dram_tensor("agg_in", [2, AG_SL], F32).ap()
    agg_out = nc.dram_tensor("agg_out", [2 * NCORES, AG_SL], F32, addr_space="Shared").ap()
    ar_inh = [nc.dram_tensor(f"ar_in{s}", [2, D], F32).ap() for s in range(4)]
    ar_outh = [nc.dram_tensor(f"ar_out{s}", [2, D], F32, addr_space="Shared").ap() for s in range(4)]
    bc_inh = nc.dram_tensor("bc_in", [2, AG_SL], F32).ap()
    bc_outh = nc.dram_tensor("bc_out", [2 * NCORES, AG_SL], F32, addr_space="Shared").ap()

    with ExitStack() as stk:
        e = stk.enter_context
        trA = e(nc.sbuf_tensor("trA", [128, 16384], BF16))
        trQ = e(nc.sbuf_tensor("trQ", [128, 16384], BF16))
        trK = e(nc.sbuf_tensor("trK", [128, 16384], BF16))
        trV = e(nc.sbuf_tensor("trV", [128, 16384], BF16))
        xring = e(nc.sbuf_tensor("xring", [128, 2 * D], F32))
        cring = e(nc.sbuf_tensor("cring", [128, 2 * D], F32))
        wring = e(nc.sbuf_tensor("wring", [128, 8192], BF16))
        chsc = e(nc.sbuf_tensor("chsc", [128, 4096], BF16))
        afbf = e(nc.sbuf_tensor("afbf", [128, 2 * D], BF16))
        iden = e(nc.sbuf_tensor("idn", [128, 128], BF16))
        maskc = e(nc.sbuf_tensor("msk", [128, 128], F32))
        ones2 = e(nc.sbuf_tensor("o2", [1, 2], BF16))
        ones128 = e(nc.sbuf_tensor("o128", [1, 128], BF16))
        sel2 = e(nc.sbuf_tensor("sel2s", [2, 128], F32))
        pngt = e(nc.sbuf_tensor("pngts", [128, NCH], F32))
        pnbt2 = e(nc.sbuf_tensor("pnbt2s", [128, 2 * NCH], BF16))
        ingt = e(nc.sbuf_tensor("ingts", [128, NCH], F32))
        inbt2 = e(nc.sbuf_tensor("inbt2s", [128, 2 * NCH], BF16))
        d1bbf = e(nc.sbuf_tensor("d1bbf", [1, TH_SL], BF16))
        d2brow = e(nc.sbuf_tensor("d2brow", [1, D], BF16))
        aggb2 = e(nc.sbuf_tensor("aggb2s", [2, AG_SL], F32))
        bcb2 = e(nc.sbuf_tensor("bcb2s", [2, AG_SL], F32))
        ebd1 = e(nc.sbuf_tensor("ebd1", [1, TH_SL], BF16))
        xtn = e(nc.sbuf_tensor("xtn", [128, 48], BF16))
        stats = e(nc.sbuf_tensor("stats", [128, 64], F32))
        srow = e(nc.sbuf_tensor("srow", [2, 16], F32))
        wjit2 = e(nc.sbuf_tensor("wjit2", [128, 4 * AG_SL], BF16))
        sqacc = e(nc.sbuf_tensor("sqacc", [128, 8], F32))
        PS = e(nc.psum_tensor("PS", [128, 4096], F32))

        semnames = ["sT", "sV", "sS", "sP", "sCC", "sSM", "sXA", "sXB", "sW0", "sW1", "sW2", "sW3", "sWR", "sWR2", "sD1", "sDC", "sD2", "sGW", "sGB", "sGV", "sGQ", "sXC"]
        sems = {n: e(nc.semaphore(n)) for n in semnames}
        block = e(nc.Block())
        S = Sched()

        def bank(b, off=0, n=512):
            return PS[:, 512 * b + off: 512 * b + off + n]

        def bankbf(b, n=128):
            return PS.ap()[:, 512 * b: 512 * b + (n + 1) // 2].bitcast(BF16)

        def bankrow(b, rows=2, n=512):
            return PS[0:rows, 512 * b: 512 * b + n]

        thought = trK.ap()[:, 10240:14336].bitcast(F32)[0:2, :]
        gamv = trK.ap()[:, 0:4096].bitcast(F32)
        betv = trK.ap()[:, 4096:8192].bitcast(F32)
        tb8 = chsc.ap()[0:2, 0:2048]
        urow = cring.ap()[0:2, D:2 * D]
        trow = chsc.ap()[:, 2048:4096].bitcast(F32)[0:2, :]
        xnbf = chsc.ap()[0:2, 0:2048]
        bcbc = chsc.ap().bitcast(F32)
        scr2 = trQ.ap()[:, 0:4096].bitcast(F32)
        scr2b = trQ.ap()[:, 4096:8192].bitcast(F32)
        gbrow = trK.ap()[0:1, 8192:10240]
        agslot = [wjit2.ap()[:, 0:4 * AG_SL], chsc.ap()[:, 2048:2048 + 4 * AG_SL]]

        x0_tiles = x0h.rearrange("(n p) d -> n p d", p=128)
        xout_tiles = xouth.rearrange("(n p) d -> n p d", p=128)
        x1s_tiles = x1scr.rearrange("(n p) d -> n p d", p=128)

        for dst, src in [(iden[:], idenh), (maskc[:], maskh), (ones2[:], ones2h),
                         (ones128[:], ones128h), (sel2[:], sel2h), (pngt[:], pngth),
                         (pnbt2[:], pnbt2h), (ingt[:], ingth), (inbt2[:], inbt2h),
                         (d1bbf[:], d1bh), (d2brow[:], d2bh), (aggb2[:], aggbh),
                         (bcb2[:], bcbh)]:
            S.op("gpsimd", lambda g, d=dst, s=src: g.dma_start(out=d, in_=s), sem="sSM", inc=16)
        sm_done = ("sSM", S.mark("sSM"))

        def ln_stats(eps, su, sq, m2, rs, nb, rows=stats):
            S.op("vector", lambda v, a=su: v.tensor_scalar(
                rows[:, a:a + 1], rows[:, a:a + 1], 1.0 / D, 0.0, op0=ALU.mult, op1=ALU.add), sem="sV", inc=1)
            S.op("vector", lambda v, a=sq, ee=eps: v.tensor_scalar(
                rows[:, a:a + 1], rows[:, a:a + 1], 1.0 / D, ee, op0=ALU.mult, op1=ALU.add), sem="sV", inc=1)
            S.wait("vector", "sV", S.mark("sV"))
            S.op("vector", lambda v, a=su, b=m2: v.tensor_tensor(
                rows[:, b:b + 1], rows[:, a:a + 1], rows[:, a:a + 1], op=ALU.mult), sem="sV", inc=1)
            S.wait("vector", "sV", S.mark("sV"))
            S.op("vector", lambda v, a=sq, b=m2: v.tensor_tensor(
                rows[:, a:a + 1], rows[:, a:a + 1], rows[:, b:b + 1], op=ALU.subtract), sem="sV", inc=1)
            S.wait("scalar", "sV", S.mark("sV"))
            S.op("scalar", lambda s, a=sq, b=rs: s.activation(
                rows[:, b:b + 1], rows[:, a:a + 1], AF.Ln), sem="sS", inc=1)
            S.wait("scalar", "sS", S.mark("sS"))
            S.op("scalar", lambda s, b=rs: s.activation(
                rows[:, b:b + 1], rows[:, b:b + 1], AF.Exp, scale=-0.5), sem="sS", inc=1)
            S.wait("vector", "sS", S.mark("sS"))
            S.op("vector", lambda v, a=su, b=rs, c=nb: v.tensor_scalar(
                rows[:, c:c + 1], rows[:, a:a + 1], rows[:, b:b + 1], -1.0,
                op0=ALU.mult, op1=ALU.mult), sem="sV", inc=1)
            return S.mark("sV")


        def finish_passthrough(wait_list):
            for sem_, val_ in wait_list:
                S.wait("sync", sem_, val_)
            for n in range(NT):
                S.wait("sync", "sDC", S.mark("sDC"))
                S.op("sync", lambda sy, t_=n: sy.dma_start(
                    out=xring[:, 0:D], in_=x0_tiles[t_]), sem="sDC", inc=16)
                S.wait("sync", "sDC", S.mark("sDC"))
                S.op("sync", lambda sy, t_=n: sy.dma_start(
                    out=xout_tiles[t_], in_=xring[:, 0:D]), sem="sDC", inc=16)
            S.wait("sync", "sDC", S.mark("sDC"))
            S.wait("gpsimd", "sDC", S.mark("sDC"))

        evac_ph0 = []
        cast_ph0 = []
        trdone_ph0 = []
        for idx in range(NT):
            tt = idx
            slot = idx % 2
            xsem = "sXA" if slot == 0 else "sXB"
            if idx >= 2:
                S.wait("sync", "sS", cast_ph0[idx - 2])
            S.op("sync", lambda sy, sl=slot, t=tt: sy.dma_start(
                out=xring[:, D * sl: D * (sl + 1)], in_=x0_tiles[t]), sem=xsem, inc=16)
            xm = (xsem, S.mark(xsem))
            xt = xring[:, D * slot: D * (slot + 1)]
            S.waitm("vector", xm)
            S.op("vector", lambda v, x=xt: v.reduce_sum(stats[:, 0:1], x, axis=AX.X), sem="sV", inc=1)
            sum_m = S.mark("sV")
            S.waitm("scalar", xm)
            if idx >= 1:
                S.wait("scalar", "sV", sq_consumed)
            S.op("scalar", lambda s, x=xt: s.activation(
                cring[:, 0:D], x, AF.Square, accum_out=stats[:, 1:2]), sem="sS", inc=1)
            sq_m = S.mark("sS")
            S.wait("vector", "sS", sq_m)
            S.wait("vector", "sV", sum_m)
            nb_m = ln_stats(EPS0, 0, 1, 2, 3, 4)
            sq_consumed = nb_m
            abf = afbf[:, slot * D:(slot + 1) * D]
            S.wait("scalar", "sV", nb_m)
            if idx >= 2:
                S.wait("scalar", "sT", trdone_ph0[idx - 2])
            S.op("scalar", lambda s, x=xt, dd=abf: s.activation(
                dd, x, AF.Identity, bias=stats[:, 4:5], scale=stats[:, 3:4]), sem="sS", inc=1)
            cast_ph0.append(S.mark("sS"))
            S.wait("tensor", "sS", cast_ph0[-1])
            if idx == 0:
                S.waitm("tensor", sm_done)
            for i in range(NCH):
                b = 6 + (i % 2)
                k = idx * NCH + i
                if k >= 2:
                    S.wait("tensor", "sV", evac_ph0[k - 2])
                S.op("tensor", lambda t, bb=b, aa=abf[:, i * 128:(i + 1) * 128]:
                     t.transpose(bankbf(bb), aa, iden[:]), sem="sT", inc=1)
                tm = S.mark("sT")
                S.wait("vector", "sT", tm)
                dst = trA[:, i * T + tt * 128: i * T + (tt + 1) * 128]
                S.op("vector", lambda v, bb=b, dd=dst: v.tensor_copy(dd, bankbf(bb)), sem="sV", inc=1)
                evac_ph0.append(S.mark("sV"))
            trdone_ph0.append(S.mark("sT"))
        atr_done = S.mark("sV")
        if stage == 0.1:
            finish_passthrough([("sV", atr_done), ("sS", S.mark("sS")), ("sT", S.mark("sT"))])
        if stage < 0.2 and stage != 1 and stage != 3:
            qkv_skip = True
        whs = [wqh, wkh, wvh]
        trX = [trQ, trK, trV]
        NSLOT = 4
        blk_ready, blk_used, evacm = [], [], []
        for j in range(NCH if stage >= 0.2 else 0):
            for w in range(3):
                k = j * 3 + w
                slot = k % NSLOT
                if k >= NSLOT:
                    S.wait("gpsimd", "sT", blk_used[k - NSLOT])
                src = whs[w].rearrange("(i p) m -> p i m", p=128)[:, :, j * 128:(j + 1) * 128]
                dst = wring.ap().rearrange("p (s i m) -> p s i m", s=NSLOT, i=NCH)[:, slot]
                wsem = f"sW{slot}"
                S.op("gpsimd", lambda g, d=dst, s=src: g.dma_start(out=d, in_=s), sem=wsem, inc=16)
                blk_ready.append((wsem, S.mark(wsem)))
                pair = k % 3
                b0, b1 = 2 * pair, 2 * pair + 1
                S.waitm("tensor", blk_ready[-1])
                if k == 0:
                    S.wait("tensor", "sV", atr_done)
                if k >= 3:
                    sem_, val_ = evacm[k - 3]
                    S.wait("tensor", sem_, val_)
                for i in range(NCH):
                    lhs = wring[:, slot * NCH * 128 + i * 128: slot * NCH * 128 + (i + 1) * 128]
                    S.op("tensor", lambda t, l=lhs, r=trA[:, i * T: i * T + 512], bb=b0,
                         st=(i == 0), sp=(i == NCH - 1): t.matmul(bank(bb), l, r, start=st, stop=sp))
                    last = (i == NCH - 1)
                    S.op("tensor", lambda t, l=lhs, r=trA[:, i * T + 512: i * T + 1024], bb=b1,
                         st=(i == 0), sp=(i == NCH - 1): t.matmul(bank(bb), l, r, start=st, stop=sp),
                         sem="sT" if last else None, inc=1 if last else 0)
                pe_m = S.mark("sT")
                blk_used.append(pe_m)
                dst2 = trX[w].ap().rearrange("p (t h) -> p h t", h=NCH)[:, j, :]
                if k % 2 == 0:
                    S.wait("vector", "sT", pe_m)
                    S.op("vector", lambda v, dd=dst2[:, 0:512], bb=b0: v.tensor_copy(dd, bank(bb)), sem="sV", inc=1)
                    S.op("vector", lambda v, dd=dst2[:, 512:1024], bb=b1: v.tensor_copy(dd, bank(bb)), sem="sV", inc=1)
                    evacm.append(("sV", S.mark("sV")))
                else:
                    S.wait("scalar", "sT", pe_m)
                    S.op("scalar", lambda s, dd=dst2[:, 0:512], bb=b0: s.copy(dd, bank(bb)), sem="sS", inc=1)
                    S.op("scalar", lambda s, dd=dst2[:, 512:1024], bb=b1: s.copy(dd, bank(bb)), sem="sS", inc=1)
                    evacm.append(("sS", S.mark("sS")))
        qkv_evV, qkv_evS = S.mark("sV"), S.mark("sS")
        if stage == 0.2:
            finish_passthrough([("sV", qkv_evV), ("sS", qkv_evS), ("sT", S.mark("sT"))])

        NG = T // 8
        scores_pe, eins_done = [], []
        sbd_m, vp_m = [], []
        if stage >= 0.3:
            S.wait("tensor", "sV", qkv_evV)
            S.wait("tensor", "sS", qkv_evS)
        pend = None
        for g in range(NG if stage >= 0.3 else 0):
            c0 = g * 128
            bS, bV, bO = (g % 2), 2 + (g % 2), 4 + (g % 2)
            sbd = afbf.ap()[:, (g % 2) * 128: (g % 2) * 128 + 128]
            vp = afbf.ap()[:, 256 + (g % 2) * 128: 256 + (g % 2) * 128 + 128]
            if g >= 2:
                S.wait("tensor", "sV", sbd_m[g - 2])
                sem_, val_ = vp_m[g - 2]
                S.wait("tensor", sem_, val_)
            if g >= 3:
                S.wait("tensor", "sS", eins_done[g - 3])
            S.op("tensor", lambda t, c=c0, bb=bS: t.matmul(
                bank(bb, 0, 128), trK[:, c: c + 128], trQ[:, c: c + 128],
                start=True, stop=True), sem="sT", inc=1)
            scores_pe.append(S.mark("sT"))
            S.op("tensor", lambda t, c=c0, bb=bV: t.transpose(
                bankbf(bb), trV[:, c: c + 128], iden[:]), sem="sT", inc=1)
            vpt = S.mark("sT")
            S.wait("vector", "sT", scores_pe[-1])
            S.op("vector", lambda v, bb=bS, dd=sbd: v.tensor_tensor(
                dd, bank(bb, 0, 128), maskc[:], op=ALU.mult), sem="sV", inc=1)
            sbd_m.append(S.mark("sV"))
            S.wait("scalar", "sT", vpt)
            S.op("scalar", lambda s, bb=bV, dd=vp: s.copy(dd, bankbf(bb)), sem="sS", inc=1)
            vp_m.append(("sS", S.mark("sS")))
            if pend is not None:
                pg, pvp, psbd, pbO = pend
                S.wait("tensor", "sV", sbd_m[pg])
                S.waitm("tensor", vp_m[pg])
                S.op("tensor", lambda t, vv=pvp, ss_=psbd, bb=pbO: t.matmul(
                    bank(bb, 0, 128), vv, ss_, start=True, stop=True), sem="sT", inc=1)
                om = S.mark("sT")
                S.wait("scalar", "sT", om)
                S.op("scalar", lambda s, bb=pbO, c=pg * 128: s.copy(trA[:, c: c + 128], bank(bb, 0, 128)),
                     sem="sS", inc=1)
                eins_done.append(S.mark("sS"))
            pend = (g, vp, sbd, bO)
        if pend is not None:
            pg, pvp, psbd, pbO = pend
            S.wait("tensor", "sV", sbd_m[pg])
            S.waitm("tensor", vp_m[pg])
            S.op("tensor", lambda t, vv=pvp, ss_=psbd, bb=pbO: t.matmul(
                bank(bb, 0, 128), vv, ss_, start=True, stop=True), sem="sT", inc=1)
            om = S.mark("sT")
            S.wait("scalar", "sT", om)
            S.op("scalar", lambda s, bb=pbO, c=pg * 128: s.copy(trA[:, c: c + 128], bank(bb, 0, 128)),
                 sem="sS", inc=1)
            eins_done.append(S.mark("sS"))

        if stage == 0.3:
            finish_passthrough([("sS", S.mark("sS")), ("sT", S.mark("sT")), ("sV", S.mark("sV"))])
        def wo_dst(i):
            if i < 7:
                return trQ.ap()[:, i * 2048:(i + 1) * 2048]
            if i < 12:
                return trK.ap()[:, (i - 7) * 2048:(i - 6) * 2048]
            return wring.ap()[:, (i - 12) * 2048:(i - 11) * 2048]

        wo_ready = []
        for i in range(NCH if stage >= 0.4 else 0):
            if i < 7:
                S.wait("gpsimd", "sT", scores_pe[16 * i + 15])
            elif i < 12:
                S.wait("gpsimd", "sT", scores_pe[16 * (i - 7) + 15])
            else:
                S.wait("gpsimd", "sT", blk_used[-1])
            src_ = woh.rearrange("(i p) m -> p i m", p=128)[:, i, :]
            wsem_ = "sWR" if i < 8 else "sWR2"
            S.op("gpsimd", lambda g, d=wo_dst(i), s=src_: g.dma_start(out=d, in_=s), sem=wsem_, inc=16)
            wo_ready.append(S.mark(wsem_))
        wo_res = ("sWR", S.mark("sWR"))
        wo_res2 = ("sWR2", S.mark("sWR2"))

        d1src = d1wh.rearrange("(i p) m -> p i m", p=128)
        if stage >= 0.4:
            S.wait("gpsimd", "sT", S.mark("sT"))
        for i4 in range(4 if stage >= 0.4 else 0):
            S.op("gpsimd", lambda g, d=trV.ap().rearrange("p (i m) -> p i m", i=NCH)[:, i4 * 4:(i4 + 1) * 4],
                 s=d1src[:, i4 * 4:(i4 + 1) * 4]: g.dma_start(out=d, in_=s), sem="sD1", inc=16)
        d1_in = ("sD1", S.mark("sD1"))

        x1_built, x1_spilled, x0_ld = [], [], []
        fstat_nb = [0] * NT
        ftr_done = [0] * NT
        wo_banksfree = []
        if stage == 0.4:
            finish_passthrough([("sWR", S.mark("sWR")), ("sD1", S.mark("sD1")), ("sT", S.mark("sT"))])
        fb_marks = [0] * NT
        fsq_l = [0] * NT
        tmark = [0] * NT
        red_done_l = [0] * NT
        stg_done_l = [0] * NT
        red_done = 0
        stg_done = 0

        def emit_ftr(m):
            ttm_ = TORD[m]
            s0m = (m * 4) % 6
            tbanks = [(s0m + 2) % 6, (s0m + 3) % 6]
            fbm_ = fb_marks[m]
            fbuf = afbf[:, (m % 2) * D: (m % 2 + 1) * D]
            S.wait("tensor", "sS", fbm_)
            S.wait("tensor", "sV", red_done_l[m])
            ev = []
            for i in range(NCH):
                b_ = tbanks[i % 2]
                if i >= 2:
                    S.wait("tensor", "sV", ev[i - 2])
                S.op("tensor", lambda t, bb=b_, aa=fbuf[:, i * 128:(i + 1) * 128]:
                     t.transpose(bankbf(bb), aa, iden[:]), sem="sT", inc=1)
                tm_ = S.mark("sT")
                S.wait("vector", "sT", tm_)
                dst = trA[:, ttm_ * 2048 + i * 128: ttm_ * 2048 + (i + 1) * 128]
                S.op("vector", lambda v, bb=b_, dd=dst: v.tensor_copy(dd, bankbf(bb)), sem="sV", inc=1)
                ev.append(S.mark("sV"))
            tmark[m] = S.mark("sT")
            ftr_done[TORD[m]] = S.mark("sV")

        for n, tt in enumerate(TORD if stage >= 0.5 else []):
            slot = n % 2
            xsem = "sXA" if slot == 0 else "sXB"
            pb = 8 * (n % 2)
            if n >= 2:
                S.waitm("sync", x1_spilled[n - 2])
                S.wait("sync", "sS", fb_marks[n - 2])
            else:
                S.wait("sync", "sS", cast_ph0[6 + n])
            S.op("sync", lambda sy, sl=slot, t=tt: sy.dma_start(
                out=xring[:, D * sl: D * (sl + 1)], in_=x0_tiles[t]), sem=xsem, inc=16)
            x0_ld.append((xsem, S.mark(xsem)))

            s0 = (n * 4) % 6
            banks = [(s0 + q) % 6 for q in range(4)]
            if n == 0:
                S.waitm("tensor", wo_res)
            S.wait("tensor", "sS", eins_done[16 * tt + 15])
            if n >= 1:
                S.wait("tensor", "sV", red_done_l[n - 1])
            if n >= 2:
                S.wait("tensor", "sV", ftr_done[TORD[n - 2]])
            for i in range(NCH):
                if n == 0 and i == 8:
                    S.waitm("tensor", wo_res2)
                lhs = trA.ap().rearrange("p (t h) -> p h t", h=NCH)[:, i, tt * 128:(tt + 1) * 128]
                wsrc = wo_dst(i)
                for q in range(4):
                    last = (i == NCH - 1 and q == 3)
                    S.op("tensor", lambda t, l=lhs, r=wsrc[:, q * 512:(q + 1) * 512], bb=banks[q],
                         st=(i == 0), sp=(i == NCH - 1): t.matmul(bank(bb), l, r, start=st, stop=sp),
                         sem="sT" if last else None, inc=1 if last else 0)
            wo_pe = S.mark("sT")
            if n >= 1 and stage >= 1:
                emit_ftr(n - 1)
            S.wait("vector", "sT", wo_pe)
            if n >= 1:
                S.wait("vector", "sS", fsq_l[n - 1])
            for q in range(4):
                S.op("vector", lambda v, bb=banks[q], q_=q: v.tensor_copy(
                    cring[:, q_ * 512:(q_ + 1) * 512], bank(bb)), sem="sV", inc=1)
            red_done = S.mark("sV")
            red_done_l[n] = red_done
            S.wait("vector", "sV", red_done)
            S.op("vector", lambda v, a=pb: v.reduce_sum(stats[:, a:a + 1], cring[:, 0:D], axis=AX.X), sem="sV", inc=1)
            S.wait("scalar", "sV", red_done)
            if n >= 1:
                S.wait("scalar", "sV", x1_built[-1])
            S.op("scalar", lambda s, a=pb: s.activation(
                cring[:, D:2 * D], cring[:, 0:D], AF.Square, accum_out=stats[:, a + 1:a + 2]), sem="sS", inc=1)
            S.wait("vector", "sS", S.mark("sS"))
            S.wait("vector", "sV", S.mark("sV"))
            nb_m = ln_stats(EPS0, pb, pb + 1, pb + 2, pb + 3, pb + 4)
            S.wait("scalar", "sV", nb_m)
            S.op("scalar", lambda s, a=pb: s.activation(
                cring[:, 0:D], cring[:, 0:D], AF.Identity,
                bias=stats[:, a + 4:a + 5], scale=stats[:, a + 3:a + 4]), sem="sS", inc=1)
            stg_done = S.mark("sS")
            stg_done_l[n] = stg_done
            S.wait("vector", "sS", stg_done)
            S.waitm("vector", x0_ld[n])
            xt = xring[:, D * slot: D * (slot + 1)]
            S.op("vector", lambda v, x=xt: v.tensor_tensor(x, x, cring[:, 0:D], op=ALU.add), sem="sV", inc=1)
            x1_built.append(S.mark("sV"))
            x1t = xt
            S.wait("vector", "sV", x1_built[-1])
            S.op("vector", lambda v, x=x1t, a=pb: v.reduce_sum(stats[:, a:a + 1], x, axis=AX.X), sem="sV", inc=1)
            S.wait("scalar", "sV", x1_built[-1])
            S.op("scalar", lambda s, x=x1t, a=pb: s.activation(
                cring[:, 0:D], x, AF.Square, accum_out=stats[:, a + 1:a + 2]), sem="sS", inc=1)
            fsq_l[n] = S.mark("sS")
            S.wait("vector", "sS", S.mark("sS"))
            S.wait("vector", "sV", S.mark("sV"))
            nbf = ln_stats(EPS0, pb, pb + 1, pb + 2, 32 + tt, 40 + tt)
            fstat_nb[tt] = nbf
            fb = afbf[:, (n % 2) * D: (n % 2 + 1) * D]
            S.wait("scalar", "sV", nbf)
            if n >= 2 and stage >= 1:
                S.wait("scalar", "sT", tmark[n - 2])
            S.op("scalar", lambda s, x=x1t, dd=fb, t_=tt: s.activation(
                dd, x, AF.Identity, bias=stats[:, 40 + t_:41 + t_], scale=stats[:, 32 + t_:33 + t_]),
                sem="sS", inc=1)
            fb_marks[n] = S.mark("sS")
            if stage < 1:
                ftr_done[tt] = S.mark("sV")
            S.wait("sync", "sV", x1_built[-1])
            S.op("sync", lambda sy, sl=slot, t_=tt: sy.dma_start(
                out=x1s_tiles[t_], in_=xring[:, D * sl: D * (sl + 1)]), sem=xsem, inc=16)
            x1_spilled.append((xsem, S.mark(xsem)))
        if stage >= 1:
            emit_ftr(NT - 1)
        wo_all_pe = S.mark("sT")
        if stage in (0.5, 0.51, 0.52, 0.53, 0.55, 0.57, 0.58, 0.6):
            finish_passthrough([("sT", wo_all_pe), ("sS", S.mark("sS")), ("sV", S.mark("sV")),
                                ("sXA", S.mark("sXA")), ("sXB", S.mark("sXB"))])
        if stage < 1:
            pass
        elif stage == 1:
            S.wait("sync", "sXA", S.mark("sXA"))
            S.wait("sync", "sXB", S.mark("sXB"))
            for n in range(NT):
                S.wait("sync", "sDC", S.mark("sDC"))
                S.op("sync", lambda sy, t_=n: sy.dma_start(
                    out=xring[:, 0:D], in_=x1s_tiles[t_]), sem="sDC", inc=16)
                S.wait("sync", "sDC", S.mark("sDC"))
                S.op("sync", lambda sy, t_=n: sy.dma_start(
                    out=xout_tiles[t_], in_=xring[:, 0:D]), sem="sDC", inc=16)
            S.wait("sync", "sDC", S.mark("sDC"))
            S.wait("gpsimd", "sDC", S.mark("sDC"))
        else:
            wg_src = wgh.rearrange("(i p) m -> p i m", p=128)
            S.wait("gpsimd", "sT", wo_all_pe)
            d2src = d2wh.rearrange("(c p) m -> p c m", p=128)
            for c4 in range(4):
                S.op("gpsimd", lambda g, d=trQ.ap().rearrange("p (c m) -> p c m", c=8)[:, c4 * 2:(c4 + 1) * 2],
                     s=d2src[:, c4 * 2:(c4 + 1) * 2]: g.dma_start(out=d, in_=s), sem="sD2", inc=16)
            d2s_in = S.mark("sD2")
            def gw_dst(c):
                if c < 8:
                    return trV.ap()[:, c * 2048:(c + 1) * 2048]
                if c < 12:
                    return wring.ap()[:, (c - 8) * 2048:(c - 7) * 2048]
                if c == 12:
                    return trK.ap()[:, 14336:16384]
                if c < 15:
                    return afbf.ap()[:, (c - 13) * 2048:(c - 12) * 2048]
                return trQ.ap()[:, 12288:14336]
            S.wait("gpsimd", "sV", max(ftr_done))
            for c in range(8, 15):
                S.op("gpsimd", lambda g, d=gw_dst(c), s=wg_src[:, c, :]: g.dma_start(out=d, in_=s),
                     sem="sGW", inc=16)
            gw_early = S.mark("sGW")
            S.waitm("tensor", d1_in)
            for i in range(NCH):
                for hh in range(2):
                    S.op("tensor", lambda t, i_=i, h_=hh: t.matmul(
                        bankrow(6 + h_, 2, 512), pnbt2[:, 2 * i_:2 * i_ + 2],
                        trV[:, i_ * 1024 + h_ * 512: i_ * 1024 + (h_ + 1) * 512],
                        start=(i_ == 0), stop=False), sem=None, inc=0)
            for hh in range(2):
                S.op("tensor", lambda t, h_=hh: t.matmul(
                    bankrow(6 + h_, 2, 512), ones2[:], d1bbf[0:1, h_ * 512:(h_ + 1) * 512],
                    start=False, stop=True), sem="sT" if hh == 1 else None, inc=1 if hh == 1 else 0)
            ebmm = S.mark("sT")
            S.wait("scalar", "sT", ebmm)
            S.op("scalar", lambda s: s.copy(ebd1[0:1, 0:512], bankrow(6, 1, 512)), sem="sS", inc=1)
            S.op("scalar", lambda s: s.copy(ebd1[0:1, 512:1024], bankrow(7, 1, 512)), sem="sS", inc=1)
            ebd1_m = S.mark("sS")
            S.wait("vector", "sT", ebmm)
            for i in range(NCH):
                S.op("vector", lambda v, i_=i: v.tensor_scalar(
                    trV[:, i_ * 1024:(i_ + 1) * 1024], trV[:, i_ * 1024:(i_ + 1) * 1024],
                    pngt[:, i_:i_ + 1], 0.0, op0=ALU.mult, op1=ALU.add), sem="sV", inc=1)
            d1s_ready = S.mark("sV")

            S.wait("vector", "sS", S.mark("sS"))
            S.op("vector", lambda v: v.memset(urow[:], 0.0), sem="sV", inc=1)
            um0 = S.mark("sV")
            S.wait("sync", "sV", um0)
            S.wait("sync", "sXA", S.mark("sXA"))
            S.wait("sync", "sXB", S.mark("sXB"))
            S.op("sync", lambda sy: sy.dma_start(out=urow[0:1, :], in_=x1scr[1023:1024, :]),
                 sem="sDC", inc=16)
            fl_m = ("sDC", S.mark("sDC"))

            def row_ln(src_row, dst_row, eps, sc=0):
                S.op("vector", lambda v: v.reduce_sum(srow[0:2, sc:sc + 1], src_row, axis=AX.X),
                     sem="sV", inc=1)
                sm = S.mark("sV")
                S.op("scalar", lambda s: s.activation(
                    cring[0:2, 0:D], src_row, AF.Square,
                    accum_out=srow[0:2, sc + 1:sc + 2]), sem="sS", inc=1)
                qm = S.mark("sS")
                S.wait("vector", "sV", sm)
                S.wait("vector", "sS", qm)
                S.op("vector", lambda v: v.tensor_scalar(
                    srow[0:2, sc:sc + 1], srow[0:2, sc:sc + 1], 1.0 / D, 0.0, op0=ALU.mult, op1=ALU.add), sem="sV", inc=1)
                S.op("vector", lambda v, ee=eps: v.tensor_scalar(
                    srow[0:2, sc + 1:sc + 2], srow[0:2, sc + 1:sc + 2], 1.0 / D, ee, op0=ALU.mult, op1=ALU.add), sem="sV", inc=1)
                S.wait("vector", "sV", S.mark("sV"))
                S.op("vector", lambda v: v.tensor_tensor(
                    srow[0:2, sc + 2:sc + 3], srow[0:2, sc:sc + 1], srow[0:2, sc:sc + 1], op=ALU.mult), sem="sV", inc=1)
                S.wait("vector", "sV", S.mark("sV"))
                S.op("vector", lambda v: v.tensor_tensor(
                    srow[0:2, sc + 1:sc + 2], srow[0:2, sc + 1:sc + 2], srow[0:2, sc + 2:sc + 3], op=ALU.subtract), sem="sV", inc=1)
                S.wait("scalar", "sV", S.mark("sV"))
                S.op("scalar", lambda s: s.activation(
                    srow[0:2, sc + 3:sc + 4], srow[0:2, sc + 1:sc + 2], AF.Ln), sem="sS", inc=1)
                S.wait("scalar", "sS", S.mark("sS"))
                S.op("scalar", lambda s: s.activation(
                    srow[0:2, sc + 3:sc + 4], srow[0:2, sc + 3:sc + 4], AF.Exp, scale=-0.5), sem="sS", inc=1)
                S.wait("vector", "sS", S.mark("sS"))
                S.op("vector", lambda v: v.tensor_scalar(
                    srow[0:2, sc + 4:sc + 5], srow[0:2, sc:sc + 1], srow[0:2, sc + 3:sc + 4], -1.0,
                    op0=ALU.mult, op1=ALU.mult), sem="sV", inc=1)
                S.wait("scalar", "sV", S.mark("sV"))
                S.op("scalar", lambda s: s.activation(
                    dst_row, src_row, AF.Identity, bias=srow[0:2, sc + 4:sc + 5],
                    scale=srow[0:2, sc + 3:sc + 4]), sem="sS", inc=1)
                return S.mark("sS")

            S.waitm("vector", fl_m)
            S.waitm("scalar", fl_m)
            m1 = row_ln(urow[:], urow[:], EPS0, sc=0)
            S.wait("vector", "sS", m1)
            S.wait("scalar", "sS", m1)
            m2 = row_ln(urow[:], cring[0:2, 0:D], EPSF, sc=5)
            S.wait("sync", "sS", m2)
            S.op("sync", lambda sy: sy.dma_start(out=ag1_in, in_=cring[0:1, 0:D]), sem="sDC", inc=16)
            S.wait("gpsimd", "sDC", S.mark("sDC"))
            S.op("gpsimd", lambda g: g.collective_compute(
                "AllGather", ALU.bypass, replica_groups=[list(range(NCORES))],
                ins=[ag1_in], outs=[ag1_out]), sem="sCC", inc=1)
            ag1_m = S.mark("sCC")
            S.wait("sync", "sCC", ag1_m)
            S.op("sync", lambda sy: sy.dma_start(out=cring[0:1, 0:D], in_=ag1_out[3:4, :]), sem="sDC", inc=16)
            S.op("sync", lambda sy: sy.dma_start(out=cring[1:2, 0:D], in_=ag1_out[7:8, :]), sem="sDC", inc=16)
            xn_m = ("sDC", S.mark("sDC"))
            S.waitm("vector", xn_m)
            S.op("vector", lambda v: v.tensor_copy(xnbf, cring[0:2, 0:D]), sem="sV", inc=1)
            xnbf_m = S.mark("sV")
            S.wait("tensor", "sV", xnbf_m)
            xe = []
            for i in range(NCH):
                bb = 6 + (i % 2)
                if i >= 2:
                    S.wait("tensor", "sV", xe[i - 2])
                S.op("tensor", lambda t, b_=bb, aa=xnbf[:, i * 128:(i + 1) * 128]:
                     t.transpose(PS.ap()[:, 512 * b_:512 * b_ + 1].bitcast(BF16), aa, iden[0:2, 0:2]),
                     sem="sT", inc=1)
                tm = S.mark("sT")
                S.wait("vector", "sT", tm)
                S.op("vector", lambda v, b_=bb, i_=i: v.tensor_copy(
                    xtn[:, 2 * i_:2 * i_ + 2], PS.ap()[:, 512 * b_:512 * b_ + 1].bitcast(BF16)), sem="sV", inc=1)
                xe.append(S.mark("sV"))
            xtn_m = S.mark("sV")
            agg_src = aggwh.rearrange("(i p) m -> p i m", p=128)
            agg_used = [0] * 4
            for b4 in range(4):
                sl = b4 % 2
                wsem = f"sW{sl}"
                if b4 >= 2:
                    S.wait("gpsimd", "sT", agg_used[b4 - 2])
                S.op("gpsimd", lambda g, d=agslot[sl].rearrange("p (c m) -> p c m", c=4),
                     s=agg_src[:, b4 * 4:(b4 + 1) * 4]: g.dma_start(out=d, in_=s), sem=wsem, inc=16)
                blkm = (wsem, S.mark(wsem))
                S.waitm("tensor", blkm)
                if b4 == 0:
                    S.wait("tensor", "sV", xtn_m)
                S.waitm("vector", blkm)
                for c in range(4):
                    i = b4 * 4 + c
                    chunk = agslot[sl][:, c * AG_SL:(c + 1) * AG_SL]
                    S.op("tensor", lambda t, i_=i, ch=chunk: t.matmul(
                        bankrow(6, 2, AG_SL), inbt2[:, 2 * i_:2 * i_ + 2], ch,
                        start=(i_ == 0), stop=(i_ == NCH - 1)), sem="sT", inc=1)
                    ebm = S.mark("sT")
                    S.wait("vector", "sT", ebm)
                    S.op("vector", lambda v, i_=i, ch=chunk: v.tensor_scalar(
                        ch, ch, ingt[:, i_:i_ + 1], 0.0, op0=ALU.mult, op1=ALU.add), sem="sV", inc=1)
                    fm = S.mark("sV")
                    S.wait("tensor", "sV", fm)
                    S.op("tensor", lambda t, i_=i, ch=chunk: t.matmul(
                        bankrow(7, 2, AG_SL), xtn[:, 2 * i_:2 * i_ + 2], ch,
                        start=(i_ == 0), stop=(i_ == NCH - 1)), sem="sT", inc=1)
                agg_used[b4] = S.mark("sT")
            agg_pe = S.mark("sT")
            S.wait("vector", "sT", agg_pe)
            S.op("vector", lambda v: v.tensor_copy(cring[0:2, 0:AG_SL], bankrow(7, 2, AG_SL)), sem="sV", inc=1)
            S.wait("vector", "sV", S.mark("sV"))
            S.op("vector", lambda v: v.tensor_tensor(
                cring[0:2, 0:AG_SL], cring[0:2, 0:AG_SL], bankrow(6, 2, AG_SL), op=ALU.add), sem="sV", inc=1)
            S.wait("vector", "sV", S.mark("sV"))
            S.op("vector", lambda v: v.tensor_tensor(
                cring[0:2, 0:AG_SL], cring[0:2, 0:AG_SL], aggb2[:], op=ALU.add), sem="sV", inc=1)
            th0_m = S.mark("sV")
            S.wait("sync", "sV", th0_m)
            S.op("sync", lambda sy: sy.dma_start(out=agg_in, in_=cring[0:2, 0:AG_SL]), sem="sDC", inc=16)
            S.wait("gpsimd", "sDC", S.mark("sDC"))
            S.op("gpsimd", lambda g: g.collective_compute(
                "AllGather", ALU.bypass, replica_groups=[list(range(NCORES))],
                ins=[agg_in], outs=[agg_out]), sem="sCC", inc=1)
            S.wait("sync", "sCC", S.mark("sCC"))
            S.op("sync", lambda sy: sy.dma_start(
                out=thought.rearrange("b (r s) -> b r s", r=NCORES),
                in_=agg_out.rearrange("(r b) s -> b r s", b=2)), sem="sDC", inc=16)
            th_m = ("sDC", S.mark("sDC"))

            d1m_l = [0] * 4
            d2m_l = [0] * 4
            prev = th_m
            for st in range(4):
                S.waitm("vector", prev)
                S.waitm("scalar", prev)
                mln = row_ln(thought, tb8, EPSF, sc=8)
                S.wait("tensor", "sS", mln)
                xe2 = []
                for i in range(NCH):
                    bb = 6 + (i % 2)
                    if i >= 2:
                        S.wait("tensor", "sV", xe2[i - 2])
                    S.op("tensor", lambda t, b_=bb, aa=tb8[:, i * 128:(i + 1) * 128]:
                         t.transpose(PS.ap()[:, 512 * b_:512 * b_ + 1].bitcast(BF16), aa, iden[0:2, 0:2]),
                         sem="sT", inc=1)
                    tm = S.mark("sT")
                    S.wait("vector", "sT", tm)
                    S.op("vector", lambda v, b_=bb, i_=i: v.tensor_copy(
                        xtn[:, 2 * i_:2 * i_ + 2], PS.ap()[:, 512 * b_:512 * b_ + 1].bitcast(BF16)), sem="sV", inc=1)
                    xe2.append(S.mark("sV"))
                ttm = S.mark("sV")
                S.wait("tensor", "sV", ttm)
                if st == 0:
                    S.wait("tensor", "sV", d1s_ready)
                    S.wait("tensor", "sS", ebd1_m)
                for i in range(NCH):
                    for hh in range(2):
                        S.op("tensor", lambda t, i_=i, h_=hh: t.matmul(
                            bankrow(6 + h_, 2, 512), xtn[:, 2 * i_:2 * i_ + 2],
                            trV[:, i_ * 1024 + h_ * 512: i_ * 1024 + (h_ + 1) * 512],
                            start=(i_ == 0), stop=False), sem=None, inc=0)
                for hh in range(2):
                    S.op("tensor", lambda t, h_=hh: t.matmul(
                        bankrow(6 + h_, 2, 512), ones2[:], ebd1[0:1, h_ * 512:(h_ + 1) * 512],
                        start=False, stop=True), sem="sT" if hh == 1 else None, inc=1 if hh == 1 else 0)
                d1m = S.mark("sT")
                d1m_l[st] = d1m
                if st == 3:
                    S.wait("gpsimd", "sT", d1m)
                    for c4 in range(4):
                        S.op("gpsimd", lambda g, d=trV.ap().rearrange("p (c m) -> p c m", c=8)[:, c4 * 2:(c4 + 1) * 2],
                             s=wg_src[:, c4 * 2:(c4 + 1) * 2, :]: g.dma_start(out=d, in_=s), sem="sGV", inc=16)
                    gv_m = S.mark("sGV")
                S.wait("scalar", "sT", d1m)
                S.op("scalar", lambda s: s.activation(tb8[:, 0:512], bankrow(6, 2, 512), AF.Gelu), sem="sS", inc=1)
                S.op("scalar", lambda s: s.activation(tb8[:, 512:1024], bankrow(7, 2, 512), AF.Gelu), sem="sS", inc=1)
                gm = S.mark("sS")
                S.wait("tensor", "sS", gm)
                xe3 = []
                for ii in range(8):
                    bb = 6 + (ii % 2)
                    if ii >= 2:
                        S.wait("tensor", "sV", xe3[ii - 2])
                    S.op("tensor", lambda t, b_=bb, i_=ii:
                         t.transpose(PS.ap()[:, 512 * b_:512 * b_ + 1].bitcast(BF16),
                                     tb8[:, i_ * 128:(i_ + 1) * 128], iden[0:2, 0:2]), sem="sT", inc=1)
                    tm = S.mark("sT")
                    S.wait("vector", "sT", tm)
                    S.op("vector", lambda v, b_=bb, i_=ii: v.tensor_copy(
                        xtn[:, 32 + 2 * i_:32 + 2 * i_ + 2], PS.ap()[:, 512 * b_:512 * b_ + 1].bitcast(BF16)),
                        sem="sV", inc=1)
                    xe3.append(S.mark("sV"))
                tgm = S.mark("sV")
                S.wait("tensor", "sV", tgm)
                if st == 0:
                    S.wait("tensor", "sD2", d2s_in)
                for c in range(8):
                    for q in range(4):
                        S.op("tensor", lambda t, c_=c, q_=q: t.matmul(
                            bankrow(q_, 2, 512), xtn[:, 32 + 2 * c_:32 + 2 * c_ + 2],
                            trQ[:, c_ * 2048 + q_ * 512: c_ * 2048 + (q_ + 1) * 512],
                            start=(c_ == 0), stop=False), sem=None, inc=0)
                for q in range(4):
                    S.op("tensor", lambda t, q_=q: t.matmul(
                        bankrow(q_, 2, 512), ones2[:], d2brow[0:1, q_ * 512:(q_ + 1) * 512],
                        start=False, stop=True), sem="sT" if q == 3 else None, inc=1 if q == 3 else 0)
                d2m = S.mark("sT")
                d2m_l[st] = d2m
                S.wait("vector", "sT", d2m)
                S.op("vector", lambda v: v.tensor_copy(urow[:], PS[0:2, 0:2048]), sem="sV", inc=1)
                um = S.mark("sV")
                if st == 3:
                    um3 = um
                S.wait("sync", "sV", um)
                S.op("sync", lambda sy, s_=st: sy.dma_start(out=ar_inh[s_], in_=urow[:]), sem="sDC", inc=16)
                S.wait("gpsimd", "sDC", S.mark("sDC"))
                S.op("gpsimd", lambda g, s_=st: g.collective_compute(
                    "AllReduce", ALU.add, replica_groups=[list(range(NCORES))],
                    ins=[ar_inh[s_]], outs=[ar_outh[s_]]), sem="sCC", inc=1)
                S.wait("sync", "sCC", S.mark("sCC"))
                S.op("sync", lambda sy, s_=st: sy.dma_start(out=urow[:], in_=ar_outh[s_]), sem="sDC", inc=16)
                arm = S.mark("sDC")
                S.wait("vector", "sDC", arm)
                S.op("vector", lambda v: v.tensor_tensor(thought, thought, urow[:], op=ALU.add), sem="sV", inc=1)
                prev = ("sV", S.mark("sV"))
                if st == 3:
                    S.wait("gpsimd", "sT", d2m)
                    S.wait("gpsimd", "sV", prev[1])
                    S.op("gpsimd", lambda g: g.dma_start(out=gw_dst(15), in_=wg_src[:, 15, :]), sem="sGQ", inc=16)
                    S.op("gpsimd", lambda g: g.dma_start(out=gbrow, in_=gatebh), sem="sGQ", inc=16)
                    gq_m = S.mark("sGQ")

            S.wait("vector", prev[0], prev[1])
            S.op("vector", lambda v: v.tensor_copy(tb8[:, 0:D], thought), sem="sV", inc=1)
            tbm = S.mark("sV")
            S.wait("tensor", "sV", tbm)
            xe4 = []
            for i in range(NCH):
                bb = 6 + (i % 2)
                if i >= 2:
                    S.wait("tensor", "sV", xe4[i - 2])
                S.op("tensor", lambda t, b_=bb, i_=i: t.transpose(
                    PS.ap()[:, 512 * b_:512 * b_ + 1].bitcast(BF16),
                    tb8[:, i_ * 128:(i_ + 1) * 128], iden[0:2, 0:2]), sem="sT", inc=1)
                tm = S.mark("sT")
                S.wait("vector", "sT", tm)
                S.op("vector", lambda v, b_=bb, i_=i: v.tensor_copy(
                    xtn[:, 2 * i_:2 * i_ + 2], PS.ap()[:, 512 * b_:512 * b_ + 1].bitcast(BF16)), sem="sV", inc=1)
                xe4.append(S.mark("sV"))
            btm = S.mark("sV")

            sig_l = [0] * NT
            sigq2_l = [0] * NT
            gpe_l = [0] * NT
            x1m_l = [None] * NT
            ln2_l = [0] * NT
            x2_l = [0] * NT
            out_m = [None] * NT
            gbanks = [[((n * 4) % 6 + q) % 6 for q in range(4)] for n in range(NT)]

            def gw_chunkap(c, q):
                return gw_dst(c)[:, q * 512:(q + 1) * 512]

            def gate_pe_half(n, half):
                tt = TORD[n]
                banks = gbanks[n]
                q0 = 2 * half
                if half == 0:
                    if n == 0:
                        S.wait("tensor", "sGV", gv_m)
                        S.wait("tensor", "sGW", gw_early)
                        S.wait("tensor", "sV", max(ftr_done))
                        S.wait("tensor", "sV", um3)
                    if n >= 2:
                        S.wait("tensor", "sS", sig_l[n - 2])
                else:
                    if n >= 1:
                        S.wait("tensor", "sS", sigq2_l[n - 1])
                for i in range(NCH - 1):
                    for dq in range(2):
                        S.op("tensor", lambda t, l=trA[:, tt * 2048 + i * 128: tt * 2048 + (i + 1) * 128],
                             r=gw_chunkap(i, q0 + dq), bb=banks[q0 + dq], st=(i == 0):
                             t.matmul(bank(bb), l, r, start=st, stop=False), sem=None, inc=0)
                if n == 0 and half == 0:
                    S.wait("tensor", "sGQ", gq_m)
                for dq in range(2):
                    S.op("tensor", lambda t, bb=banks[q0 + dq], q_=q0 + dq: t.matmul(
                        bank(bb), ones128[:], gbrow[0:1, q_ * 512:(q_ + 1) * 512],
                        start=False, stop=False), sem=None, inc=0)
                for dq in range(2):
                    S.op("tensor", lambda t, l=trA[:, tt * 2048 + 15 * 128: tt * 2048 + 16 * 128],
                         r=gw_chunkap(15, q0 + dq), bb=banks[q0 + dq]:
                         t.matmul(bank(bb), l, r, start=False, stop=True),
                         sem="sT" if dq == 1 else None, inc=1 if dq == 1 else 0)
                if half == 1:
                    gpe_l[n] = S.mark("sT")

            def gate_sig(n):
                tt = TORD[n]
                banks = gbanks[n]
                S.wait("scalar", "sT", gpe_l[n])
                for q in range(4):
                    S.op("scalar", lambda s, bb=banks[q], q_=q: s.activation(
                        trA[:, tt * 2048 + q_ * 512: tt * 2048 + (q_ + 1) * 512],
                        bank(bb), AF.Sigmoid), sem="sS", inc=1)
                    if q == 1:
                        sigq2_l[n] = S.mark("sS")
                sig_l[n] = S.mark("sS")

            bc_src = bcwh.rearrange("(i p) m -> p i m", p=128)
            bc_used = [0] * 4
            for b4 in range(4):
                sl = b4 % 2
                wsem = f"sW{sl}"
                if b4 >= 2:
                    S.wait("gpsimd", "sT", bc_used[b4 - 2])
                S.op("gpsimd", lambda g, d=agslot[sl].rearrange("p (c m) -> p c m", c=4),
                     s=bc_src[:, b4 * 4:(b4 + 1) * 4]: g.dma_start(out=d, in_=s), sem=wsem, inc=16)
                blkm = (wsem, S.mark(wsem))
                S.waitm("tensor", blkm)
                if b4 == 0:
                    S.wait("tensor", "sV", btm)
                for c in range(4):
                    i = b4 * 4 + c
                    chunk = agslot[sl][:, c * AG_SL:(c + 1) * AG_SL]
                    S.op("tensor", lambda t, i_=i, ch=chunk: t.matmul(
                        bankrow(6, 2, AG_SL), xtn[:, 2 * i_:2 * i_ + 2], ch,
                        start=(i_ == 0), stop=(i_ == NCH - 1)), sem="sT", inc=1)
                bc_used[b4] = S.mark("sT")
                gate_pe_half(b4 // 2, b4 % 2)
                if b4 % 2 == 1:
                    gate_sig(b4 // 2)
            bc_pe = bc_used[3]
            S.wait("vector", "sT", bc_pe)
            S.op("vector", lambda v: v.tensor_copy(cring[0:2, 0:AG_SL], bankrow(6, 2, AG_SL)), sem="sV", inc=1)
            S.wait("vector", "sV", S.mark("sV"))
            S.op("vector", lambda v: v.tensor_tensor(
                cring[0:2, 0:AG_SL], cring[0:2, 0:AG_SL], bcb2[:], op=ALU.add), sem="sV", inc=1)
            bcm = S.mark("sV")
            S.wait("sync", "sV", bcm)
            S.op("sync", lambda sy: sy.dma_start(out=bc_inh, in_=cring[0:2, 0:AG_SL]), sem="sDC", inc=16)
            S.wait("gpsimd", "sDC", S.mark("sDC"))
            S.op("gpsimd", lambda g: g.collective_compute(
                "AllGather", ALU.bypass, replica_groups=[list(range(NCORES))],
                ins=[bc_inh], outs=[bc_outh]), sem="sCC", inc=1)
            S.wait("sync", "sCC", S.mark("sCC"))
            S.op("sync", lambda sy: sy.dma_start(
                out=cring.ap()[0:2, 0:D].rearrange("b (r s) -> b r s", r=NCORES),
                in_=bc_outh.rearrange("(r b) s -> b r s", b=2)), sem="sDC", inc=16)
            bcrow_m = S.mark("sDC")
            if debug >= 2:
                S.wait("sync", "sDC", bcrow_m)
                S.op("sync", lambda sy: sy.dma_start(out=dbgbc, in_=cring[0:2, 0:D]), sem="sDC", inc=16)
                bcrow_m = S.mark("sDC")
            def emit_sel2_bcbc():
                S.wait("tensor", "sDC", bcrow_m)
                bm = 0
                for p in range(2):
                    if p == 1:
                        S.wait("tensor", "sV", bm)
                    for h in range(2):
                        q = 2 * p + h
                        S.op("tensor", lambda t, b_=6 + h, q_=q: t.matmul(
                            bank(b_), sel2[:], cring[0:2, q_ * 512:(q_ + 1) * 512],
                            start=True, stop=True), sem="sT" if h == 1 else None, inc=1 if h == 1 else 0)
                    pm = S.mark("sT")
                    S.wait("vector", "sT", pm)
                    for h in range(2):
                        q = 2 * p + h
                        S.op("vector", lambda v, b_=6 + h, q_=q: v.tensor_copy(
                            bcbc[:, q_ * 512:(q_ + 1) * 512], bank(b_)), sem="sV", inc=1)
                    bm = S.mark("sV")
                return bm

            S.op("sync", lambda sy: sy.dma_start(out=gamv, in_=gambh), sem="sGB", inc=16)
            S.op("sync", lambda sy: sy.dma_start(out=betv, in_=betbh), sem="sGB", inc=16)
            gb_m = S.mark("sGB")
            S.op("sync", lambda sy: sy.dma_start(
                out=xring[:, 0:D], in_=x1s_tiles[TORD[0]]), sem="sXA", inc=16)
            x1m_l[0] = ("sXA", S.mark("sXA"))
            S.op("sync", lambda sy: sy.dma_start(
                out=xring[:, D:2 * D], in_=x1s_tiles[TORD[1]]), sem="sXB", inc=16)
            x1m_l[1] = ("sXB", S.mark("sXB"))

            def combine_stages(k, bufs=None):
                tt_ = TORD[k]
                sl2 = k % 2
                if bufs is None:
                    xsem_ = "sXA" if sl2 == 0 else "sXB"
                    xt = xring[:, D * sl2: D * (sl2 + 1)]
                    crX = cring[:, 0:D] if sl2 == 0 else cring[:, D:2 * D]
                    scrX = scr2 if sl2 == 0 else scr2b
                    st0 = 0 if sl2 == 0 else 8
                else:
                    crX, scrX, st0, xt, xsem_ = bufs
                gp = trA[:, tt_ * 2048:(tt_ + 1) * 2048]
                mk = {}

                def s_gbc():
                    S.wait("gpsimd", "sS", sig_l[k])
                    if k == 0:
                        S.wait("gpsimd", "sV", bcbc_m)
                    if bufs is None and k >= 2:
                        S.wait("gpsimd", "sV", x2_l[k - 2])
                    S.op("gpsimd", lambda g, g_=gp, c_=crX: g.tensor_tensor(
                        c_, g_, bcbc[:, 0:D], op=ALU.mult), sem="sP", inc=1)
                    mk["gbc"] = S.mark("sP")

                def s_t1():
                    S.waitm("scalar", x1m_l[k])
                    if bufs is not None:
                        pass
                    elif k < 2:
                        S.wait("scalar", "sT", d2m_l[3])
                    else:
                        S.wait("scalar", "sS", ln2_l[k - 2])
                    S.op("scalar", lambda s, x=xt, t_=tt_, d_=scrX: s.activation(
                        d_, x, AF.Identity, bias=stats[:, 40 + t_:41 + t_],
                        scale=stats[:, 32 + t_:33 + t_]), sem="sS", inc=1)
                    mk["t1"] = S.mark("sS")

                def s_outt():
                    S.wait("vector", "sS", mk["t1"])
                    S.wait("vector", "sP", mk["gbc"])
                    S.op("vector", lambda v, c_=crX, d_=scrX: v.tensor_tensor(
                        c_, c_, d_, op=ALU.add), sem="sV", inc=1)
                    mk["outt"] = S.mark("sV")

                def s_stats1():
                    S.wait("vector", "sV", mk["outt"])
                    S.op("vector", lambda v, c_=crX: v.reduce_sum(
                        stats[:, st0:st0 + 1], c_, axis=AX.X), sem="sV", inc=1)
                    mk["red1"] = S.mark("sV")
                    S.wait("scalar", "sV", mk["outt"])
                    S.op("scalar", lambda s, c_=crX, d_=scrX: s.activation(
                        d_, c_, AF.Square, accum_out=stats[:, st0 + 1:st0 + 2]), sem="sS", inc=1)
                    mk["sq1"] = S.mark("sS")

                def s_ln1():
                    S.wait("vector", "sS", mk["sq1"])
                    S.wait("vector", "sV", mk["red1"])
                    mk["nb2"] = ln_stats(EPSF, st0, st0 + 1, st0 + 2, st0 + 3, st0 + 4)

                def s_yh():
                    S.wait("scalar", "sV", mk["nb2"])
                    S.op("scalar", lambda s, c_=crX, d_=scrX: s.activation(
                        d_, c_, AF.Identity, bias=stats[:, st0 + 4:st0 + 5],
                        scale=stats[:, st0 + 3:st0 + 4]), sem="sS", inc=1)
                    mk["yh"] = S.mark("sS")

                def s_gmul():
                    S.wait("vector", "sS", mk["yh"])
                    if k == 0:
                        S.wait("vector", "sGB", gb_m)
                    S.op("vector", lambda v, d_=scrX: v.tensor_tensor(
                        d_, d_, gamv, op=ALU.mult), sem="sV", inc=1)
                    mk["gmul"] = S.mark("sV")

                def s_beta():
                    S.wait("gpsimd", "sV", mk["gmul"])
                    if k == 0:
                        S.wait("gpsimd", "sGB", gb_m)
                    S.op("gpsimd", lambda g, d_=scrX: g.tensor_tensor(
                        d_, d_, betv, op=ALU.add), sem="sP", inc=1)
                    mk["y"] = S.mark("sP")

                def s_stats2():
                    S.wait("vector", "sP", mk["y"])
                    S.op("vector", lambda v, d_=scrX: v.reduce_sum(
                        stats[:, st0:st0 + 1], d_, axis=AX.X), sem="sV", inc=1)
                    mk["red2"] = S.mark("sV")
                    S.wait("scalar", "sP", mk["y"])
                    S.op("scalar", lambda s, c_=crX, d_=scrX: s.activation(
                        c_, d_, AF.Square, accum_out=stats[:, st0 + 1:st0 + 2]), sem="sS", inc=1)
                    mk["sq2"] = S.mark("sS")

                def s_ln2():
                    S.wait("vector", "sS", mk["sq2"])
                    S.wait("vector", "sV", mk["red2"])
                    mk["nb3"] = ln_stats(EPS0, st0, st0 + 1, st0 + 2, st0 + 3, st0 + 4)

                def s_t3():
                    S.wait("scalar", "sV", mk["nb3"])
                    S.op("scalar", lambda s, c_=crX, d_=scrX: s.activation(
                        c_, d_, AF.Identity, bias=stats[:, st0 + 4:st0 + 5],
                        scale=stats[:, st0 + 3:st0 + 4]), sem="sS", inc=1)
                    ln2_l[k] = S.mark("sS")

                def s_x2dma():
                    S.wait("vector", "sS", ln2_l[k])
                    S.op("vector", lambda v, x=xt, c_=crX: v.tensor_tensor(x, x, c_, op=ALU.add), sem="sV", inc=1)
                    x2_l[k] = S.mark("sV")
                    S.wait("sync", "sV", x2_l[k])
                    S.op("sync", lambda sy, x=xt, t_=tt_: sy.dma_start(
                        out=xout_tiles[t_], in_=x), sem=xsem_, inc=16)
                    out_m[k] = (xsem_, S.mark(xsem_))
                    if k + 2 < 6:
                        S.waitm("sync", out_m[k])
                        S.op("sync", lambda sy, sl=sl2, t_=TORD[k + 2]: sy.dma_start(
                            out=xring[:, D * sl: D * (sl + 1)], in_=x1s_tiles[t_]), sem=xsem_, inc=16)
                        x1m_l[k + 2] = (xsem_, S.mark(xsem_))

                return [s_gbc, s_t1, s_outt, s_stats1, s_ln1, s_yh, s_gmul,
                        s_beta, s_stats2, s_ln2, s_t3, s_x2dma]

            def emit_combine_pair(ka, kb):
                for a, b in zip(combine_stages(ka), combine_stages(kb)):
                    a()
                    b()

            def emit_combine_quad():
                crC = trQ.ap()[:, 8192:12288].bitcast(F32)
                scrC = wring.ap()[:, 0:4096].bitcast(F32)
                crD = wring.ap()[:, 4096:8192].bitcast(F32)
                scrD = trV.ap()[:, 0:4096].bitcast(F32)
                xC = trV.ap()[:, 4096:8192].bitcast(F32)
                xD = trV.ap()[:, 8192:12288].bitcast(F32)
                S.wait("sync", "sT", gpe_l[7])
                S.op("sync", lambda sy: sy.dma_start(
                    out=xC, in_=x1s_tiles[TORD[6]]), sem="sDC", inc=16)
                x1m_l[6] = ("sDC", S.mark("sDC"))
                S.op("sync", lambda sy: sy.dma_start(
                    out=xD, in_=x1s_tiles[TORD[7]]), sem="sXC", inc=16)
                x1m_l[7] = ("sXC", S.mark("sXC"))
                stl = [combine_stages(4), combine_stages(5),
                       combine_stages(6, bufs=(crC, scrC, 16, xC, "sXA")),
                       combine_stages(7, bufs=(crD, scrD, 24, xD, "sXB"))]
                for stage in range(len(stl[0])):
                    for t4 in range(4):
                        stl[t4][stage]()

            gate_pe_half(2, 0)
            gate_pe_half(2, 1)
            gate_sig(2)
            bcbc_m = emit_sel2_bcbc()
            gate_pe_half(3, 0)
            gate_pe_half(3, 1)
            gate_sig(3)
            emit_combine_pair(0, 1)
            for n in (4, 5):
                gate_pe_half(n, 0)
                gate_pe_half(n, 1)
                gate_sig(n)
            emit_combine_pair(2, 3)
            for n in (6, 7):
                gate_pe_half(n, 0)
                gate_pe_half(n, 1)
                gate_sig(n)
            emit_combine_quad()
            S.wait("sync", "sXA", S.mark("sXA"))
            S.wait("sync", "sXB", S.mark("sXB"))
            S.wait("gpsimd", "sXA", S.mark("sXA"))
            S.wait("gpsimd", "sXB", S.mark("sXB"))

        @block.tensor
        def _(t):
            S.emit("tensor", t, sems)

        @block.vector
        def _(v):
            S.emit("vector", v, sems)

        @block.scalar
        def _(s):
            S.emit("scalar", s, sems)

        @block.gpsimd
        def _(g):
            S.emit("gpsimd", g, sems)

        @block.sync
        def _(sy):
            S.emit("sync", sy, sems)

    return nc


_BF = ml_dtypes.bfloat16


def _prep_inputs(inp):
    x = np.asarray(inp["L0_attn_x"], np.float32).reshape(2 * 4096, D)
    cores = []
    iden = np.eye(128, dtype=_BF)
    maskc = np.kron(np.eye(8, dtype=np.float32), np.ones((16, 16), np.float32))
    ones2 = np.ones((1, 2), dtype=_BF)
    ones128 = np.ones((1, 128), dtype=_BF)
    tr = lambda v: np.ascontiguousarray(np.asarray(v, np.float32).reshape(NCH, 128).T)
    dup2 = lambda m: np.ascontiguousarray(np.repeat(m, 2, axis=1))
    pngt = tr(inp["L0_ffn_pn_gamma"])
    pnbt2 = dup2(tr(inp["L0_ffn_pn_beta"]))
    ingt = tr(inp["L0_ffn_in_gamma"])
    inbt2 = dup2(tr(inp["L0_ffn_in_beta"]))
    gamb = np.ascontiguousarray(np.broadcast_to(np.asarray(inp["L0_ffn_out_gamma"], np.float32), (128, D)))
    betb = np.ascontiguousarray(np.broadcast_to(np.asarray(inp["L0_ffn_out_beta"], np.float32), (128, D)))
    d2b_full = np.asarray(inp["L0_ffn_d2_b"], np.float32).reshape(1, D)
    zeros_d2b = np.zeros_like(d2b_full)
    for c in range(NCORES):
        b = c // 4
        sel2 = np.zeros((2, 128), np.float32)
        sel2[b, :] = 1.0
        m = {
            "x0": np.ascontiguousarray(x[c * T:(c + 1) * T]),
            "wq": np.asarray(inp["L0_attn_Wq"], np.float32),
            "wk": np.asarray(inp["L0_attn_Wk"], np.float32),
            "wv": np.asarray(inp["L0_attn_Wv"], np.float32),
            "wo": np.asarray(inp["L0_attn_Wo"], np.float32),
            "wg": np.asarray(inp["L0_ffn_gate_w"], np.float32),
            "d1w": np.ascontiguousarray(np.asarray(inp["L0_ffn_d1_w"], np.float32)[:, c * TH_SL:(c + 1) * TH_SL]),
            "d2w": np.ascontiguousarray(np.asarray(inp["L0_ffn_d2_w"], np.float32)[c * TH_SL:(c + 1) * TH_SL, :]),
            "aggw": np.ascontiguousarray(np.asarray(inp["L0_ffn_agg_w"], np.float32)[:, c * AG_SL:(c + 1) * AG_SL]),
            "bcw": np.ascontiguousarray(np.asarray(inp["L0_ffn_bc_w"], np.float32)[:, c * AG_SL:(c + 1) * AG_SL]),
            "d1b": np.ascontiguousarray(np.asarray(inp["L0_ffn_d1_b"], np.float32)[None, c * TH_SL:(c + 1) * TH_SL]),
            "d2b": d2b_full if c == 0 else zeros_d2b,
            "aggb": np.ascontiguousarray(np.broadcast_to(
                np.asarray(inp["L0_ffn_agg_b"], np.float32)[None, c * AG_SL:(c + 1) * AG_SL], (2, AG_SL))),
            "bcb2": np.ascontiguousarray(np.broadcast_to(
                np.asarray(inp["L0_ffn_bc_b"], np.float32)[None, c * AG_SL:(c + 1) * AG_SL], (2, AG_SL))),
            "gateb": np.asarray(inp["L0_ffn_gate_b"], np.float32).reshape(1, D),
            "gamb": gamb, "betb": betb,
            "pngt": pngt, "pnbt2": pnbt2, "ingt": ingt, "inbt2": inbt2,
            "iden": iden, "maskc": maskc, "ones2": ones2, "ones128": ones128,
            "sel2": sel2,
        }
        cores.append(m)
    return cores


def kernel(**inputs):
    in_maps = _prep_inputs(inputs)
    nc = build(stage=STAGE, debug=DEBUG)
    res = run_bass_kernel_spmd(nc, in_maps, core_ids=list(range(NCORES)))
    out = np.concatenate([np.asarray(r["xout"], np.float32) for r in res.results], axis=0)
    return out.reshape(2, 4096, D)


# version 19
# speedup vs baseline: 1.0841x; 1.0841x over previous
import numpy as np
import ml_dtypes
import concourse.bass as bass
import concourse.mybir as mybir
from concourse.bass_utils import run_bass_kernel_spmd
from contextlib import ExitStack

F32 = mybir.dt.float32
BF16 = mybir.dt.bfloat16
AF = mybir.ActivationFunctionType
ALU = mybir.AluOpType
AX = mybir.AxisListType

NCORES = 8
T = 1024
NT = 8
D = 2048
NCH = 16
EPS0 = 1e-5
EPSF = 1e-6
TH_SL = 8192 // NCORES
AG_SL = D // NCORES
TORD = [7, 0, 1, 2, 3, 4, 5, 6]

STAGE = 3
DEBUG = 0


class Sched:
    ENGS = ("tensor", "vector", "scalar", "gpsimd", "sync")

    def __init__(self):
        self.items = {e: [] for e in self.ENGS}
        self.cnt = {}

    def op(self, eng, fn, sem=None, inc=0):
        self.items[eng].append(("op", fn, sem, inc))
        if sem is not None and inc:
            self.cnt[sem] = self.cnt.get(sem, 0) + inc
        return self.cnt.get(sem, 0) if sem else 0

    def wait(self, eng, sem, val):
        assert val <= self.cnt.get(sem, 0), f"wait {sem}>={val} unreachable ({self.cnt.get(sem, 0)})"
        self.items[eng].append(("wait", sem, val))

    def waitm(self, eng, mark):
        self.wait(eng, mark[0], mark[1])

    def mark(self, sem):
        return self.cnt.get(sem, 0)

    def emit(self, eng_name, eng, sems):
        for it in self.items[eng_name]:
            if it[0] == "wait":
                eng.wait_ge(sems[it[1]], it[2])
            else:
                inst = it[1](eng)
                if it[2] is not None and it[3]:
                    inst.then_inc(sems[it[2]], it[3])


def build(stage=3, debug=0):
    nc = bass.Bass("TRN2", target_bir_lowering=False)

    I = lambda n, s, d=F32: nc.dram_tensor(n, s, d, kind="ExternalInput").ap()
    x0h = I("x0", [T, D])
    wqh, wkh, wvh = I("wq", [D, D]), I("wk", [D, D]), I("wv", [D, D])
    woh, wgh = I("wo", [D, D]), I("wg", [D, D])
    d1wh, d2wh = I("d1w", [D, TH_SL]), I("d2w", [TH_SL, D])
    aggwh, bcwh = I("aggw", [D, AG_SL]), I("bcw", [D, AG_SL])
    d1bh = I("d1b", [1, TH_SL])
    d2bh = I("d2b", [1, D])
    aggbh, bcbh = I("aggb", [2, AG_SL]), I("bcb2", [2, AG_SL])
    gatebh = I("gateb", [1, D])
    gambh, betbh = I("gamb", [128, D]), I("betb", [128, D])
    pngth, pnbt2h = I("pngt", [128, NCH]), I("pnbt2", [128, 2 * NCH])
    ingth, inbt2h = I("ingt", [128, NCH]), I("inbt2", [128, 2 * NCH])
    idenh = I("iden", [128, 128], BF16)
    maskh = I("maskc", [128, 128])
    ones2h = I("ones2", [1, 2], BF16)
    ones128h = I("ones128", [1, 128], BF16)
    sel2h = I("sel2", [2, 128])

    xouth = nc.dram_tensor("xout", [T, D], F32, kind="ExternalOutput").ap()
    dbgbc = nc.dram_tensor("dbg_bc", [2, D], F32, kind="ExternalOutput").ap() if debug >= 2 else None

    x1scr = nc.dram_tensor("x1scr", [T, D], F32).ap()
    ag1_in = nc.dram_tensor("ag1_in", [1, D], F32).ap()
    ag1_out = nc.dram_tensor("ag1_out", [NCORES, D], F32, addr_space="Shared").ap()
    agg_in = nc.dram_tensor("agg_in", [2, AG_SL], F32).ap()
    agg_out = nc.dram_tensor("agg_out", [2 * NCORES, AG_SL], F32, addr_space="Shared").ap()
    ar_inh = [nc.dram_tensor(f"ar_in{s}", [2, D], F32).ap() for s in range(4)]
    ar_outh = [nc.dram_tensor(f"ar_out{s}", [2, D], F32, addr_space="Shared").ap() for s in range(4)]
    bc_inh = nc.dram_tensor("bc_in", [2, AG_SL], F32).ap()
    bc_outh = nc.dram_tensor("bc_out", [2 * NCORES, AG_SL], F32, addr_space="Shared").ap()

    with ExitStack() as stk:
        e = stk.enter_context
        trA = e(nc.sbuf_tensor("trA", [128, 16384], BF16))
        trQ = e(nc.sbuf_tensor("trQ", [128, 16384], BF16))
        trK = e(nc.sbuf_tensor("trK", [128, 16384], BF16))
        trV = e(nc.sbuf_tensor("trV", [128, 16384], BF16))
        xring = e(nc.sbuf_tensor("xring", [128, 2 * D], F32))
        cring = e(nc.sbuf_tensor("cring", [128, 2 * D], F32))
        wring = e(nc.sbuf_tensor("wring", [128, 8192], BF16))
        chsc = e(nc.sbuf_tensor("chsc", [128, 4096], BF16))
        afbf = e(nc.sbuf_tensor("afbf", [128, 2 * D], BF16))
        iden = e(nc.sbuf_tensor("idn", [128, 128], BF16))
        maskc = e(nc.sbuf_tensor("msk", [128, 128], F32))
        ones2 = e(nc.sbuf_tensor("o2", [1, 2], BF16))
        ones128 = e(nc.sbuf_tensor("o128", [1, 128], BF16))
        sel2 = e(nc.sbuf_tensor("sel2s", [2, 128], F32))
        pngt = e(nc.sbuf_tensor("pngts", [128, NCH], F32))
        pnbt2 = e(nc.sbuf_tensor("pnbt2s", [128, 2 * NCH], BF16))
        ingt = e(nc.sbuf_tensor("ingts", [128, NCH], F32))
        inbt2 = e(nc.sbuf_tensor("inbt2s", [128, 2 * NCH], BF16))
        d1bbf = e(nc.sbuf_tensor("d1bbf", [1, TH_SL], BF16))
        d2brow = e(nc.sbuf_tensor("d2brow", [1, D], BF16))
        aggb2 = e(nc.sbuf_tensor("aggb2s", [2, AG_SL], F32))
        bcb2 = e(nc.sbuf_tensor("bcb2s", [2, AG_SL], F32))
        ebd1 = e(nc.sbuf_tensor("ebd1", [1, TH_SL], BF16))
        xtn = e(nc.sbuf_tensor("xtn", [128, 48], BF16))
        stats = e(nc.sbuf_tensor("stats", [128, 64], F32))
        srow = e(nc.sbuf_tensor("srow", [2, 16], F32))
        wjit2 = e(nc.sbuf_tensor("wjit2", [128, 4 * AG_SL], BF16))
        sqacc = e(nc.sbuf_tensor("sqacc", [128, 8], F32))
        PS = e(nc.psum_tensor("PS", [128, 4096], F32))

        semnames = ["sT", "sV", "sS", "sP", "sCC", "sSM", "sXA", "sXB", "sW0", "sW1", "sW2", "sW3", "sWR", "sWR2", "sD1", "sDC", "sD2", "sGW", "sGB", "sGV", "sGQ", "sXC"]
        sems = {n: e(nc.semaphore(n)) for n in semnames}
        block = e(nc.Block())
        S = Sched()

        def bank(b, off=0, n=512):
            return PS[:, 512 * b + off: 512 * b + off + n]

        def bankbf(b, n=128):
            return PS.ap()[:, 512 * b: 512 * b + (n + 1) // 2].bitcast(BF16)

        def bankrow(b, rows=2, n=512):
            return PS[0:rows, 512 * b: 512 * b + n]

        thought = trK.ap()[:, 10240:14336].bitcast(F32)[0:2, :]
        gamv = trK.ap()[:, 0:4096].bitcast(F32)
        betv = trK.ap()[:, 4096:8192].bitcast(F32)
        tb8 = chsc.ap()[0:2, 0:2048]
        urow = cring.ap()[0:2, D:2 * D]
        trow = chsc.ap()[:, 2048:4096].bitcast(F32)[0:2, :]
        xnbf = chsc.ap()[0:2, 0:2048]
        bcbc = chsc.ap().bitcast(F32)
        scr2 = trQ.ap()[:, 0:4096].bitcast(F32)
        scr2b = trQ.ap()[:, 4096:8192].bitcast(F32)
        gbrow = trK.ap()[0:1, 8192:10240]
        agslot = [wjit2.ap()[:, 0:4 * AG_SL], chsc.ap()[:, 2048:2048 + 4 * AG_SL]]

        x0_tiles = x0h.rearrange("(n p) d -> n p d", p=128)
        xout_tiles = xouth.rearrange("(n p) d -> n p d", p=128)
        x1s_tiles = x1scr.rearrange("(n p) d -> n p d", p=128)

        for dst, src in [(iden[:], idenh), (maskc[:], maskh), (ones2[:], ones2h),
                         (ones128[:], ones128h), (sel2[:], sel2h), (pngt[:], pngth),
                         (pnbt2[:], pnbt2h), (ingt[:], ingth), (inbt2[:], inbt2h),
                         (d1bbf[:], d1bh), (d2brow[:], d2bh), (aggb2[:], aggbh),
                         (bcb2[:], bcbh)]:
            S.op("gpsimd", lambda g, d=dst, s=src: g.dma_start(out=d, in_=s), sem="sSM", inc=16)
        sm_done = ("sSM", S.mark("sSM"))

        def ln_stats(eps, su, sq, m2, rs, nb, rows=stats):
            S.op("vector", lambda v, a=su: v.tensor_scalar(
                rows[:, a:a + 1], rows[:, a:a + 1], 1.0 / D, 0.0, op0=ALU.mult, op1=ALU.add), sem="sV", inc=1)
            S.op("vector", lambda v, a=sq, ee=eps: v.tensor_scalar(
                rows[:, a:a + 1], rows[:, a:a + 1], 1.0 / D, ee, op0=ALU.mult, op1=ALU.add), sem="sV", inc=1)
            S.wait("vector", "sV", S.mark("sV"))
            S.op("vector", lambda v, a=su, b=m2: v.tensor_tensor(
                rows[:, b:b + 1], rows[:, a:a + 1], rows[:, a:a + 1], op=ALU.mult), sem="sV", inc=1)
            S.wait("vector", "sV", S.mark("sV"))
            S.op("vector", lambda v, a=sq, b=m2: v.tensor_tensor(
                rows[:, a:a + 1], rows[:, a:a + 1], rows[:, b:b + 1], op=ALU.subtract), sem="sV", inc=1)
            S.wait("scalar", "sV", S.mark("sV"))
            S.op("scalar", lambda s, a=sq, b=rs: s.activation(
                rows[:, b:b + 1], rows[:, a:a + 1], AF.Ln), sem="sS", inc=1)
            S.wait("scalar", "sS", S.mark("sS"))
            S.op("scalar", lambda s, b=rs: s.activation(
                rows[:, b:b + 1], rows[:, b:b + 1], AF.Exp, scale=-0.5), sem="sS", inc=1)
            S.wait("vector", "sS", S.mark("sS"))
            S.op("vector", lambda v, a=su, b=rs, c=nb: v.tensor_scalar(
                rows[:, c:c + 1], rows[:, a:a + 1], rows[:, b:b + 1], -1.0,
                op0=ALU.mult, op1=ALU.mult), sem="sV", inc=1)
            return S.mark("sV")


        def finish_passthrough(wait_list):
            for sem_, val_ in wait_list:
                S.wait("sync", sem_, val_)
            for n in range(NT):
                S.wait("sync", "sDC", S.mark("sDC"))
                S.op("sync", lambda sy, t_=n: sy.dma_start(
                    out=xring[:, 0:D], in_=x0_tiles[t_]), sem="sDC", inc=16)
                S.wait("sync", "sDC", S.mark("sDC"))
                S.op("sync", lambda sy, t_=n: sy.dma_start(
                    out=xout_tiles[t_], in_=xring[:, 0:D]), sem="sDC", inc=16)
            S.wait("sync", "sDC", S.mark("sDC"))
            S.wait("gpsimd", "sDC", S.mark("sDC"))

        evac_ph0 = []
        cast_ph0 = []
        trdone_ph0 = []
        for idx in range(NT):
            tt = idx
            slot = idx % 2
            xsem = "sXA" if slot == 0 else "sXB"
            if idx >= 2:
                S.wait("sync", "sS", cast_ph0[idx - 2])
            S.op("sync", lambda sy, sl=slot, t=tt: sy.dma_start(
                out=xring[:, D * sl: D * (sl + 1)], in_=x0_tiles[t]), sem=xsem, inc=16)
            xm = (xsem, S.mark(xsem))
            xt = xring[:, D * slot: D * (slot + 1)]
            S.waitm("vector", xm)
            S.op("vector", lambda v, x=xt: v.reduce_sum(stats[:, 0:1], x, axis=AX.X), sem="sV", inc=1)
            sum_m = S.mark("sV")
            S.waitm("scalar", xm)
            if idx >= 1:
                S.wait("scalar", "sV", sq_consumed)
            S.op("scalar", lambda s, x=xt: s.activation(
                cring[:, 0:D], x, AF.Square, accum_out=stats[:, 1:2]), sem="sS", inc=1)
            sq_m = S.mark("sS")
            S.wait("vector", "sS", sq_m)
            S.wait("vector", "sV", sum_m)
            nb_m = ln_stats(EPS0, 0, 1, 2, 3, 4)
            sq_consumed = nb_m
            abf = afbf[:, slot * D:(slot + 1) * D]
            S.wait("scalar", "sV", nb_m)
            if idx >= 2:
                S.wait("scalar", "sT", trdone_ph0[idx - 2])
            S.op("scalar", lambda s, x=xt, dd=abf: s.activation(
                dd, x, AF.Identity, bias=stats[:, 4:5], scale=stats[:, 3:4]), sem="sS", inc=1)
            cast_ph0.append(S.mark("sS"))
            S.wait("tensor", "sS", cast_ph0[-1])
            if idx == 0:
                S.waitm("tensor", sm_done)
            for i in range(NCH):
                b = 6 + (i % 2)
                k = idx * NCH + i
                if k >= 2:
                    S.wait("tensor", "sV", evac_ph0[k - 2])
                S.op("tensor", lambda t, bb=b, aa=abf[:, i * 128:(i + 1) * 128]:
                     t.transpose(bankbf(bb), aa, iden[:]), sem="sT", inc=1)
                tm = S.mark("sT")
                S.wait("vector", "sT", tm)
                dst = trA[:, i * T + tt * 128: i * T + (tt + 1) * 128]
                S.op("vector", lambda v, bb=b, dd=dst: v.tensor_copy(dd, bankbf(bb)), sem="sV", inc=1)
                evac_ph0.append(S.mark("sV"))
            trdone_ph0.append(S.mark("sT"))
        atr_done = S.mark("sV")
        if stage == 0.1:
            finish_passthrough([("sV", atr_done), ("sS", S.mark("sS")), ("sT", S.mark("sT"))])
        if stage < 0.2 and stage != 1 and stage != 3:
            qkv_skip = True
        whs = [wqh, wkh, wvh]
        trX = [trQ, trK, trV]
        NSLOT = 4
        blk_ready, blk_used, evacm = [], [], []
        for j in range(NCH if stage >= 0.2 else 0):
            for w in range(3):
                k = j * 3 + w
                slot = k % NSLOT
                if k >= NSLOT:
                    S.wait("gpsimd", "sT", blk_used[k - NSLOT])
                src = whs[w].rearrange("(i p) m -> p i m", p=128)[:, :, j * 128:(j + 1) * 128]
                dst = wring.ap().rearrange("p (s i m) -> p s i m", s=NSLOT, i=NCH)[:, slot]
                wsem = f"sW{slot}"
                S.op("gpsimd", lambda g, d=dst, s=src: g.dma_start(out=d, in_=s), sem=wsem, inc=16)
                blk_ready.append((wsem, S.mark(wsem)))
                pair = k % 3
                b0, b1 = 2 * pair, 2 * pair + 1
                S.waitm("tensor", blk_ready[-1])
                if k == 0:
                    S.wait("tensor", "sV", atr_done)
                if k >= 3:
                    sem_, val_ = evacm[k - 3]
                    S.wait("tensor", sem_, val_)
                for i in range(NCH):
                    lhs = wring[:, slot * NCH * 128 + i * 128: slot * NCH * 128 + (i + 1) * 128]
                    S.op("tensor", lambda t, l=lhs, r=trA[:, i * T: i * T + 512], bb=b0,
                         st=(i == 0), sp=(i == NCH - 1): t.matmul(bank(bb), l, r, start=st, stop=sp))
                    last = (i == NCH - 1)
                    S.op("tensor", lambda t, l=lhs, r=trA[:, i * T + 512: i * T + 1024], bb=b1,
                         st=(i == 0), sp=(i == NCH - 1): t.matmul(bank(bb), l, r, start=st, stop=sp),
                         sem="sT" if last else None, inc=1 if last else 0)
                pe_m = S.mark("sT")
                blk_used.append(pe_m)
                dst2 = trX[w].ap().rearrange("p (t h) -> p h t", h=NCH)[:, j, :]
                if k % 2 == 0:
                    S.wait("vector", "sT", pe_m)
                    S.op("vector", lambda v, dd=dst2[:, 0:512], bb=b0: v.tensor_copy(dd, bank(bb)), sem="sV", inc=1)
                    S.op("vector", lambda v, dd=dst2[:, 512:1024], bb=b1: v.tensor_copy(dd, bank(bb)), sem="sV", inc=1)
                    evacm.append(("sV", S.mark("sV")))
                else:
                    S.wait("scalar", "sT", pe_m)
                    S.op("scalar", lambda s, dd=dst2[:, 0:512], bb=b0: s.copy(dd, bank(bb)), sem="sS", inc=1)
                    S.op("scalar", lambda s, dd=dst2[:, 512:1024], bb=b1: s.copy(dd, bank(bb)), sem="sS", inc=1)
                    evacm.append(("sS", S.mark("sS")))
        qkv_evV, qkv_evS = S.mark("sV"), S.mark("sS")
        if stage == 0.2:
            finish_passthrough([("sV", qkv_evV), ("sS", qkv_evS), ("sT", S.mark("sT"))])

        NG = T // 8
        scores_pe, eins_done = [], []
        sbd_m, vp_m = [], []
        if stage >= 0.3:
            S.wait("tensor", "sV", qkv_evV)
            S.wait("tensor", "sS", qkv_evS)
        pend = None
        for g in range(NG if stage >= 0.3 else 0):
            c0 = g * 128
            bS, bV, bO = (g % 2), 2 + (g % 2), 4 + (g % 2)
            sbd = afbf.ap()[:, (g % 2) * 128: (g % 2) * 128 + 128]
            vp = afbf.ap()[:, 256 + (g % 2) * 128: 256 + (g % 2) * 128 + 128]
            if g >= 2:
                S.wait("tensor", "sV", sbd_m[g - 2])
                sem_, val_ = vp_m[g - 2]
                S.wait("tensor", sem_, val_)
            if g >= 3:
                S.wait("tensor", "sS", eins_done[g - 3])
            S.op("tensor", lambda t, c=c0, bb=bS: t.matmul(
                bank(bb, 0, 128), trK[:, c: c + 128], trQ[:, c: c + 128],
                start=True, stop=True), sem="sT", inc=1)
            scores_pe.append(S.mark("sT"))
            S.op("tensor", lambda t, c=c0, bb=bV: t.transpose(
                bankbf(bb), trV[:, c: c + 128], iden[:]), sem="sT", inc=1)
            vpt = S.mark("sT")
            S.wait("vector", "sT", scores_pe[-1])
            S.op("vector", lambda v, bb=bS, dd=sbd: v.tensor_tensor(
                dd, bank(bb, 0, 128), maskc[:], op=ALU.mult), sem="sV", inc=1)
            sbd_m.append(S.mark("sV"))
            S.wait("scalar", "sT", vpt)
            S.op("scalar", lambda s, bb=bV, dd=vp: s.copy(dd, bankbf(bb)), sem="sS", inc=1)
            vp_m.append(("sS", S.mark("sS")))
            if pend is not None:
                pg, pvp, psbd, pbO = pend
                S.wait("tensor", "sV", sbd_m[pg])
                S.waitm("tensor", vp_m[pg])
                S.op("tensor", lambda t, vv=pvp, ss_=psbd, bb=pbO: t.matmul(
                    bank(bb, 0, 128), vv, ss_, start=True, stop=True), sem="sT", inc=1)
                om = S.mark("sT")
                S.wait("scalar", "sT", om)
                S.op("scalar", lambda s, bb=pbO, c=pg * 128: s.copy(trA[:, c: c + 128], bank(bb, 0, 128)),
                     sem="sS", inc=1)
                eins_done.append(S.mark("sS"))
            pend = (g, vp, sbd, bO)
        if pend is not None:
            pg, pvp, psbd, pbO = pend
            S.wait("tensor", "sV", sbd_m[pg])
            S.waitm("tensor", vp_m[pg])
            S.op("tensor", lambda t, vv=pvp, ss_=psbd, bb=pbO: t.matmul(
                bank(bb, 0, 128), vv, ss_, start=True, stop=True), sem="sT", inc=1)
            om = S.mark("sT")
            S.wait("scalar", "sT", om)
            S.op("scalar", lambda s, bb=pbO, c=pg * 128: s.copy(trA[:, c: c + 128], bank(bb, 0, 128)),
                 sem="sS", inc=1)
            eins_done.append(S.mark("sS"))

        if stage == 0.3:
            finish_passthrough([("sS", S.mark("sS")), ("sT", S.mark("sT")), ("sV", S.mark("sV"))])
        def wo_dst(i):
            if i < 7:
                return trQ.ap()[:, i * 2048:(i + 1) * 2048]
            if i < 12:
                return trK.ap()[:, (i - 7) * 2048:(i - 6) * 2048]
            return wring.ap()[:, (i - 12) * 2048:(i - 11) * 2048]

        wo_ready = []
        for i in range(NCH if stage >= 0.4 else 0):
            if i < 7:
                S.wait("gpsimd", "sT", scores_pe[16 * i + 15])
            elif i < 12:
                S.wait("gpsimd", "sT", scores_pe[16 * (i - 7) + 15])
            else:
                S.wait("gpsimd", "sT", blk_used[-1])
            src_ = woh.rearrange("(i p) m -> p i m", p=128)[:, i, :]
            wsem_ = "sWR" if i < 8 else "sWR2"
            S.op("gpsimd", lambda g, d=wo_dst(i), s=src_: g.dma_start(out=d, in_=s), sem=wsem_, inc=16)
            wo_ready.append(S.mark(wsem_))
        wo_res = ("sWR", S.mark("sWR"))
        wo_res2 = ("sWR2", S.mark("sWR2"))

        d1src = d1wh.rearrange("(i p) m -> p i m", p=128)
        if stage >= 0.4:
            S.wait("gpsimd", "sT", S.mark("sT"))
        for i4 in range(4 if stage >= 0.4 else 0):
            S.op("gpsimd", lambda g, d=trV.ap().rearrange("p (i m) -> p i m", i=NCH)[:, i4 * 4:(i4 + 1) * 4],
                 s=d1src[:, i4 * 4:(i4 + 1) * 4]: g.dma_start(out=d, in_=s), sem="sD1", inc=16)
        d1_in = ("sD1", S.mark("sD1"))

        x1_built, x1_spilled, x0_ld = [], [], []
        fstat_nb = [0] * NT
        ftr_done = [0] * NT
        wo_banksfree = []
        if stage == 0.4:
            finish_passthrough([("sWR", S.mark("sWR")), ("sD1", S.mark("sD1")), ("sT", S.mark("sT"))])
        fb_marks = [0] * NT
        fsq_l = [0] * NT
        tmark = [0] * NT
        red_done_l = [0] * NT
        stg_done_l = [0] * NT
        red_done = 0
        stg_done = 0

        def emit_ftr(m):
            ttm_ = TORD[m]
            s0m = (m * 4) % 6
            tbanks = [(s0m + 2) % 6, (s0m + 3) % 6]
            fbm_ = fb_marks[m]
            fbuf = afbf[:, (m % 2) * D: (m % 2 + 1) * D]
            S.wait("tensor", "sS", fbm_)
            S.wait("tensor", "sV", red_done_l[m])
            ev = []
            for i in range(NCH):
                b_ = tbanks[i % 2]
                if i >= 2:
                    S.wait("tensor", "sV", ev[i - 2])
                S.op("tensor", lambda t, bb=b_, aa=fbuf[:, i * 128:(i + 1) * 128]:
                     t.transpose(bankbf(bb), aa, iden[:]), sem="sT", inc=1)
                tm_ = S.mark("sT")
                S.wait("vector", "sT", tm_)
                dst = trA[:, ttm_ * 2048 + i * 128: ttm_ * 2048 + (i + 1) * 128]
                S.op("vector", lambda v, bb=b_, dd=dst: v.tensor_copy(dd, bankbf(bb)), sem="sV", inc=1)
                ev.append(S.mark("sV"))
            tmark[m] = S.mark("sT")
            ftr_done[TORD[m]] = S.mark("sV")

        for n, tt in enumerate(TORD if stage >= 0.5 else []):
            slot = n % 2
            xsem = "sXA" if slot == 0 else "sXB"
            pb = 8 * (n % 2)
            if n >= 2:
                S.waitm("sync", x1_spilled[n - 2])
                S.wait("sync", "sS", fb_marks[n - 2])
            else:
                S.wait("sync", "sS", cast_ph0[6 + n])
            S.op("sync", lambda sy, sl=slot, t=tt: sy.dma_start(
                out=xring[:, D * sl: D * (sl + 1)], in_=x0_tiles[t]), sem=xsem, inc=16)
            x0_ld.append((xsem, S.mark(xsem)))

            s0 = (n * 4) % 6
            banks = [(s0 + q) % 6 for q in range(4)]
            if n == 0:
                S.waitm("tensor", wo_res)
            S.wait("tensor", "sS", eins_done[16 * tt + 15])
            if n >= 1:
                S.wait("tensor", "sV", red_done_l[n - 1])
            if n >= 2:
                S.wait("tensor", "sV", ftr_done[TORD[n - 2]])
            for i in range(NCH):
                if n == 0 and i == 8:
                    S.waitm("tensor", wo_res2)
                lhs = trA.ap().rearrange("p (t h) -> p h t", h=NCH)[:, i, tt * 128:(tt + 1) * 128]
                wsrc = wo_dst(i)
                for q in range(4):
                    last = (i == NCH - 1 and q == 3)
                    S.op("tensor", lambda t, l=lhs, r=wsrc[:, q * 512:(q + 1) * 512], bb=banks[q],
                         st=(i == 0), sp=(i == NCH - 1): t.matmul(bank(bb), l, r, start=st, stop=sp),
                         sem="sT" if last else None, inc=1 if last else 0)
            wo_pe = S.mark("sT")
            if n >= 1 and stage >= 1:
                emit_ftr(n - 1)
            S.wait("vector", "sT", wo_pe)
            if n >= 1:
                S.wait("vector", "sS", fsq_l[n - 1])
            for q in range(4):
                S.op("vector", lambda v, bb=banks[q], q_=q: v.tensor_copy(
                    cring[:, q_ * 512:(q_ + 1) * 512], bank(bb)), sem="sV", inc=1)
            red_done = S.mark("sV")
            red_done_l[n] = red_done
            S.wait("vector", "sV", red_done)
            S.op("vector", lambda v, a=pb: v.reduce_sum(stats[:, a:a + 1], cring[:, 0:D], axis=AX.X), sem="sV", inc=1)
            S.wait("scalar", "sV", red_done)
            if n >= 1:
                S.wait("scalar", "sV", x1_built[-1])
            S.op("scalar", lambda s, a=pb: s.activation(
                cring[:, D:2 * D], cring[:, 0:D], AF.Square, accum_out=stats[:, a + 1:a + 2]), sem="sS", inc=1)
            S.wait("vector", "sS", S.mark("sS"))
            S.wait("vector", "sV", S.mark("sV"))
            nb_m = ln_stats(EPS0, pb, pb + 1, pb + 2, pb + 3, pb + 4)
            S.wait("scalar", "sV", nb_m)
            S.op("scalar", lambda s, a=pb: s.activation(
                cring[:, 0:D], cring[:, 0:D], AF.Identity,
                bias=stats[:, a + 4:a + 5], scale=stats[:, a + 3:a + 4]), sem="sS", inc=1)
            stg_done = S.mark("sS")
            stg_done_l[n] = stg_done
            S.wait("vector", "sS", stg_done)
            S.waitm("vector", x0_ld[n])
            xt = xring[:, D * slot: D * (slot + 1)]
            S.op("vector", lambda v, x=xt: v.tensor_tensor(x, x, cring[:, 0:D], op=ALU.add), sem="sV", inc=1)
            x1_built.append(S.mark("sV"))
            x1t = xt
            S.wait("vector", "sV", x1_built[-1])
            S.op("vector", lambda v, x=x1t, a=pb: v.reduce_sum(stats[:, a:a + 1], x, axis=AX.X), sem="sV", inc=1)
            S.wait("scalar", "sV", x1_built[-1])
            S.op("scalar", lambda s, x=x1t, a=pb: s.activation(
                cring[:, 0:D], x, AF.Square, accum_out=stats[:, a + 1:a + 2]), sem="sS", inc=1)
            fsq_l[n] = S.mark("sS")
            S.wait("vector", "sS", S.mark("sS"))
            S.wait("vector", "sV", S.mark("sV"))
            nbf = ln_stats(EPS0, pb, pb + 1, pb + 2, 32 + tt, 40 + tt)
            fstat_nb[tt] = nbf
            fb = afbf[:, (n % 2) * D: (n % 2 + 1) * D]
            S.wait("scalar", "sV", nbf)
            if n >= 2 and stage >= 1:
                S.wait("scalar", "sT", tmark[n - 2])
            S.op("scalar", lambda s, x=x1t, dd=fb, t_=tt: s.activation(
                dd, x, AF.Identity, bias=stats[:, 40 + t_:41 + t_], scale=stats[:, 32 + t_:33 + t_]),
                sem="sS", inc=1)
            fb_marks[n] = S.mark("sS")
            if stage < 1:
                ftr_done[tt] = S.mark("sV")
            S.wait("sync", "sV", x1_built[-1])
            S.op("sync", lambda sy, sl=slot, t_=tt: sy.dma_start(
                out=x1s_tiles[t_], in_=xring[:, D * sl: D * (sl + 1)]), sem=xsem, inc=16)
            x1_spilled.append((xsem, S.mark(xsem)))
        if stage >= 1:
            emit_ftr(NT - 1)
        wo_all_pe = S.mark("sT")
        if stage in (0.5, 0.51, 0.52, 0.53, 0.55, 0.57, 0.58, 0.6):
            finish_passthrough([("sT", wo_all_pe), ("sS", S.mark("sS")), ("sV", S.mark("sV")),
                                ("sXA", S.mark("sXA")), ("sXB", S.mark("sXB"))])
        if stage < 1:
            pass
        elif stage == 1:
            S.wait("sync", "sXA", S.mark("sXA"))
            S.wait("sync", "sXB", S.mark("sXB"))
            for n in range(NT):
                S.wait("sync", "sDC", S.mark("sDC"))
                S.op("sync", lambda sy, t_=n: sy.dma_start(
                    out=xring[:, 0:D], in_=x1s_tiles[t_]), sem="sDC", inc=16)
                S.wait("sync", "sDC", S.mark("sDC"))
                S.op("sync", lambda sy, t_=n: sy.dma_start(
                    out=xout_tiles[t_], in_=xring[:, 0:D]), sem="sDC", inc=16)
            S.wait("sync", "sDC", S.mark("sDC"))
            S.wait("gpsimd", "sDC", S.mark("sDC"))
        else:
            wg_src = wgh.rearrange("(i p) m -> p i m", p=128)
            S.wait("gpsimd", "sT", wo_all_pe)
            d2src = d2wh.rearrange("(c p) m -> p c m", p=128)
            for c4 in range(4):
                S.op("gpsimd", lambda g, d=trQ.ap().rearrange("p (c m) -> p c m", c=8)[:, c4 * 2:(c4 + 1) * 2],
                     s=d2src[:, c4 * 2:(c4 + 1) * 2]: g.dma_start(out=d, in_=s), sem="sD2", inc=16)
            d2s_in = S.mark("sD2")
            def gw_dst(c):
                if c < 8:
                    return trV.ap()[:, c * 2048:(c + 1) * 2048]
                if c < 12:
                    return wring.ap()[:, (c - 8) * 2048:(c - 7) * 2048]
                if c == 12:
                    return trK.ap()[:, 14336:16384]
                if c < 15:
                    return afbf.ap()[:, (c - 13) * 2048:(c - 12) * 2048]
                return trQ.ap()[:, 12288:14336]
            S.wait("gpsimd", "sV", max(ftr_done))
            for c in range(8, 15):
                S.op("gpsimd", lambda g, d=gw_dst(c), s=wg_src[:, c, :]: g.dma_start(out=d, in_=s),
                     sem="sGW", inc=16)
            gw_early = S.mark("sGW")
            S.waitm("tensor", d1_in)
            for i in range(NCH):
                for hh in range(2):
                    S.op("tensor", lambda t, i_=i, h_=hh: t.matmul(
                        bankrow(6 + h_, 2, 512), pnbt2[:, 2 * i_:2 * i_ + 2],
                        trV[:, i_ * 1024 + h_ * 512: i_ * 1024 + (h_ + 1) * 512],
                        start=(i_ == 0), stop=False), sem=None, inc=0)
            for hh in range(2):
                S.op("tensor", lambda t, h_=hh: t.matmul(
                    bankrow(6 + h_, 2, 512), ones2[:], d1bbf[0:1, h_ * 512:(h_ + 1) * 512],
                    start=False, stop=True), sem="sT" if hh == 1 else None, inc=1 if hh == 1 else 0)
            ebmm = S.mark("sT")
            S.wait("scalar", "sT", ebmm)
            S.op("scalar", lambda s: s.copy(ebd1[0:1, 0:512], bankrow(6, 1, 512)), sem="sS", inc=1)
            S.op("scalar", lambda s: s.copy(ebd1[0:1, 512:1024], bankrow(7, 1, 512)), sem="sS", inc=1)
            ebd1_m = S.mark("sS")
            S.wait("vector", "sT", ebmm)
            for i in range(NCH):
                S.op("vector", lambda v, i_=i: v.tensor_scalar(
                    trV[:, i_ * 1024:(i_ + 1) * 1024], trV[:, i_ * 1024:(i_ + 1) * 1024],
                    pngt[:, i_:i_ + 1], 0.0, op0=ALU.mult, op1=ALU.add), sem="sV", inc=1)
            d1s_ready = S.mark("sV")

            S.wait("vector", "sS", S.mark("sS"))
            S.op("vector", lambda v: v.memset(urow[:], 0.0), sem="sV", inc=1)
            um0 = S.mark("sV")
            S.wait("sync", "sV", um0)
            S.wait("sync", "sXA", S.mark("sXA"))
            S.wait("sync", "sXB", S.mark("sXB"))
            S.op("sync", lambda sy: sy.dma_start(out=urow[0:1, :], in_=x1scr[1023:1024, :]),
                 sem="sDC", inc=16)
            fl_m = ("sDC", S.mark("sDC"))

            def row_ln(src_row, dst_row, eps, sc=0):
                S.op("vector", lambda v: v.reduce_sum(srow[0:2, sc:sc + 1], src_row, axis=AX.X),
                     sem="sV", inc=1)
                sm = S.mark("sV")
                S.op("scalar", lambda s: s.activation(
                    cring[0:2, 0:D], src_row, AF.Square,
                    accum_out=srow[0:2, sc + 1:sc + 2]), sem="sS", inc=1)
                qm = S.mark("sS")
                S.wait("vector", "sV", sm)
                S.wait("vector", "sS", qm)
                S.op("vector", lambda v: v.tensor_scalar(
                    srow[0:2, sc:sc + 1], srow[0:2, sc:sc + 1], 1.0 / D, 0.0, op0=ALU.mult, op1=ALU.add), sem="sV", inc=1)
                S.op("vector", lambda v, ee=eps: v.tensor_scalar(
                    srow[0:2, sc + 1:sc + 2], srow[0:2, sc + 1:sc + 2], 1.0 / D, ee, op0=ALU.mult, op1=ALU.add), sem="sV", inc=1)
                S.wait("vector", "sV", S.mark("sV"))
                S.op("vector", lambda v: v.tensor_tensor(
                    srow[0:2, sc + 2:sc + 3], srow[0:2, sc:sc + 1], srow[0:2, sc:sc + 1], op=ALU.mult), sem="sV", inc=1)
                S.wait("vector", "sV", S.mark("sV"))
                S.op("vector", lambda v: v.tensor_tensor(
                    srow[0:2, sc + 1:sc + 2], srow[0:2, sc + 1:sc + 2], srow[0:2, sc + 2:sc + 3], op=ALU.subtract), sem="sV", inc=1)
                S.wait("scalar", "sV", S.mark("sV"))
                S.op("scalar", lambda s: s.activation(
                    srow[0:2, sc + 3:sc + 4], srow[0:2, sc + 1:sc + 2], AF.Ln), sem="sS", inc=1)
                S.wait("scalar", "sS", S.mark("sS"))
                S.op("scalar", lambda s: s.activation(
                    srow[0:2, sc + 3:sc + 4], srow[0:2, sc + 3:sc + 4], AF.Exp, scale=-0.5), sem="sS", inc=1)
                S.wait("vector", "sS", S.mark("sS"))
                S.op("vector", lambda v: v.tensor_scalar(
                    srow[0:2, sc + 4:sc + 5], srow[0:2, sc:sc + 1], srow[0:2, sc + 3:sc + 4], -1.0,
                    op0=ALU.mult, op1=ALU.mult), sem="sV", inc=1)
                S.wait("scalar", "sV", S.mark("sV"))
                S.op("scalar", lambda s: s.activation(
                    dst_row, src_row, AF.Identity, bias=srow[0:2, sc + 4:sc + 5],
                    scale=srow[0:2, sc + 3:sc + 4]), sem="sS", inc=1)
                return S.mark("sS")

            S.waitm("vector", fl_m)
            S.waitm("scalar", fl_m)
            m1 = row_ln(urow[:], urow[:], EPS0, sc=0)
            S.wait("vector", "sS", m1)
            S.wait("scalar", "sS", m1)
            m2 = row_ln(urow[:], cring[0:2, 0:D], EPSF, sc=5)
            S.wait("sync", "sS", m2)
            S.op("sync", lambda sy: sy.dma_start(out=ag1_in, in_=cring[0:1, 0:D]), sem="sDC", inc=16)
            S.wait("gpsimd", "sDC", S.mark("sDC"))
            S.op("gpsimd", lambda g: g.collective_compute(
                "AllGather", ALU.bypass, replica_groups=[list(range(NCORES))],
                ins=[ag1_in], outs=[ag1_out]), sem="sCC", inc=1)
            ag1_m = S.mark("sCC")
            S.wait("sync", "sCC", ag1_m)
            S.op("sync", lambda sy: sy.dma_start(out=cring[0:1, 0:D], in_=ag1_out[3:4, :]), sem="sDC", inc=16)
            S.op("sync", lambda sy: sy.dma_start(out=cring[1:2, 0:D], in_=ag1_out[7:8, :]), sem="sDC", inc=16)
            xn_m = ("sDC", S.mark("sDC"))
            S.waitm("vector", xn_m)
            S.op("vector", lambda v: v.tensor_copy(xnbf, cring[0:2, 0:D]), sem="sV", inc=1)
            xnbf_m = S.mark("sV")
            S.wait("tensor", "sV", xnbf_m)
            xe = []
            for i in range(NCH):
                bb = 6 + (i % 2)
                if i >= 2:
                    S.wait("tensor", "sV", xe[i - 2])
                S.op("tensor", lambda t, b_=bb, aa=xnbf[:, i * 128:(i + 1) * 128]:
                     t.transpose(PS.ap()[:, 512 * b_:512 * b_ + 1].bitcast(BF16), aa, iden[0:2, 0:2]),
                     sem="sT", inc=1)
                tm = S.mark("sT")
                S.wait("vector", "sT", tm)
                S.op("vector", lambda v, b_=bb, i_=i: v.tensor_copy(
                    xtn[:, 2 * i_:2 * i_ + 2], PS.ap()[:, 512 * b_:512 * b_ + 1].bitcast(BF16)), sem="sV", inc=1)
                xe.append(S.mark("sV"))
            xtn_m = S.mark("sV")
            agg_src = aggwh.rearrange("(i p) m -> p i m", p=128)
            agg_used = [0] * 4
            for b4 in range(4):
                sl = b4 % 2
                wsem = f"sW{sl}"
                if b4 >= 2:
                    S.wait("gpsimd", "sT", agg_used[b4 - 2])
                S.op("gpsimd", lambda g, d=agslot[sl].rearrange("p (c m) -> p c m", c=4),
                     s=agg_src[:, b4 * 4:(b4 + 1) * 4]: g.dma_start(out=d, in_=s), sem=wsem, inc=16)
                blkm = (wsem, S.mark(wsem))
                S.waitm("tensor", blkm)
                if b4 == 0:
                    S.wait("tensor", "sV", xtn_m)
                S.waitm("vector", blkm)
                for c in range(4):
                    i = b4 * 4 + c
                    chunk = agslot[sl][:, c * AG_SL:(c + 1) * AG_SL]
                    S.op("tensor", lambda t, i_=i, ch=chunk: t.matmul(
                        bankrow(6, 2, AG_SL), inbt2[:, 2 * i_:2 * i_ + 2], ch,
                        start=(i_ == 0), stop=(i_ == NCH - 1)), sem="sT", inc=1)
                    ebm = S.mark("sT")
                    S.wait("vector", "sT", ebm)
                    S.op("vector", lambda v, i_=i, ch=chunk: v.tensor_scalar(
                        ch, ch, ingt[:, i_:i_ + 1], 0.0, op0=ALU.mult, op1=ALU.add), sem="sV", inc=1)
                    fm = S.mark("sV")
                    S.wait("tensor", "sV", fm)
                    S.op("tensor", lambda t, i_=i, ch=chunk: t.matmul(
                        bankrow(7, 2, AG_SL), xtn[:, 2 * i_:2 * i_ + 2], ch,
                        start=(i_ == 0), stop=(i_ == NCH - 1)), sem="sT", inc=1)
                agg_used[b4] = S.mark("sT")
            agg_pe = S.mark("sT")
            S.wait("vector", "sT", agg_pe)
            S.op("vector", lambda v: v.tensor_copy(cring[0:2, 0:AG_SL], bankrow(7, 2, AG_SL)), sem="sV", inc=1)
            S.wait("vector", "sV", S.mark("sV"))
            S.op("vector", lambda v: v.tensor_tensor(
                cring[0:2, 0:AG_SL], cring[0:2, 0:AG_SL], bankrow(6, 2, AG_SL), op=ALU.add), sem="sV", inc=1)
            S.wait("vector", "sV", S.mark("sV"))
            S.op("vector", lambda v: v.tensor_tensor(
                cring[0:2, 0:AG_SL], cring[0:2, 0:AG_SL], aggb2[:], op=ALU.add), sem="sV", inc=1)
            th0_m = S.mark("sV")
            S.wait("sync", "sV", th0_m)
            S.op("sync", lambda sy: sy.dma_start(out=agg_in, in_=cring[0:2, 0:AG_SL]), sem="sDC", inc=16)
            S.wait("gpsimd", "sDC", S.mark("sDC"))
            S.op("gpsimd", lambda g: g.collective_compute(
                "AllGather", ALU.bypass, replica_groups=[list(range(NCORES))],
                ins=[agg_in], outs=[agg_out]), sem="sCC", inc=1)
            S.wait("sync", "sCC", S.mark("sCC"))
            S.op("sync", lambda sy: sy.dma_start(
                out=thought.rearrange("b (r s) -> b r s", r=NCORES),
                in_=agg_out.rearrange("(r b) s -> b r s", b=2)), sem="sDC", inc=16)
            th_m = ("sDC", S.mark("sDC"))

            d1m_l = [0] * 4
            d2m_l = [0] * 4
            prev = th_m
            for st in range(4):
                S.waitm("vector", prev)
                S.waitm("scalar", prev)
                mln = row_ln(thought, tb8, EPSF, sc=8)
                S.wait("tensor", "sS", mln)
                xe2 = []
                for i in range(NCH):
                    bb = 6 + (i % 2)
                    if i >= 2:
                        S.wait("tensor", "sV", xe2[i - 2])
                    S.op("tensor", lambda t, b_=bb, aa=tb8[:, i * 128:(i + 1) * 128]:
                         t.transpose(PS.ap()[:, 512 * b_:512 * b_ + 1].bitcast(BF16), aa, iden[0:2, 0:2]),
                         sem="sT", inc=1)
                    tm = S.mark("sT")
                    S.wait("vector", "sT", tm)
                    S.op("vector", lambda v, b_=bb, i_=i: v.tensor_copy(
                        xtn[:, 2 * i_:2 * i_ + 2], PS.ap()[:, 512 * b_:512 * b_ + 1].bitcast(BF16)), sem="sV", inc=1)
                    xe2.append(S.mark("sV"))
                ttm = S.mark("sV")
                S.wait("tensor", "sV", ttm)
                if st == 0:
                    S.wait("tensor", "sV", d1s_ready)
                    S.wait("tensor", "sS", ebd1_m)
                for i in range(NCH):
                    for hh in range(2):
                        S.op("tensor", lambda t, i_=i, h_=hh: t.matmul(
                            bankrow(6 + h_, 2, 512), xtn[:, 2 * i_:2 * i_ + 2],
                            trV[:, i_ * 1024 + h_ * 512: i_ * 1024 + (h_ + 1) * 512],
                            start=(i_ == 0), stop=False), sem=None, inc=0)
                for hh in range(2):
                    S.op("tensor", lambda t, h_=hh: t.matmul(
                        bankrow(6 + h_, 2, 512), ones2[:], ebd1[0:1, h_ * 512:(h_ + 1) * 512],
                        start=False, stop=True), sem="sT" if hh == 1 else None, inc=1 if hh == 1 else 0)
                d1m = S.mark("sT")
                d1m_l[st] = d1m
                if st == 3:
                    S.wait("gpsimd", "sT", d1m)
                    for c4 in range(4):
                        S.op("gpsimd", lambda g, d=trV.ap().rearrange("p (c m) -> p c m", c=8)[:, c4 * 2:(c4 + 1) * 2],
                             s=wg_src[:, c4 * 2:(c4 + 1) * 2, :]: g.dma_start(out=d, in_=s), sem="sGV", inc=16)
                    gv_m = S.mark("sGV")
                S.wait("scalar", "sT", d1m)
                S.op("scalar", lambda s: s.activation(tb8[:, 0:512], bankrow(6, 2, 512), AF.Gelu), sem="sS", inc=1)
                S.op("scalar", lambda s: s.activation(tb8[:, 512:1024], bankrow(7, 2, 512), AF.Gelu), sem="sS", inc=1)
                gm = S.mark("sS")
                S.wait("tensor", "sS", gm)
                xe3 = []
                for ii in range(8):
                    bb = 6 + (ii % 2)
                    if ii >= 2:
                        S.wait("tensor", "sV", xe3[ii - 2])
                    S.op("tensor", lambda t, b_=bb, i_=ii:
                         t.transpose(PS.ap()[:, 512 * b_:512 * b_ + 1].bitcast(BF16),
                                     tb8[:, i_ * 128:(i_ + 1) * 128], iden[0:2, 0:2]), sem="sT", inc=1)
                    tm = S.mark("sT")
                    S.wait("vector", "sT", tm)
                    S.op("vector", lambda v, b_=bb, i_=ii: v.tensor_copy(
                        xtn[:, 32 + 2 * i_:32 + 2 * i_ + 2], PS.ap()[:, 512 * b_:512 * b_ + 1].bitcast(BF16)),
                        sem="sV", inc=1)
                    xe3.append(S.mark("sV"))
                tgm = S.mark("sV")
                S.wait("tensor", "sV", tgm)
                if st == 0:
                    S.wait("tensor", "sD2", d2s_in)
                for c in range(8):
                    for q in range(4):
                        S.op("tensor", lambda t, c_=c, q_=q: t.matmul(
                            bankrow(q_, 2, 512), xtn[:, 32 + 2 * c_:32 + 2 * c_ + 2],
                            trQ[:, c_ * 2048 + q_ * 512: c_ * 2048 + (q_ + 1) * 512],
                            start=(c_ == 0), stop=False), sem=None, inc=0)
                for q in range(4):
                    S.op("tensor", lambda t, q_=q: t.matmul(
                        bankrow(q_, 2, 512), ones2[:], d2brow[0:1, q_ * 512:(q_ + 1) * 512],
                        start=False, stop=True), sem="sT" if q == 3 else None, inc=1 if q == 3 else 0)
                d2m = S.mark("sT")
                d2m_l[st] = d2m
                S.wait("vector", "sT", d2m)
                S.op("vector", lambda v: v.tensor_copy(urow[:], PS[0:2, 0:2048]), sem="sV", inc=1)
                um = S.mark("sV")
                if st == 3:
                    um3 = um
                S.wait("sync", "sV", um)
                S.op("sync", lambda sy, s_=st: sy.dma_start(out=ar_inh[s_], in_=urow[:]), sem="sDC", inc=16)
                S.wait("gpsimd", "sDC", S.mark("sDC"))
                S.op("gpsimd", lambda g, s_=st: g.collective_compute(
                    "AllReduce", ALU.add, replica_groups=[list(range(NCORES))],
                    ins=[ar_inh[s_]], outs=[ar_outh[s_]]), sem="sCC", inc=1)
                S.wait("sync", "sCC", S.mark("sCC"))
                S.op("sync", lambda sy, s_=st: sy.dma_start(out=urow[:], in_=ar_outh[s_]), sem="sDC", inc=16)
                arm = S.mark("sDC")
                S.wait("vector", "sDC", arm)
                S.op("vector", lambda v: v.tensor_tensor(thought, thought, urow[:], op=ALU.add), sem="sV", inc=1)
                prev = ("sV", S.mark("sV"))
                if st == 3:
                    S.wait("gpsimd", "sT", d2m)
                    S.wait("gpsimd", "sV", prev[1])
                    S.op("gpsimd", lambda g: g.dma_start(out=gw_dst(15), in_=wg_src[:, 15, :]), sem="sGQ", inc=16)
                    S.op("gpsimd", lambda g: g.dma_start(out=gbrow, in_=gatebh), sem="sGQ", inc=16)
                    gq_m = S.mark("sGQ")

            S.wait("vector", prev[0], prev[1])
            S.op("vector", lambda v: v.tensor_copy(tb8[:, 0:D], thought), sem="sV", inc=1)
            tbm = S.mark("sV")
            S.wait("tensor", "sV", tbm)
            xe4 = []
            for i in range(NCH):
                bb = 6 + (i % 2)
                if i >= 2:
                    S.wait("tensor", "sV", xe4[i - 2])
                S.op("tensor", lambda t, b_=bb, i_=i: t.transpose(
                    PS.ap()[:, 512 * b_:512 * b_ + 1].bitcast(BF16),
                    tb8[:, i_ * 128:(i_ + 1) * 128], iden[0:2, 0:2]), sem="sT", inc=1)
                tm = S.mark("sT")
                S.wait("vector", "sT", tm)
                S.op("vector", lambda v, b_=bb, i_=i: v.tensor_copy(
                    xtn[:, 2 * i_:2 * i_ + 2], PS.ap()[:, 512 * b_:512 * b_ + 1].bitcast(BF16)), sem="sV", inc=1)
                xe4.append(S.mark("sV"))
            btm = S.mark("sV")

            sig_l = [0] * NT
            sigq2_l = [0] * NT
            gpe_l = [0] * NT
            x1m_l = [None] * NT
            ln2_l = [0] * NT
            x2_l = [0] * NT
            out_m = [None] * NT
            gbanks = [[((n * 4) % 6 + q) % 6 for q in range(4)] for n in range(NT)]

            def gw_chunkap(c, q):
                return gw_dst(c)[:, q * 512:(q + 1) * 512]

            def gate_pe_half(n, half):
                tt = TORD[n]
                banks = gbanks[n]
                q0 = 2 * half
                if half == 0:
                    if n == 0:
                        S.wait("tensor", "sGV", gv_m)
                        S.wait("tensor", "sGW", gw_early)
                        S.wait("tensor", "sV", max(ftr_done))
                        S.wait("tensor", "sV", um3)
                    if n >= 2:
                        S.wait("tensor", "sS", sig_l[n - 2])
                else:
                    if n >= 1:
                        S.wait("tensor", "sS", sigq2_l[n - 1])
                for i in range(NCH - 1):
                    for dq in range(2):
                        S.op("tensor", lambda t, l=trA[:, tt * 2048 + i * 128: tt * 2048 + (i + 1) * 128],
                             r=gw_chunkap(i, q0 + dq), bb=banks[q0 + dq], st=(i == 0):
                             t.matmul(bank(bb), l, r, start=st, stop=False), sem=None, inc=0)
                if n == 0 and half == 0:
                    S.wait("tensor", "sGQ", gq_m)
                for dq in range(2):
                    S.op("tensor", lambda t, bb=banks[q0 + dq], q_=q0 + dq: t.matmul(
                        bank(bb), ones128[:], gbrow[0:1, q_ * 512:(q_ + 1) * 512],
                        start=False, stop=False), sem=None, inc=0)
                for dq in range(2):
                    S.op("tensor", lambda t, l=trA[:, tt * 2048 + 15 * 128: tt * 2048 + 16 * 128],
                         r=gw_chunkap(15, q0 + dq), bb=banks[q0 + dq]:
                         t.matmul(bank(bb), l, r, start=False, stop=True),
                         sem="sT" if dq == 1 else None, inc=1 if dq == 1 else 0)
                if half == 1:
                    gpe_l[n] = S.mark("sT")

            def gate_sig(n):
                tt = TORD[n]
                banks = gbanks[n]
                S.wait("scalar", "sT", gpe_l[n])
                for q in range(4):
                    S.op("scalar", lambda s, bb=banks[q], q_=q: s.activation(
                        trA[:, tt * 2048 + q_ * 512: tt * 2048 + (q_ + 1) * 512],
                        bank(bb), AF.Sigmoid), sem="sS", inc=1)
                    if q == 1:
                        sigq2_l[n] = S.mark("sS")
                sig_l[n] = S.mark("sS")

            bc_src = bcwh.rearrange("(i p) m -> p i m", p=128)
            bc_used = [0] * 4
            for b4 in range(4):
                sl = b4 % 2
                wsem = f"sW{sl}"
                if b4 >= 2:
                    S.wait("gpsimd", "sT", bc_used[b4 - 2])
                S.op("gpsimd", lambda g, d=agslot[sl].rearrange("p (c m) -> p c m", c=4),
                     s=bc_src[:, b4 * 4:(b4 + 1) * 4]: g.dma_start(out=d, in_=s), sem=wsem, inc=16)
                blkm = (wsem, S.mark(wsem))
                S.waitm("tensor", blkm)
                if b4 == 0:
                    S.wait("tensor", "sV", btm)
                for c in range(4):
                    i = b4 * 4 + c
                    chunk = agslot[sl][:, c * AG_SL:(c + 1) * AG_SL]
                    S.op("tensor", lambda t, i_=i, ch=chunk: t.matmul(
                        bankrow(6, 2, AG_SL), xtn[:, 2 * i_:2 * i_ + 2], ch,
                        start=(i_ == 0), stop=(i_ == NCH - 1)), sem="sT", inc=1)
                bc_used[b4] = S.mark("sT")
                gate_pe_half(b4 // 2, b4 % 2)
                if b4 % 2 == 1:
                    gate_sig(b4 // 2)
            bc_pe = bc_used[3]
            S.wait("vector", "sT", bc_pe)
            S.op("vector", lambda v: v.tensor_copy(cring[0:2, 0:AG_SL], bankrow(6, 2, AG_SL)), sem="sV", inc=1)
            S.wait("vector", "sV", S.mark("sV"))
            S.op("vector", lambda v: v.tensor_tensor(
                cring[0:2, 0:AG_SL], cring[0:2, 0:AG_SL], bcb2[:], op=ALU.add), sem="sV", inc=1)
            bcm = S.mark("sV")
            S.wait("sync", "sV", bcm)
            S.op("sync", lambda sy: sy.dma_start(out=bc_inh, in_=cring[0:2, 0:AG_SL]), sem="sDC", inc=16)
            S.wait("gpsimd", "sDC", S.mark("sDC"))
            S.op("gpsimd", lambda g: g.collective_compute(
                "AllGather", ALU.bypass, replica_groups=[list(range(NCORES))],
                ins=[bc_inh], outs=[bc_outh]), sem="sCC", inc=1)
            S.wait("sync", "sCC", S.mark("sCC"))
            S.op("sync", lambda sy: sy.dma_start(
                out=cring.ap()[0:2, 0:D].rearrange("b (r s) -> b r s", r=NCORES),
                in_=bc_outh.rearrange("(r b) s -> b r s", b=2)), sem="sDC", inc=16)
            bcrow_m = S.mark("sDC")
            if debug >= 2:
                S.wait("sync", "sDC", bcrow_m)
                S.op("sync", lambda sy: sy.dma_start(out=dbgbc, in_=cring[0:2, 0:D]), sem="sDC", inc=16)
                bcrow_m = S.mark("sDC")
            def emit_sel2_bcbc():
                S.wait("tensor", "sDC", bcrow_m)
                bm = 0
                for p in range(2):
                    if p == 1:
                        S.wait("tensor", "sV", bm)
                    for h in range(2):
                        q = 2 * p + h
                        S.op("tensor", lambda t, b_=6 + h, q_=q: t.matmul(
                            bank(b_), sel2[:], cring[0:2, q_ * 512:(q_ + 1) * 512],
                            start=True, stop=True), sem="sT" if h == 1 else None, inc=1 if h == 1 else 0)
                    pm = S.mark("sT")
                    S.wait("vector", "sT", pm)
                    for h in range(2):
                        q = 2 * p + h
                        S.op("vector", lambda v, b_=6 + h, q_=q: v.tensor_copy(
                            bcbc[:, q_ * 512:(q_ + 1) * 512], bank(b_)), sem="sV", inc=1)
                    bm = S.mark("sV")
                return bm

            S.op("sync", lambda sy: sy.dma_start(out=gamv, in_=gambh), sem="sGB", inc=16)
            S.op("sync", lambda sy: sy.dma_start(out=betv, in_=betbh), sem="sGB", inc=16)
            gb_m = S.mark("sGB")
            S.op("sync", lambda sy: sy.dma_start(
                out=xring[:, 0:D], in_=x1s_tiles[TORD[0]]), sem="sXA", inc=16)
            x1m_l[0] = ("sXA", S.mark("sXA"))
            S.op("sync", lambda sy: sy.dma_start(
                out=xring[:, D:2 * D], in_=x1s_tiles[TORD[1]]), sem="sXB", inc=16)
            x1m_l[1] = ("sXB", S.mark("sXB"))

            def combine_stages(k, bufs=None):
                tt_ = TORD[k]
                sl2 = k % 2
                if bufs is None:
                    xsem_ = "sXA" if sl2 == 0 else "sXB"
                    xt = xring[:, D * sl2: D * (sl2 + 1)]
                    crX = cring[:, 0:D] if sl2 == 0 else cring[:, D:2 * D]
                    scrX = scr2 if sl2 == 0 else scr2b
                    st0 = 0 if sl2 == 0 else 8
                else:
                    crX, scrX, st0, xt, xsem_ = bufs
                gp = trA[:, tt_ * 2048:(tt_ + 1) * 2048]
                mk = {}

                def s_gbc():
                    S.wait("gpsimd", "sS", sig_l[k])
                    if k == 0:
                        S.wait("gpsimd", "sV", bcbc_m)
                    if bufs is None and k >= 2:
                        S.wait("gpsimd", "sV", x2_l[k - 2])
                    S.op("gpsimd", lambda g, g_=gp, c_=crX: g.tensor_tensor(
                        c_, g_, bcbc[:, 0:D], op=ALU.mult), sem="sP", inc=1)
                    mk["gbc"] = S.mark("sP")

                def s_t1():
                    S.waitm("scalar", x1m_l[k])
                    if bufs is not None:
                        pass
                    elif k < 2:
                        S.wait("scalar", "sT", d2m_l[3])
                    else:
                        S.wait("scalar", "sS", ln2_l[k - 2])
                    S.op("scalar", lambda s, x=xt, t_=tt_, d_=scrX: s.activation(
                        d_, x, AF.Identity, bias=stats[:, 40 + t_:41 + t_],
                        scale=stats[:, 32 + t_:33 + t_]), sem="sS", inc=1)
                    mk["t1"] = S.mark("sS")

                def s_outt():
                    S.wait("vector", "sS", mk["t1"])
                    S.wait("vector", "sP", mk["gbc"])
                    S.op("vector", lambda v, c_=crX, d_=scrX: v.tensor_tensor(
                        c_, c_, d_, op=ALU.add), sem="sV", inc=1)
                    mk["outt"] = S.mark("sV")

                def s_stats1():
                    S.wait("vector", "sV", mk["outt"])
                    S.op("vector", lambda v, c_=crX: v.reduce_sum(
                        stats[:, st0:st0 + 1], c_, axis=AX.X), sem="sV", inc=1)
                    mk["red1"] = S.mark("sV")
                    S.wait("scalar", "sV", mk["outt"])
                    S.op("scalar", lambda s, c_=crX, d_=scrX: s.activation(
                        d_, c_, AF.Square, accum_out=stats[:, st0 + 1:st0 + 2]), sem="sS", inc=1)
                    mk["sq1"] = S.mark("sS")

                def s_ln1():
                    S.wait("vector", "sS", mk["sq1"])
                    S.wait("vector", "sV", mk["red1"])
                    mk["nb2"] = ln_stats(EPSF, st0, st0 + 1, st0 + 2, st0 + 3, st0 + 4)

                def s_yh():
                    S.wait("scalar", "sV", mk["nb2"])
                    S.op("scalar", lambda s, c_=crX, d_=scrX: s.activation(
                        d_, c_, AF.Identity, bias=stats[:, st0 + 4:st0 + 5],
                        scale=stats[:, st0 + 3:st0 + 4]), sem="sS", inc=1)
                    mk["yh"] = S.mark("sS")

                def s_gmul():
                    S.wait("vector", "sS", mk["yh"])
                    if k == 0:
                        S.wait("vector", "sGB", gb_m)
                    S.op("vector", lambda v, d_=scrX: v.tensor_tensor(
                        d_, d_, gamv, op=ALU.mult), sem="sV", inc=1)
                    mk["gmul"] = S.mark("sV")

                def s_beta():
                    S.wait("gpsimd", "sV", mk["gmul"])
                    if k == 0:
                        S.wait("gpsimd", "sGB", gb_m)
                    S.op("gpsimd", lambda g, d_=scrX: g.tensor_tensor(
                        d_, d_, betv, op=ALU.add), sem="sP", inc=1)
                    mk["y"] = S.mark("sP")

                def s_stats2():
                    if k < 4:
                        S.wait("vector", "sP", mk["y"])
                        S.op("vector", lambda v, d_=scrX: v.reduce_sum(
                            stats[:, st0:st0 + 1], d_, axis=AX.X), sem="sV", inc=1)
                        mk["red2"] = ("sV", S.mark("sV"))
                    S.wait("scalar", "sP", mk["y"])
                    S.op("scalar", lambda s, c_=crX, d_=scrX: s.activation(
                        c_, d_, AF.Square, accum_out=stats[:, st0 + 1:st0 + 2]), sem="sS", inc=1)
                    if k >= 4:
                        S.op("scalar", lambda s, c_=crX, d_=scrX: s.activation(
                            c_, d_, AF.Identity, accum_out=stats[:, st0:st0 + 1]), sem="sS", inc=1)
                        mk["red2"] = ("sS", S.mark("sS"))
                    mk["sq2"] = S.mark("sS")

                def s_ln2():
                    S.wait("vector", "sS", mk["sq2"])
                    S.waitm("vector", mk["red2"])
                    mk["nb3"] = ln_stats(EPS0, st0, st0 + 1, st0 + 2, st0 + 3, st0 + 4)

                def s_t3():
                    S.wait("scalar", "sV", mk["nb3"])
                    S.op("scalar", lambda s, c_=crX, d_=scrX: s.activation(
                        c_, d_, AF.Identity, bias=stats[:, st0 + 4:st0 + 5],
                        scale=stats[:, st0 + 3:st0 + 4]), sem="sS", inc=1)
                    ln2_l[k] = S.mark("sS")

                def s_x2dma():
                    S.wait("vector", "sS", ln2_l[k])
                    S.op("vector", lambda v, x=xt, c_=crX: v.tensor_tensor(x, x, c_, op=ALU.add), sem="sV", inc=1)
                    x2_l[k] = S.mark("sV")
                    S.wait("sync", "sV", x2_l[k])
                    S.op("sync", lambda sy, x=xt, t_=tt_: sy.dma_start(
                        out=xout_tiles[t_], in_=x), sem=xsem_, inc=16)
                    out_m[k] = (xsem_, S.mark(xsem_))
                    if k + 2 < 6:
                        S.waitm("sync", out_m[k])
                        S.op("sync", lambda sy, sl=sl2, t_=TORD[k + 2]: sy.dma_start(
                            out=xring[:, D * sl: D * (sl + 1)], in_=x1s_tiles[t_]), sem=xsem_, inc=16)
                        x1m_l[k + 2] = (xsem_, S.mark(xsem_))

                return [s_gbc, s_t1, s_outt, s_stats1, s_ln1, s_yh, s_gmul,
                        s_beta, s_stats2, s_ln2, s_t3, s_x2dma]

            def emit_combine_pair(ka, kb):
                for a, b in zip(combine_stages(ka), combine_stages(kb)):
                    a()
                    b()

            def emit_combine_quad():
                crC = trQ.ap()[:, 8192:12288].bitcast(F32)
                scrC = wring.ap()[:, 0:4096].bitcast(F32)
                crD = wring.ap()[:, 4096:8192].bitcast(F32)
                scrD = trV.ap()[:, 0:4096].bitcast(F32)
                xC = trV.ap()[:, 4096:8192].bitcast(F32)
                xD = trV.ap()[:, 8192:12288].bitcast(F32)
                S.wait("sync", "sT", gpe_l[7])
                S.op("sync", lambda sy: sy.dma_start(
                    out=xC, in_=x1s_tiles[TORD[6]]), sem="sDC", inc=16)
                x1m_l[6] = ("sDC", S.mark("sDC"))
                S.op("sync", lambda sy: sy.dma_start(
                    out=xD, in_=x1s_tiles[TORD[7]]), sem="sXC", inc=16)
                x1m_l[7] = ("sXC", S.mark("sXC"))
                stl = [combine_stages(4), combine_stages(5),
                       combine_stages(6, bufs=(crC, scrC, 16, xC, "sXA")),
                       combine_stages(7, bufs=(crD, scrD, 24, xD, "sXB"))]
                for stage in range(len(stl[0])):
                    for t4 in range(4):
                        stl[t4][stage]()

            gate_pe_half(2, 0)
            gate_pe_half(2, 1)
            gate_sig(2)
            bcbc_m = emit_sel2_bcbc()
            gate_pe_half(3, 0)
            gate_pe_half(3, 1)
            gate_sig(3)
            emit_combine_pair(0, 1)
            for n in (4, 5):
                gate_pe_half(n, 0)
                gate_pe_half(n, 1)
                gate_sig(n)
            emit_combine_pair(2, 3)
            for n in (6, 7):
                gate_pe_half(n, 0)
                gate_pe_half(n, 1)
                gate_sig(n)
            emit_combine_quad()
            S.wait("sync", "sXA", S.mark("sXA"))
            S.wait("sync", "sXB", S.mark("sXB"))
            S.wait("gpsimd", "sXA", S.mark("sXA"))
            S.wait("gpsimd", "sXB", S.mark("sXB"))

        @block.tensor
        def _(t):
            S.emit("tensor", t, sems)

        @block.vector
        def _(v):
            S.emit("vector", v, sems)

        @block.scalar
        def _(s):
            S.emit("scalar", s, sems)

        @block.gpsimd
        def _(g):
            S.emit("gpsimd", g, sems)

        @block.sync
        def _(sy):
            S.emit("sync", sy, sems)

    return nc


_BF = ml_dtypes.bfloat16


def _prep_inputs(inp):
    x = np.asarray(inp["L0_attn_x"], np.float32).reshape(2 * 4096, D)
    cores = []
    iden = np.eye(128, dtype=_BF)
    maskc = np.kron(np.eye(8, dtype=np.float32), np.ones((16, 16), np.float32))
    ones2 = np.ones((1, 2), dtype=_BF)
    ones128 = np.ones((1, 128), dtype=_BF)
    tr = lambda v: np.ascontiguousarray(np.asarray(v, np.float32).reshape(NCH, 128).T)
    dup2 = lambda m: np.ascontiguousarray(np.repeat(m, 2, axis=1))
    pngt = tr(inp["L0_ffn_pn_gamma"])
    pnbt2 = dup2(tr(inp["L0_ffn_pn_beta"]))
    ingt = tr(inp["L0_ffn_in_gamma"])
    inbt2 = dup2(tr(inp["L0_ffn_in_beta"]))
    gamb = np.ascontiguousarray(np.broadcast_to(np.asarray(inp["L0_ffn_out_gamma"], np.float32), (128, D)))
    betb = np.ascontiguousarray(np.broadcast_to(np.asarray(inp["L0_ffn_out_beta"], np.float32), (128, D)))
    d2b_full = np.asarray(inp["L0_ffn_d2_b"], np.float32).reshape(1, D)
    zeros_d2b = np.zeros_like(d2b_full)
    for c in range(NCORES):
        b = c // 4
        sel2 = np.zeros((2, 128), np.float32)
        sel2[b, :] = 1.0
        m = {
            "x0": np.ascontiguousarray(x[c * T:(c + 1) * T]),
            "wq": np.asarray(inp["L0_attn_Wq"], np.float32),
            "wk": np.asarray(inp["L0_attn_Wk"], np.float32),
            "wv": np.asarray(inp["L0_attn_Wv"], np.float32),
            "wo": np.asarray(inp["L0_attn_Wo"], np.float32),
            "wg": np.asarray(inp["L0_ffn_gate_w"], np.float32),
            "d1w": np.ascontiguousarray(np.asarray(inp["L0_ffn_d1_w"], np.float32)[:, c * TH_SL:(c + 1) * TH_SL]),
            "d2w": np.ascontiguousarray(np.asarray(inp["L0_ffn_d2_w"], np.float32)[c * TH_SL:(c + 1) * TH_SL, :]),
            "aggw": np.ascontiguousarray(np.asarray(inp["L0_ffn_agg_w"], np.float32)[:, c * AG_SL:(c + 1) * AG_SL]),
            "bcw": np.ascontiguousarray(np.asarray(inp["L0_ffn_bc_w"], np.float32)[:, c * AG_SL:(c + 1) * AG_SL]),
            "d1b": np.ascontiguousarray(np.asarray(inp["L0_ffn_d1_b"], np.float32)[None, c * TH_SL:(c + 1) * TH_SL]),
            "d2b": d2b_full if c == 0 else zeros_d2b,
            "aggb": np.ascontiguousarray(np.broadcast_to(
                np.asarray(inp["L0_ffn_agg_b"], np.float32)[None, c * AG_SL:(c + 1) * AG_SL], (2, AG_SL))),
            "bcb2": np.ascontiguousarray(np.broadcast_to(
                np.asarray(inp["L0_ffn_bc_b"], np.float32)[None, c * AG_SL:(c + 1) * AG_SL], (2, AG_SL))),
            "gateb": np.asarray(inp["L0_ffn_gate_b"], np.float32).reshape(1, D),
            "gamb": gamb, "betb": betb,
            "pngt": pngt, "pnbt2": pnbt2, "ingt": ingt, "inbt2": inbt2,
            "iden": iden, "maskc": maskc, "ones2": ones2, "ones128": ones128,
            "sel2": sel2,
        }
        cores.append(m)
    return cores


def kernel(**inputs):
    in_maps = _prep_inputs(inputs)
    nc = build(stage=STAGE, debug=DEBUG)
    res = run_bass_kernel_spmd(nc, in_maps, core_ids=list(range(NCORES)))
    out = np.concatenate([np.asarray(r["xout"], np.float32) for r in res.results], axis=0)
    return out.reshape(2, 4096, D)


# version 20
# speedup vs baseline: 1.0999x; 1.0145x over previous
import numpy as np
import ml_dtypes
import concourse.bass as bass
import concourse.mybir as mybir
from concourse.bass_utils import run_bass_kernel_spmd
from contextlib import ExitStack

F32 = mybir.dt.float32
BF16 = mybir.dt.bfloat16
AF = mybir.ActivationFunctionType
ALU = mybir.AluOpType
AX = mybir.AxisListType

NCORES = 8
T = 1024
NT = 8
D = 2048
NCH = 16
EPS0 = 1e-5
EPSF = 1e-6
TH_SL = 8192 // NCORES
AG_SL = D // NCORES
TORD = [7, 0, 1, 2, 3, 4, 5, 6]

STAGE = 3
DEBUG = 0


class Sched:
    ENGS = ("tensor", "vector", "scalar", "gpsimd", "sync")

    def __init__(self):
        self.items = {e: [] for e in self.ENGS}
        self.cnt = {}

    def op(self, eng, fn, sem=None, inc=0):
        self.items[eng].append(("op", fn, sem, inc))
        if sem is not None and inc:
            self.cnt[sem] = self.cnt.get(sem, 0) + inc
        return self.cnt.get(sem, 0) if sem else 0

    def wait(self, eng, sem, val):
        assert val <= self.cnt.get(sem, 0), f"wait {sem}>={val} unreachable ({self.cnt.get(sem, 0)})"
        self.items[eng].append(("wait", sem, val))

    def waitm(self, eng, mark):
        self.wait(eng, mark[0], mark[1])

    def mark(self, sem):
        return self.cnt.get(sem, 0)

    def emit(self, eng_name, eng, sems):
        for it in self.items[eng_name]:
            if it[0] == "wait":
                eng.wait_ge(sems[it[1]], it[2])
            else:
                inst = it[1](eng)
                if it[2] is not None and it[3]:
                    inst.then_inc(sems[it[2]], it[3])


def build(stage=3, debug=0):
    nc = bass.Bass("TRN2", target_bir_lowering=False)

    I = lambda n, s, d=F32: nc.dram_tensor(n, s, d, kind="ExternalInput").ap()
    x0h = I("x0", [T, D])
    wqh, wkh, wvh = I("wq", [D, D]), I("wk", [D, D]), I("wv", [D, D])
    woh, wgh = I("wo", [D, D]), I("wg", [D, D])
    d1wh, d2wh = I("d1w", [D, TH_SL]), I("d2w", [TH_SL, D])
    aggwh, bcwh = I("aggw", [D, AG_SL]), I("bcw", [D, AG_SL])
    d1bh = I("d1b", [1, TH_SL])
    d2bh = I("d2b", [1, D])
    aggbh, bcbh = I("aggb", [2, AG_SL]), I("bcb2", [2, AG_SL])
    gatebh = I("gateb", [1, D])
    gambh, betbh = I("gamb", [128, D]), I("betb", [128, D])
    pngth, pnbt2h = I("pngt", [128, NCH]), I("pnbt2", [128, 2 * NCH])
    ingth, inbt2h = I("ingt", [128, NCH]), I("inbt2", [128, 2 * NCH])
    idenh = I("iden", [128, 128], BF16)
    maskh = I("maskc", [128, 128])
    ones2h = I("ones2", [1, 2], BF16)
    ones128h = I("ones128", [1, 128], BF16)
    sel2h = I("sel2", [2, 128])

    xouth = nc.dram_tensor("xout", [T, D], F32, kind="ExternalOutput").ap()
    dbgbc = nc.dram_tensor("dbg_bc", [2, D], F32, kind="ExternalOutput").ap() if debug >= 2 else None

    x1scr = nc.dram_tensor("x1scr", [T, D], F32).ap()
    ag1_in = nc.dram_tensor("ag1_in", [1, D], F32).ap()
    ag1_out = nc.dram_tensor("ag1_out", [NCORES, D], F32, addr_space="Shared").ap()
    agg_in = nc.dram_tensor("agg_in", [2, AG_SL], F32).ap()
    agg_out = nc.dram_tensor("agg_out", [2 * NCORES, AG_SL], F32, addr_space="Shared").ap()
    ar_inh = [nc.dram_tensor(f"ar_in{s}", [2, D], F32).ap() for s in range(4)]
    ar_outh = [nc.dram_tensor(f"ar_out{s}", [2, D], F32, addr_space="Shared").ap() for s in range(4)]
    bc_inh = nc.dram_tensor("bc_in", [2, AG_SL], F32).ap()
    bc_outh = nc.dram_tensor("bc_out", [2 * NCORES, AG_SL], F32, addr_space="Shared").ap()

    with ExitStack() as stk:
        e = stk.enter_context
        trA = e(nc.sbuf_tensor("trA", [128, 16384], BF16))
        trQ = e(nc.sbuf_tensor("trQ", [128, 16384], BF16))
        trK = e(nc.sbuf_tensor("trK", [128, 16384], BF16))
        trV = e(nc.sbuf_tensor("trV", [128, 16384], BF16))
        xring = e(nc.sbuf_tensor("xring", [128, 2 * D], F32))
        cring = e(nc.sbuf_tensor("cring", [128, 2 * D], F32))
        wring = e(nc.sbuf_tensor("wring", [128, 8192], BF16))
        chsc = e(nc.sbuf_tensor("chsc", [128, 4096], BF16))
        afbf = e(nc.sbuf_tensor("afbf", [128, 2 * D], BF16))
        iden = e(nc.sbuf_tensor("idn", [128, 128], BF16))
        maskc = e(nc.sbuf_tensor("msk", [128, 128], F32))
        ones2 = e(nc.sbuf_tensor("o2", [1, 2], BF16))
        ones128 = e(nc.sbuf_tensor("o128", [1, 128], BF16))
        sel2 = e(nc.sbuf_tensor("sel2s", [2, 128], F32))
        pngt = e(nc.sbuf_tensor("pngts", [128, NCH], F32))
        pnbt2 = e(nc.sbuf_tensor("pnbt2s", [128, 2 * NCH], BF16))
        ingt = e(nc.sbuf_tensor("ingts", [128, NCH], F32))
        inbt2 = e(nc.sbuf_tensor("inbt2s", [128, 2 * NCH], BF16))
        d1bbf = e(nc.sbuf_tensor("d1bbf", [1, TH_SL], BF16))
        d2brow = e(nc.sbuf_tensor("d2brow", [1, D], BF16))
        aggb2 = e(nc.sbuf_tensor("aggb2s", [2, AG_SL], F32))
        bcb2 = e(nc.sbuf_tensor("bcb2s", [2, AG_SL], F32))
        ebd1 = e(nc.sbuf_tensor("ebd1", [1, TH_SL], BF16))
        xtn = e(nc.sbuf_tensor("xtn", [128, 48], BF16))
        stats = e(nc.sbuf_tensor("stats", [128, 64], F32))
        srow = e(nc.sbuf_tensor("srow", [2, 16], F32))
        wjit2 = e(nc.sbuf_tensor("wjit2", [128, 4 * AG_SL], BF16))
        sqacc = e(nc.sbuf_tensor("sqacc", [128, 8], F32))
        PS = e(nc.psum_tensor("PS", [128, 4096], F32))

        semnames = ["sT", "sV", "sS", "sP", "sCC", "sSM", "sXA", "sXB", "sW0", "sW1", "sW2", "sW3", "sWR", "sWR2", "sD1", "sDC", "sD2", "sGW", "sGB", "sGV", "sGQ", "sXC"]
        sems = {n: e(nc.semaphore(n)) for n in semnames}
        block = e(nc.Block())
        S = Sched()

        def bank(b, off=0, n=512):
            return PS[:, 512 * b + off: 512 * b + off + n]

        def bankbf(b, n=128):
            return PS.ap()[:, 512 * b: 512 * b + (n + 1) // 2].bitcast(BF16)

        def bankrow(b, rows=2, n=512):
            return PS[0:rows, 512 * b: 512 * b + n]

        thought = trK.ap()[:, 10240:14336].bitcast(F32)[0:2, :]
        gamv = trK.ap()[:, 0:4096].bitcast(F32)
        betv = trK.ap()[:, 4096:8192].bitcast(F32)
        tb8 = chsc.ap()[0:2, 0:2048]
        urow = cring.ap()[0:2, D:2 * D]
        trow = chsc.ap()[:, 2048:4096].bitcast(F32)[0:2, :]
        xnbf = chsc.ap()[0:2, 0:2048]
        bcbc = chsc.ap().bitcast(F32)
        scr2 = trQ.ap()[:, 0:4096].bitcast(F32)
        scr2b = trQ.ap()[:, 4096:8192].bitcast(F32)
        gbrow = trK.ap()[0:1, 8192:10240]
        agslot = [wjit2.ap()[:, 0:4 * AG_SL], chsc.ap()[:, 2048:2048 + 4 * AG_SL]]

        x0_tiles = x0h.rearrange("(n p) d -> n p d", p=128)
        xout_tiles = xouth.rearrange("(n p) d -> n p d", p=128)
        x1s_tiles = x1scr.rearrange("(n p) d -> n p d", p=128)

        for dst, src in [(iden[:], idenh), (maskc[:], maskh), (ones2[:], ones2h),
                         (ones128[:], ones128h), (sel2[:], sel2h), (pngt[:], pngth),
                         (pnbt2[:], pnbt2h), (ingt[:], ingth), (inbt2[:], inbt2h),
                         (d1bbf[:], d1bh), (d2brow[:], d2bh), (aggb2[:], aggbh),
                         (bcb2[:], bcbh)]:
            S.op("gpsimd", lambda g, d=dst, s=src: g.dma_start(out=d, in_=s), sem="sSM", inc=16)
        sm_done = ("sSM", S.mark("sSM"))

        def ln_stats(eps, su, sq, m2, rs, nb, rows=stats):
            S.op("vector", lambda v, a=su: v.tensor_scalar(
                rows[:, a:a + 1], rows[:, a:a + 1], 1.0 / D, 0.0, op0=ALU.mult, op1=ALU.add), sem="sV", inc=1)
            S.op("vector", lambda v, a=sq, ee=eps: v.tensor_scalar(
                rows[:, a:a + 1], rows[:, a:a + 1], 1.0 / D, ee, op0=ALU.mult, op1=ALU.add), sem="sV", inc=1)
            S.wait("vector", "sV", S.mark("sV"))
            S.op("vector", lambda v, a=su, b=m2: v.tensor_tensor(
                rows[:, b:b + 1], rows[:, a:a + 1], rows[:, a:a + 1], op=ALU.mult), sem="sV", inc=1)
            S.wait("vector", "sV", S.mark("sV"))
            S.op("vector", lambda v, a=sq, b=m2: v.tensor_tensor(
                rows[:, a:a + 1], rows[:, a:a + 1], rows[:, b:b + 1], op=ALU.subtract), sem="sV", inc=1)
            S.wait("scalar", "sV", S.mark("sV"))
            S.op("scalar", lambda s, a=sq, b=rs: s.activation(
                rows[:, b:b + 1], rows[:, a:a + 1], AF.Ln), sem="sS", inc=1)
            S.wait("scalar", "sS", S.mark("sS"))
            S.op("scalar", lambda s, b=rs: s.activation(
                rows[:, b:b + 1], rows[:, b:b + 1], AF.Exp, scale=-0.5), sem="sS", inc=1)
            S.wait("vector", "sS", S.mark("sS"))
            S.op("vector", lambda v, a=su, b=rs, c=nb: v.tensor_scalar(
                rows[:, c:c + 1], rows[:, a:a + 1], rows[:, b:b + 1], -1.0,
                op0=ALU.mult, op1=ALU.mult), sem="sV", inc=1)
            return S.mark("sV")


        def finish_passthrough(wait_list):
            for sem_, val_ in wait_list:
                S.wait("sync", sem_, val_)
            for n in range(NT):
                S.wait("sync", "sDC", S.mark("sDC"))
                S.op("sync", lambda sy, t_=n: sy.dma_start(
                    out=xring[:, 0:D], in_=x0_tiles[t_]), sem="sDC", inc=16)
                S.wait("sync", "sDC", S.mark("sDC"))
                S.op("sync", lambda sy, t_=n: sy.dma_start(
                    out=xout_tiles[t_], in_=xring[:, 0:D]), sem="sDC", inc=16)
            S.wait("sync", "sDC", S.mark("sDC"))
            S.wait("gpsimd", "sDC", S.mark("sDC"))

        evac_ph0 = []
        cast_ph0 = []
        trdone_ph0 = []
        for idx in range(NT):
            tt = idx
            slot = idx % 2
            xsem = "sXA" if slot == 0 else "sXB"
            if idx >= 2:
                S.wait("sync", "sS", cast_ph0[idx - 2])
            S.op("sync", lambda sy, sl=slot, t=tt: sy.dma_start(
                out=xring[:, D * sl: D * (sl + 1)], in_=x0_tiles[t]), sem=xsem, inc=16)
            xm = (xsem, S.mark(xsem))
            xt = xring[:, D * slot: D * (slot + 1)]
            S.waitm("vector", xm)
            S.op("vector", lambda v, x=xt: v.reduce_sum(stats[:, 0:1], x, axis=AX.X), sem="sV", inc=1)
            sum_m = S.mark("sV")
            S.waitm("scalar", xm)
            if idx >= 1:
                S.wait("scalar", "sV", sq_consumed)
            S.op("scalar", lambda s, x=xt: s.activation(
                cring[:, 0:D], x, AF.Square, accum_out=stats[:, 1:2]), sem="sS", inc=1)
            sq_m = S.mark("sS")
            S.wait("vector", "sS", sq_m)
            S.wait("vector", "sV", sum_m)
            nb_m = ln_stats(EPS0, 0, 1, 2, 3, 4)
            sq_consumed = nb_m
            abf = afbf[:, slot * D:(slot + 1) * D]
            S.wait("scalar", "sV", nb_m)
            if idx >= 2:
                S.wait("scalar", "sT", trdone_ph0[idx - 2])
            S.op("scalar", lambda s, x=xt, dd=abf: s.activation(
                dd, x, AF.Identity, bias=stats[:, 4:5], scale=stats[:, 3:4]), sem="sS", inc=1)
            cast_ph0.append(S.mark("sS"))
            S.wait("tensor", "sS", cast_ph0[-1])
            if idx == 0:
                S.waitm("tensor", sm_done)
            for i in range(NCH):
                b = 6 + (i % 2)
                k = idx * NCH + i
                if k >= 2:
                    S.wait("tensor", "sV", evac_ph0[k - 2])
                S.op("tensor", lambda t, bb=b, aa=abf[:, i * 128:(i + 1) * 128]:
                     t.transpose(bankbf(bb), aa, iden[:]), sem="sT", inc=1)
                tm = S.mark("sT")
                S.wait("vector", "sT", tm)
                dst = trA[:, i * T + tt * 128: i * T + (tt + 1) * 128]
                S.op("vector", lambda v, bb=b, dd=dst: v.tensor_copy(dd, bankbf(bb)), sem="sV", inc=1)
                evac_ph0.append(S.mark("sV"))
            trdone_ph0.append(S.mark("sT"))
        atr_done = S.mark("sV")
        if stage == 0.1:
            finish_passthrough([("sV", atr_done), ("sS", S.mark("sS")), ("sT", S.mark("sT"))])
        if stage < 0.2 and stage != 1 and stage != 3:
            qkv_skip = True
        whs = [wqh, wkh, wvh]
        trX = [trQ, trK, trV]
        NSLOT = 4
        blk_ready, blk_used, evacm = [], [], []
        for j in range(NCH if stage >= 0.2 else 0):
            for w in range(3):
                k = j * 3 + w
                slot = k % NSLOT
                if k >= NSLOT:
                    S.wait("gpsimd", "sT", blk_used[k - NSLOT])
                src = whs[w].rearrange("(i p) m -> p i m", p=128)[:, :, j * 128:(j + 1) * 128]
                dst = wring.ap().rearrange("p (s i m) -> p s i m", s=NSLOT, i=NCH)[:, slot]
                wsem = f"sW{slot}"
                S.op("gpsimd", lambda g, d=dst, s=src: g.dma_start(out=d, in_=s), sem=wsem, inc=16)
                blk_ready.append((wsem, S.mark(wsem)))
                pair = k % 3
                b0, b1 = 2 * pair, 2 * pair + 1
                S.waitm("tensor", blk_ready[-1])
                if k == 0:
                    S.wait("tensor", "sV", atr_done)
                if k >= 3:
                    sem_, val_ = evacm[k - 3]
                    S.wait("tensor", sem_, val_)
                for i in range(NCH):
                    lhs = wring[:, slot * NCH * 128 + i * 128: slot * NCH * 128 + (i + 1) * 128]
                    S.op("tensor", lambda t, l=lhs, r=trA[:, i * T: i * T + 512], bb=b0,
                         st=(i == 0), sp=(i == NCH - 1): t.matmul(bank(bb), l, r, start=st, stop=sp))
                    last = (i == NCH - 1)
                    S.op("tensor", lambda t, l=lhs, r=trA[:, i * T + 512: i * T + 1024], bb=b1,
                         st=(i == 0), sp=(i == NCH - 1): t.matmul(bank(bb), l, r, start=st, stop=sp),
                         sem="sT" if last else None, inc=1 if last else 0)
                pe_m = S.mark("sT")
                blk_used.append(pe_m)
                dst2 = trX[w].ap().rearrange("p (t h) -> p h t", h=NCH)[:, j, :]
                if k % 2 == 0:
                    S.wait("vector", "sT", pe_m)
                    S.op("vector", lambda v, dd=dst2[:, 0:512], bb=b0: v.tensor_copy(dd, bank(bb)), sem="sV", inc=1)
                    S.op("vector", lambda v, dd=dst2[:, 512:1024], bb=b1: v.tensor_copy(dd, bank(bb)), sem="sV", inc=1)
                    evacm.append(("sV", S.mark("sV")))
                else:
                    S.wait("scalar", "sT", pe_m)
                    S.op("scalar", lambda s, dd=dst2[:, 0:512], bb=b0: s.copy(dd, bank(bb)), sem="sS", inc=1)
                    S.op("scalar", lambda s, dd=dst2[:, 512:1024], bb=b1: s.copy(dd, bank(bb)), sem="sS", inc=1)
                    evacm.append(("sS", S.mark("sS")))
        qkv_evV, qkv_evS = S.mark("sV"), S.mark("sS")
        if stage == 0.2:
            finish_passthrough([("sV", qkv_evV), ("sS", qkv_evS), ("sT", S.mark("sT"))])

        NG = T // 8
        scores_pe, eins_done = [], []
        sbd_m, vp_m = [], []
        if stage >= 0.3:
            S.wait("tensor", "sV", qkv_evV)
            S.wait("tensor", "sS", qkv_evS)
        pend = None
        for g in range(NG if stage >= 0.3 else 0):
            c0 = g * 128
            bS, bV, bO = (g % 2), 2 + (g % 2), 4 + (g % 2)
            sbd = afbf.ap()[:, (g % 2) * 128: (g % 2) * 128 + 128]
            vp = afbf.ap()[:, 256 + (g % 2) * 128: 256 + (g % 2) * 128 + 128]
            if g >= 2:
                S.wait("tensor", "sV", sbd_m[g - 2])
                sem_, val_ = vp_m[g - 2]
                S.wait("tensor", sem_, val_)
            if g >= 3:
                S.wait("tensor", "sS", eins_done[g - 3])
            S.op("tensor", lambda t, c=c0, bb=bS: t.matmul(
                bank(bb, 0, 128), trK[:, c: c + 128], trQ[:, c: c + 128],
                start=True, stop=True), sem="sT", inc=1)
            scores_pe.append(S.mark("sT"))
            S.op("tensor", lambda t, c=c0, bb=bV: t.transpose(
                bankbf(bb), trV[:, c: c + 128], iden[:]), sem="sT", inc=1)
            vpt = S.mark("sT")
            S.wait("vector", "sT", scores_pe[-1])
            S.op("vector", lambda v, bb=bS, dd=sbd: v.tensor_tensor(
                dd, bank(bb, 0, 128), maskc[:], op=ALU.mult), sem="sV", inc=1)
            sbd_m.append(S.mark("sV"))
            S.wait("scalar", "sT", vpt)
            S.op("scalar", lambda s, bb=bV, dd=vp: s.copy(dd, bankbf(bb)), sem="sS", inc=1)
            vp_m.append(("sS", S.mark("sS")))
            if pend is not None:
                pg, pvp, psbd, pbO = pend
                S.wait("tensor", "sV", sbd_m[pg])
                S.waitm("tensor", vp_m[pg])
                S.op("tensor", lambda t, vv=pvp, ss_=psbd, bb=pbO: t.matmul(
                    bank(bb, 0, 128), vv, ss_, start=True, stop=True), sem="sT", inc=1)
                om = S.mark("sT")
                S.wait("scalar", "sT", om)
                S.op("scalar", lambda s, bb=pbO, c=pg * 128: s.copy(trA[:, c: c + 128], bank(bb, 0, 128)),
                     sem="sS", inc=1)
                eins_done.append(S.mark("sS"))
            pend = (g, vp, sbd, bO)
        if pend is not None:
            pg, pvp, psbd, pbO = pend
            S.wait("tensor", "sV", sbd_m[pg])
            S.waitm("tensor", vp_m[pg])
            S.op("tensor", lambda t, vv=pvp, ss_=psbd, bb=pbO: t.matmul(
                bank(bb, 0, 128), vv, ss_, start=True, stop=True), sem="sT", inc=1)
            om = S.mark("sT")
            S.wait("scalar", "sT", om)
            S.op("scalar", lambda s, bb=pbO, c=pg * 128: s.copy(trA[:, c: c + 128], bank(bb, 0, 128)),
                 sem="sS", inc=1)
            eins_done.append(S.mark("sS"))

        if stage == 0.3:
            finish_passthrough([("sS", S.mark("sS")), ("sT", S.mark("sT")), ("sV", S.mark("sV"))])
        def wo_dst(i):
            if i < 7:
                return trQ.ap()[:, i * 2048:(i + 1) * 2048]
            if i < 12:
                return trK.ap()[:, (i - 7) * 2048:(i - 6) * 2048]
            return wring.ap()[:, (i - 12) * 2048:(i - 11) * 2048]

        wo_ready = []
        for i in range(NCH if stage >= 0.4 else 0):
            if i < 7:
                S.wait("gpsimd", "sT", scores_pe[16 * i + 15])
            elif i < 12:
                S.wait("gpsimd", "sT", scores_pe[16 * (i - 7) + 15])
            else:
                S.wait("gpsimd", "sT", blk_used[-1])
            src_ = woh.rearrange("(i p) m -> p i m", p=128)[:, i, :]
            wsem_ = "sWR" if i < 8 else "sWR2"
            S.op("gpsimd", lambda g, d=wo_dst(i), s=src_: g.dma_start(out=d, in_=s), sem=wsem_, inc=16)
            wo_ready.append(S.mark(wsem_))
        wo_res = ("sWR", S.mark("sWR"))
        wo_res2 = ("sWR2", S.mark("sWR2"))

        d1src = d1wh.rearrange("(i p) m -> p i m", p=128)
        if stage >= 0.4:
            S.wait("gpsimd", "sT", S.mark("sT"))
        for i4 in range(4 if stage >= 0.4 else 0):
            S.op("gpsimd", lambda g, d=trV.ap().rearrange("p (i m) -> p i m", i=NCH)[:, i4 * 4:(i4 + 1) * 4],
                 s=d1src[:, i4 * 4:(i4 + 1) * 4]: g.dma_start(out=d, in_=s), sem="sD1", inc=16)
        d1_in = ("sD1", S.mark("sD1"))

        x1_built, x1_spilled, x0_ld = [], [], []
        fstat_nb = [0] * NT
        ftr_done = [0] * NT
        wo_banksfree = []
        if stage == 0.4:
            finish_passthrough([("sWR", S.mark("sWR")), ("sD1", S.mark("sD1")), ("sT", S.mark("sT"))])
        fb_marks = [0] * NT
        fsq_l = [0] * NT
        tmark = [0] * NT
        red_done_l = [0] * NT
        stg_done_l = [0] * NT
        red_done = 0
        stg_done = 0

        def emit_ftr(m):
            ttm_ = TORD[m]
            s0m = (m * 4) % 6
            tbanks = [(s0m + 2) % 6, (s0m + 3) % 6]
            fbm_ = fb_marks[m]
            fbuf = afbf[:, (m % 2) * D: (m % 2 + 1) * D]
            S.wait("tensor", "sS", fbm_)
            S.wait("tensor", "sV", red_done_l[m])
            ev = []
            for i in range(NCH):
                b_ = tbanks[i % 2]
                if i >= 2:
                    S.wait("tensor", "sV", ev[i - 2])
                S.op("tensor", lambda t, bb=b_, aa=fbuf[:, i * 128:(i + 1) * 128]:
                     t.transpose(bankbf(bb), aa, iden[:]), sem="sT", inc=1)
                tm_ = S.mark("sT")
                S.wait("vector", "sT", tm_)
                dst = trA[:, ttm_ * 2048 + i * 128: ttm_ * 2048 + (i + 1) * 128]
                S.op("vector", lambda v, bb=b_, dd=dst: v.tensor_copy(dd, bankbf(bb)), sem="sV", inc=1)
                ev.append(S.mark("sV"))
            tmark[m] = S.mark("sT")
            ftr_done[TORD[m]] = S.mark("sV")

        for n, tt in enumerate(TORD if stage >= 0.5 else []):
            slot = n % 2
            xsem = "sXA" if slot == 0 else "sXB"
            pb = 8 * (n % 2)
            if n >= 2:
                S.waitm("sync", x1_spilled[n - 2])
                S.wait("sync", "sS", fb_marks[n - 2])
            else:
                S.wait("sync", "sS", cast_ph0[6 + n])
            S.op("sync", lambda sy, sl=slot, t=tt: sy.dma_start(
                out=xring[:, D * sl: D * (sl + 1)], in_=x0_tiles[t]), sem=xsem, inc=16)
            x0_ld.append((xsem, S.mark(xsem)))

            s0 = (n * 4) % 6
            banks = [(s0 + q) % 6 for q in range(4)]
            if n == 0:
                S.waitm("tensor", wo_res)
            S.wait("tensor", "sS", eins_done[16 * tt + 15])
            if n >= 1:
                S.wait("tensor", "sV", red_done_l[n - 1])
            if n >= 2:
                S.wait("tensor", "sV", ftr_done[TORD[n - 2]])
            for i in range(NCH):
                if n == 0 and i == 8:
                    S.waitm("tensor", wo_res2)
                lhs = trA.ap().rearrange("p (t h) -> p h t", h=NCH)[:, i, tt * 128:(tt + 1) * 128]
                wsrc = wo_dst(i)
                for q in range(4):
                    last = (i == NCH - 1 and q == 3)
                    S.op("tensor", lambda t, l=lhs, r=wsrc[:, q * 512:(q + 1) * 512], bb=banks[q],
                         st=(i == 0), sp=(i == NCH - 1): t.matmul(bank(bb), l, r, start=st, stop=sp),
                         sem="sT" if last else None, inc=1 if last else 0)
            wo_pe = S.mark("sT")
            if n >= 1 and stage >= 1:
                emit_ftr(n - 1)
            S.wait("vector", "sT", wo_pe)
            if n >= 1:
                S.wait("vector", "sS", fsq_l[n - 1])
            for q in range(4):
                S.op("vector", lambda v, bb=banks[q], q_=q: v.tensor_copy(
                    cring[:, q_ * 512:(q_ + 1) * 512], bank(bb)), sem="sV", inc=1)
            red_done = S.mark("sV")
            red_done_l[n] = red_done
            S.wait("vector", "sV", red_done)
            S.op("vector", lambda v, a=pb: v.reduce_sum(stats[:, a:a + 1], cring[:, 0:D], axis=AX.X), sem="sV", inc=1)
            S.wait("scalar", "sV", red_done)
            if n >= 1:
                S.wait("scalar", "sV", x1_built[-1])
            S.op("scalar", lambda s, a=pb: s.activation(
                cring[:, D:2 * D], cring[:, 0:D], AF.Square, accum_out=stats[:, a + 1:a + 2]), sem="sS", inc=1)
            S.wait("vector", "sS", S.mark("sS"))
            S.wait("vector", "sV", S.mark("sV"))
            nb_m = ln_stats(EPS0, pb, pb + 1, pb + 2, pb + 3, pb + 4)
            S.wait("scalar", "sV", nb_m)
            S.op("scalar", lambda s, a=pb: s.activation(
                cring[:, 0:D], cring[:, 0:D], AF.Identity,
                bias=stats[:, a + 4:a + 5], scale=stats[:, a + 3:a + 4]), sem="sS", inc=1)
            stg_done = S.mark("sS")
            stg_done_l[n] = stg_done
            S.wait("vector", "sS", stg_done)
            S.waitm("vector", x0_ld[n])
            xt = xring[:, D * slot: D * (slot + 1)]
            S.op("vector", lambda v, x=xt: v.tensor_tensor(x, x, cring[:, 0:D], op=ALU.add), sem="sV", inc=1)
            x1_built.append(S.mark("sV"))
            x1t = xt
            S.wait("vector", "sV", x1_built[-1])
            S.op("vector", lambda v, x=x1t, a=pb: v.reduce_sum(stats[:, a:a + 1], x, axis=AX.X), sem="sV", inc=1)
            S.wait("scalar", "sV", x1_built[-1])
            S.op("scalar", lambda s, x=x1t, a=pb: s.activation(
                cring[:, 0:D], x, AF.Square, accum_out=stats[:, a + 1:a + 2]), sem="sS", inc=1)
            fsq_l[n] = S.mark("sS")
            S.wait("vector", "sS", S.mark("sS"))
            S.wait("vector", "sV", S.mark("sV"))
            nbf = ln_stats(EPS0, pb, pb + 1, pb + 2, 32 + tt, 40 + tt)
            fstat_nb[tt] = nbf
            fb = afbf[:, (n % 2) * D: (n % 2 + 1) * D]
            S.wait("scalar", "sV", nbf)
            if n >= 2 and stage >= 1:
                S.wait("scalar", "sT", tmark[n - 2])
            S.op("scalar", lambda s, x=x1t, dd=fb, t_=tt: s.activation(
                dd, x, AF.Identity, bias=stats[:, 40 + t_:41 + t_], scale=stats[:, 32 + t_:33 + t_]),
                sem="sS", inc=1)
            fb_marks[n] = S.mark("sS")
            if stage < 1:
                ftr_done[tt] = S.mark("sV")
            S.wait("sync", "sV", x1_built[-1])
            S.op("sync", lambda sy, sl=slot, t_=tt: sy.dma_start(
                out=x1s_tiles[t_], in_=xring[:, D * sl: D * (sl + 1)]), sem=xsem, inc=16)
            x1_spilled.append((xsem, S.mark(xsem)))
        if stage >= 1:
            emit_ftr(NT - 1)
        wo_all_pe = S.mark("sT")
        if stage in (0.5, 0.51, 0.52, 0.53, 0.55, 0.57, 0.58, 0.6):
            finish_passthrough([("sT", wo_all_pe), ("sS", S.mark("sS")), ("sV", S.mark("sV")),
                                ("sXA", S.mark("sXA")), ("sXB", S.mark("sXB"))])
        if stage < 1:
            pass
        elif stage == 1:
            S.wait("sync", "sXA", S.mark("sXA"))
            S.wait("sync", "sXB", S.mark("sXB"))
            for n in range(NT):
                S.wait("sync", "sDC", S.mark("sDC"))
                S.op("sync", lambda sy, t_=n: sy.dma_start(
                    out=xring[:, 0:D], in_=x1s_tiles[t_]), sem="sDC", inc=16)
                S.wait("sync", "sDC", S.mark("sDC"))
                S.op("sync", lambda sy, t_=n: sy.dma_start(
                    out=xout_tiles[t_], in_=xring[:, 0:D]), sem="sDC", inc=16)
            S.wait("sync", "sDC", S.mark("sDC"))
            S.wait("gpsimd", "sDC", S.mark("sDC"))
        else:
            wg_src = wgh.rearrange("(i p) m -> p i m", p=128)
            S.wait("gpsimd", "sT", wo_all_pe)
            d2src = d2wh.rearrange("(c p) m -> p c m", p=128)
            for c4 in range(4):
                S.op("gpsimd", lambda g, d=trQ.ap().rearrange("p (c m) -> p c m", c=8)[:, c4 * 2:(c4 + 1) * 2],
                     s=d2src[:, c4 * 2:(c4 + 1) * 2]: g.dma_start(out=d, in_=s), sem="sD2", inc=16)
            d2s_in = S.mark("sD2")
            def gw_dst(c):
                if c < 8:
                    return trV.ap()[:, c * 2048:(c + 1) * 2048]
                if c < 12:
                    return wring.ap()[:, (c - 8) * 2048:(c - 7) * 2048]
                if c == 12:
                    return trK.ap()[:, 14336:16384]
                if c < 15:
                    return afbf.ap()[:, (c - 13) * 2048:(c - 12) * 2048]
                return trQ.ap()[:, 12288:14336]
            S.wait("gpsimd", "sV", max(ftr_done))
            for c in range(8, 15):
                S.op("gpsimd", lambda g, d=gw_dst(c), s=wg_src[:, c, :]: g.dma_start(out=d, in_=s),
                     sem="sGW", inc=16)
            gw_early = S.mark("sGW")
            S.waitm("tensor", d1_in)
            for i in range(NCH):
                for hh in range(2):
                    S.op("tensor", lambda t, i_=i, h_=hh: t.matmul(
                        bankrow(6 + h_, 2, 512), pnbt2[:, 2 * i_:2 * i_ + 2],
                        trV[:, i_ * 1024 + h_ * 512: i_ * 1024 + (h_ + 1) * 512],
                        start=(i_ == 0), stop=False), sem=None, inc=0)
            for hh in range(2):
                S.op("tensor", lambda t, h_=hh: t.matmul(
                    bankrow(6 + h_, 2, 512), ones2[:], d1bbf[0:1, h_ * 512:(h_ + 1) * 512],
                    start=False, stop=True), sem="sT" if hh == 1 else None, inc=1 if hh == 1 else 0)
            ebmm = S.mark("sT")
            S.wait("scalar", "sT", ebmm)
            S.op("scalar", lambda s: s.copy(ebd1[0:1, 0:512], bankrow(6, 1, 512)), sem="sS", inc=1)
            S.op("scalar", lambda s: s.copy(ebd1[0:1, 512:1024], bankrow(7, 1, 512)), sem="sS", inc=1)
            ebd1_m = S.mark("sS")
            S.wait("vector", "sT", ebmm)
            for i in range(NCH):
                S.op("vector", lambda v, i_=i: v.tensor_scalar(
                    trV[:, i_ * 1024:(i_ + 1) * 1024], trV[:, i_ * 1024:(i_ + 1) * 1024],
                    pngt[:, i_:i_ + 1], 0.0, op0=ALU.mult, op1=ALU.add), sem="sV", inc=1)
            d1s_ready = S.mark("sV")

            S.wait("vector", "sS", S.mark("sS"))
            S.op("vector", lambda v: v.memset(urow[:], 0.0), sem="sV", inc=1)
            um0 = S.mark("sV")
            S.wait("sync", "sV", um0)
            S.wait("sync", "sXA", S.mark("sXA"))
            S.wait("sync", "sXB", S.mark("sXB"))
            S.op("sync", lambda sy: sy.dma_start(out=urow[0:1, :], in_=x1scr[1023:1024, :]),
                 sem="sDC", inc=16)
            fl_m = ("sDC", S.mark("sDC"))

            def row_ln(src_row, dst_row, eps, sc=0):
                S.op("vector", lambda v: v.reduce_sum(srow[0:2, sc:sc + 1], src_row, axis=AX.X),
                     sem="sV", inc=1)
                sm = S.mark("sV")
                S.op("scalar", lambda s: s.activation(
                    cring[0:2, 0:D], src_row, AF.Square,
                    accum_out=srow[0:2, sc + 1:sc + 2]), sem="sS", inc=1)
                qm = S.mark("sS")
                S.wait("vector", "sV", sm)
                S.wait("vector", "sS", qm)
                S.op("vector", lambda v: v.tensor_scalar(
                    srow[0:2, sc:sc + 1], srow[0:2, sc:sc + 1], 1.0 / D, 0.0, op0=ALU.mult, op1=ALU.add), sem="sV", inc=1)
                S.op("vector", lambda v, ee=eps: v.tensor_scalar(
                    srow[0:2, sc + 1:sc + 2], srow[0:2, sc + 1:sc + 2], 1.0 / D, ee, op0=ALU.mult, op1=ALU.add), sem="sV", inc=1)
                S.wait("vector", "sV", S.mark("sV"))
                S.op("vector", lambda v: v.tensor_tensor(
                    srow[0:2, sc + 2:sc + 3], srow[0:2, sc:sc + 1], srow[0:2, sc:sc + 1], op=ALU.mult), sem="sV", inc=1)
                S.wait("vector", "sV", S.mark("sV"))
                S.op("vector", lambda v: v.tensor_tensor(
                    srow[0:2, sc + 1:sc + 2], srow[0:2, sc + 1:sc + 2], srow[0:2, sc + 2:sc + 3], op=ALU.subtract), sem="sV", inc=1)
                S.wait("scalar", "sV", S.mark("sV"))
                S.op("scalar", lambda s: s.activation(
                    srow[0:2, sc + 3:sc + 4], srow[0:2, sc + 1:sc + 2], AF.Ln), sem="sS", inc=1)
                S.wait("scalar", "sS", S.mark("sS"))
                S.op("scalar", lambda s: s.activation(
                    srow[0:2, sc + 3:sc + 4], srow[0:2, sc + 3:sc + 4], AF.Exp, scale=-0.5), sem="sS", inc=1)
                S.wait("vector", "sS", S.mark("sS"))
                S.op("vector", lambda v: v.tensor_scalar(
                    srow[0:2, sc + 4:sc + 5], srow[0:2, sc:sc + 1], srow[0:2, sc + 3:sc + 4], -1.0,
                    op0=ALU.mult, op1=ALU.mult), sem="sV", inc=1)
                S.wait("scalar", "sV", S.mark("sV"))
                S.op("scalar", lambda s: s.activation(
                    dst_row, src_row, AF.Identity, bias=srow[0:2, sc + 4:sc + 5],
                    scale=srow[0:2, sc + 3:sc + 4]), sem="sS", inc=1)
                return S.mark("sS")

            S.waitm("vector", fl_m)
            S.waitm("scalar", fl_m)
            m1 = row_ln(urow[:], urow[:], EPS0, sc=0)
            S.wait("vector", "sS", m1)
            S.wait("scalar", "sS", m1)
            m2 = row_ln(urow[:], cring[0:2, 0:D], EPSF, sc=5)
            S.wait("sync", "sS", m2)
            S.op("sync", lambda sy: sy.dma_start(out=ag1_in, in_=cring[0:1, 0:D]), sem="sDC", inc=16)
            S.wait("gpsimd", "sDC", S.mark("sDC"))
            S.op("gpsimd", lambda g: g.collective_compute(
                "AllGather", ALU.bypass, replica_groups=[list(range(NCORES))],
                ins=[ag1_in], outs=[ag1_out]), sem="sCC", inc=1)
            ag1_m = S.mark("sCC")
            S.wait("sync", "sCC", ag1_m)
            S.op("sync", lambda sy: sy.dma_start(out=cring[0:1, 0:D], in_=ag1_out[3:4, :]), sem="sDC", inc=16)
            S.op("sync", lambda sy: sy.dma_start(out=cring[1:2, 0:D], in_=ag1_out[7:8, :]), sem="sDC", inc=16)
            xn_m = ("sDC", S.mark("sDC"))
            S.waitm("vector", xn_m)
            S.op("vector", lambda v: v.tensor_copy(xnbf, cring[0:2, 0:D]), sem="sV", inc=1)
            xnbf_m = S.mark("sV")
            S.wait("tensor", "sV", xnbf_m)
            xe = []
            for i in range(NCH):
                bb = 6 + (i % 2)
                if i >= 2:
                    S.wait("tensor", "sV", xe[i - 2])
                S.op("tensor", lambda t, b_=bb, aa=xnbf[:, i * 128:(i + 1) * 128]:
                     t.transpose(PS.ap()[:, 512 * b_:512 * b_ + 1].bitcast(BF16), aa, iden[0:2, 0:2]),
                     sem="sT", inc=1)
                tm = S.mark("sT")
                S.wait("vector", "sT", tm)
                S.op("vector", lambda v, b_=bb, i_=i: v.tensor_copy(
                    xtn[:, 2 * i_:2 * i_ + 2], PS.ap()[:, 512 * b_:512 * b_ + 1].bitcast(BF16)), sem="sV", inc=1)
                xe.append(S.mark("sV"))
            xtn_m = S.mark("sV")
            agg_src = aggwh.rearrange("(i p) m -> p i m", p=128)
            agg_used = [0] * 4
            for b4 in range(4):
                sl = b4 % 2
                wsem = f"sW{sl}"
                if b4 >= 2:
                    S.wait("gpsimd", "sT", agg_used[b4 - 2])
                S.op("gpsimd", lambda g, d=agslot[sl].rearrange("p (c m) -> p c m", c=4),
                     s=agg_src[:, b4 * 4:(b4 + 1) * 4]: g.dma_start(out=d, in_=s), sem=wsem, inc=16)
                blkm = (wsem, S.mark(wsem))
                S.waitm("tensor", blkm)
                if b4 == 0:
                    S.wait("tensor", "sV", xtn_m)
                S.waitm("vector", blkm)
                for c in range(4):
                    i = b4 * 4 + c
                    chunk = agslot[sl][:, c * AG_SL:(c + 1) * AG_SL]
                    S.op("tensor", lambda t, i_=i, ch=chunk: t.matmul(
                        bankrow(6, 2, AG_SL), inbt2[:, 2 * i_:2 * i_ + 2], ch,
                        start=(i_ == 0), stop=(i_ == NCH - 1)), sem="sT", inc=1)
                    ebm = S.mark("sT")
                    S.wait("vector", "sT", ebm)
                    S.op("vector", lambda v, i_=i, ch=chunk: v.tensor_scalar(
                        ch, ch, ingt[:, i_:i_ + 1], 0.0, op0=ALU.mult, op1=ALU.add), sem="sV", inc=1)
                    fm = S.mark("sV")
                    S.wait("tensor", "sV", fm)
                    S.op("tensor", lambda t, i_=i, ch=chunk: t.matmul(
                        bankrow(7, 2, AG_SL), xtn[:, 2 * i_:2 * i_ + 2], ch,
                        start=(i_ == 0), stop=(i_ == NCH - 1)), sem="sT", inc=1)
                agg_used[b4] = S.mark("sT")
            agg_pe = S.mark("sT")
            S.wait("vector", "sT", agg_pe)
            S.op("vector", lambda v: v.tensor_copy(cring[0:2, 0:AG_SL], bankrow(7, 2, AG_SL)), sem="sV", inc=1)
            S.wait("vector", "sV", S.mark("sV"))
            S.op("vector", lambda v: v.tensor_tensor(
                cring[0:2, 0:AG_SL], cring[0:2, 0:AG_SL], bankrow(6, 2, AG_SL), op=ALU.add), sem="sV", inc=1)
            S.wait("vector", "sV", S.mark("sV"))
            S.op("vector", lambda v: v.tensor_tensor(
                cring[0:2, 0:AG_SL], cring[0:2, 0:AG_SL], aggb2[:], op=ALU.add), sem="sV", inc=1)
            th0_m = S.mark("sV")
            S.wait("sync", "sV", th0_m)
            S.op("sync", lambda sy: sy.dma_start(out=agg_in, in_=cring[0:2, 0:AG_SL]), sem="sDC", inc=16)
            S.wait("gpsimd", "sDC", S.mark("sDC"))
            S.op("gpsimd", lambda g: g.collective_compute(
                "AllGather", ALU.bypass, replica_groups=[list(range(NCORES))],
                ins=[agg_in], outs=[agg_out]), sem="sCC", inc=1)
            S.wait("sync", "sCC", S.mark("sCC"))
            S.op("sync", lambda sy: sy.dma_start(
                out=thought.rearrange("b (r s) -> b r s", r=NCORES),
                in_=agg_out.rearrange("(r b) s -> b r s", b=2)), sem="sDC", inc=16)
            th_m = ("sDC", S.mark("sDC"))

            d1m_l = [0] * 4
            d2m_l = [0] * 4
            prev = th_m
            for st in range(4):
                S.waitm("vector", prev)
                S.waitm("scalar", prev)
                mln = row_ln(thought, tb8, EPSF, sc=8)
                S.wait("tensor", "sS", mln)
                xe2 = []
                for i in range(NCH):
                    bb = 6 + (i % 2)
                    if i >= 2:
                        S.wait("tensor", "sV", xe2[i - 2])
                    S.op("tensor", lambda t, b_=bb, aa=tb8[:, i * 128:(i + 1) * 128]:
                         t.transpose(PS.ap()[:, 512 * b_:512 * b_ + 1].bitcast(BF16), aa, iden[0:2, 0:2]),
                         sem="sT", inc=1)
                    tm = S.mark("sT")
                    S.wait("vector", "sT", tm)
                    S.op("vector", lambda v, b_=bb, i_=i: v.tensor_copy(
                        xtn[:, 2 * i_:2 * i_ + 2], PS.ap()[:, 512 * b_:512 * b_ + 1].bitcast(BF16)), sem="sV", inc=1)
                    xe2.append(S.mark("sV"))
                ttm = S.mark("sV")
                S.wait("tensor", "sV", ttm)
                if st == 0:
                    S.wait("tensor", "sV", d1s_ready)
                    S.wait("tensor", "sS", ebd1_m)
                for i in range(NCH):
                    for hh in range(2):
                        S.op("tensor", lambda t, i_=i, h_=hh: t.matmul(
                            bankrow(6 + h_, 2, 512), xtn[:, 2 * i_:2 * i_ + 2],
                            trV[:, i_ * 1024 + h_ * 512: i_ * 1024 + (h_ + 1) * 512],
                            start=(i_ == 0), stop=False), sem=None, inc=0)
                for hh in range(2):
                    S.op("tensor", lambda t, h_=hh: t.matmul(
                        bankrow(6 + h_, 2, 512), ones2[:], ebd1[0:1, h_ * 512:(h_ + 1) * 512],
                        start=False, stop=True), sem="sT" if hh == 1 else None, inc=1 if hh == 1 else 0)
                d1m = S.mark("sT")
                d1m_l[st] = d1m
                if st == 3:
                    S.wait("gpsimd", "sT", d1m)
                    for c4 in range(4):
                        S.op("gpsimd", lambda g, d=trV.ap().rearrange("p (c m) -> p c m", c=8)[:, c4 * 2:(c4 + 1) * 2],
                             s=wg_src[:, c4 * 2:(c4 + 1) * 2, :]: g.dma_start(out=d, in_=s), sem="sGV", inc=16)
                    gv_m = S.mark("sGV")
                S.wait("scalar", "sT", d1m)
                S.op("scalar", lambda s: s.activation(tb8[:, 0:512], bankrow(6, 2, 512), AF.Gelu), sem="sS", inc=1)
                S.op("scalar", lambda s: s.activation(tb8[:, 512:1024], bankrow(7, 2, 512), AF.Gelu), sem="sS", inc=1)
                gm = S.mark("sS")
                S.wait("tensor", "sS", gm)
                xe3 = []
                for ii in range(8):
                    bb = 6 + (ii % 2)
                    if ii >= 2:
                        S.wait("tensor", "sV", xe3[ii - 2])
                    S.op("tensor", lambda t, b_=bb, i_=ii:
                         t.transpose(PS.ap()[:, 512 * b_:512 * b_ + 1].bitcast(BF16),
                                     tb8[:, i_ * 128:(i_ + 1) * 128], iden[0:2, 0:2]), sem="sT", inc=1)
                    tm = S.mark("sT")
                    S.wait("vector", "sT", tm)
                    S.op("vector", lambda v, b_=bb, i_=ii: v.tensor_copy(
                        xtn[:, 32 + 2 * i_:32 + 2 * i_ + 2], PS.ap()[:, 512 * b_:512 * b_ + 1].bitcast(BF16)),
                        sem="sV", inc=1)
                    xe3.append(S.mark("sV"))
                tgm = S.mark("sV")
                S.wait("tensor", "sV", tgm)
                if st == 0:
                    S.wait("tensor", "sD2", d2s_in)
                for c in range(8):
                    for q in range(4):
                        S.op("tensor", lambda t, c_=c, q_=q: t.matmul(
                            bankrow(q_, 2, 512), xtn[:, 32 + 2 * c_:32 + 2 * c_ + 2],
                            trQ[:, c_ * 2048 + q_ * 512: c_ * 2048 + (q_ + 1) * 512],
                            start=(c_ == 0), stop=False), sem=None, inc=0)
                for q in range(4):
                    S.op("tensor", lambda t, q_=q: t.matmul(
                        bankrow(q_, 2, 512), ones2[:], d2brow[0:1, q_ * 512:(q_ + 1) * 512],
                        start=False, stop=True), sem="sT" if q == 3 else None, inc=1 if q == 3 else 0)
                d2m = S.mark("sT")
                d2m_l[st] = d2m
                S.wait("vector", "sT", d2m)
                S.op("vector", lambda v: v.tensor_copy(urow[:], PS[0:2, 0:2048]), sem="sV", inc=1)
                um = S.mark("sV")
                if st == 3:
                    um3 = um
                S.wait("sync", "sV", um)
                S.op("sync", lambda sy, s_=st: sy.dma_start(out=ar_inh[s_], in_=urow[:]), sem="sDC", inc=16)
                S.wait("gpsimd", "sDC", S.mark("sDC"))
                S.op("gpsimd", lambda g, s_=st: g.collective_compute(
                    "AllReduce", ALU.add, replica_groups=[list(range(NCORES))],
                    ins=[ar_inh[s_]], outs=[ar_outh[s_]]), sem="sCC", inc=1)
                S.wait("sync", "sCC", S.mark("sCC"))
                S.op("sync", lambda sy, s_=st: sy.dma_start(out=urow[:], in_=ar_outh[s_]), sem="sDC", inc=16)
                arm = S.mark("sDC")
                S.wait("vector", "sDC", arm)
                S.op("vector", lambda v: v.tensor_tensor(thought, thought, urow[:], op=ALU.add), sem="sV", inc=1)
                prev = ("sV", S.mark("sV"))
                if st == 3:
                    S.wait("gpsimd", "sT", d2m)
                    S.wait("gpsimd", "sV", prev[1])
                    S.op("gpsimd", lambda g: g.dma_start(out=gw_dst(15), in_=wg_src[:, 15, :]), sem="sGQ", inc=16)
                    S.op("gpsimd", lambda g: g.dma_start(out=gbrow, in_=gatebh), sem="sGQ", inc=16)
                    gq_m = S.mark("sGQ")

            S.wait("vector", prev[0], prev[1])
            S.op("vector", lambda v: v.tensor_copy(tb8[:, 0:D], thought), sem="sV", inc=1)
            tbm = S.mark("sV")
            S.wait("tensor", "sV", tbm)
            xe4 = []
            for i in range(NCH):
                bb = 6 + (i % 2)
                if i >= 2:
                    S.wait("tensor", "sV", xe4[i - 2])
                S.op("tensor", lambda t, b_=bb, i_=i: t.transpose(
                    PS.ap()[:, 512 * b_:512 * b_ + 1].bitcast(BF16),
                    tb8[:, i_ * 128:(i_ + 1) * 128], iden[0:2, 0:2]), sem="sT", inc=1)
                tm = S.mark("sT")
                S.wait("vector", "sT", tm)
                S.op("vector", lambda v, b_=bb, i_=i: v.tensor_copy(
                    xtn[:, 2 * i_:2 * i_ + 2], PS.ap()[:, 512 * b_:512 * b_ + 1].bitcast(BF16)), sem="sV", inc=1)
                xe4.append(S.mark("sV"))
            btm = S.mark("sV")

            sig_l = [0] * NT
            sigq2_l = [0] * NT
            gpe_l = [0] * NT
            x1m_l = [None] * NT
            ln2_l = [0] * NT
            x2_l = [0] * NT
            out_m = [None] * NT
            gbanks = [[((n * 4) % 6 + q) % 6 for q in range(4)] for n in range(NT)]

            def gw_chunkap(c, q):
                return gw_dst(c)[:, q * 512:(q + 1) * 512]

            def gate_pe_half(n, half):
                tt = TORD[n]
                banks = gbanks[n]
                q0 = 2 * half
                if half == 0:
                    if n == 0:
                        S.wait("tensor", "sGV", gv_m)
                        S.wait("tensor", "sGW", gw_early)
                        S.wait("tensor", "sV", max(ftr_done))
                        S.wait("tensor", "sV", um3)
                    if n >= 2:
                        S.wait("tensor", "sS", sig_l[n - 2])
                else:
                    if n >= 1:
                        S.wait("tensor", "sS", sigq2_l[n - 1])
                for i in range(NCH - 1):
                    for dq in range(2):
                        S.op("tensor", lambda t, l=trA[:, tt * 2048 + i * 128: tt * 2048 + (i + 1) * 128],
                             r=gw_chunkap(i, q0 + dq), bb=banks[q0 + dq], st=(i == 0):
                             t.matmul(bank(bb), l, r, start=st, stop=False), sem=None, inc=0)
                if n == 0 and half == 0:
                    S.wait("tensor", "sGQ", gq_m)
                for dq in range(2):
                    S.op("tensor", lambda t, bb=banks[q0 + dq], q_=q0 + dq: t.matmul(
                        bank(bb), ones128[:], gbrow[0:1, q_ * 512:(q_ + 1) * 512],
                        start=False, stop=False), sem=None, inc=0)
                for dq in range(2):
                    S.op("tensor", lambda t, l=trA[:, tt * 2048 + 15 * 128: tt * 2048 + 16 * 128],
                         r=gw_chunkap(15, q0 + dq), bb=banks[q0 + dq]:
                         t.matmul(bank(bb), l, r, start=False, stop=True),
                         sem="sT" if dq == 1 else None, inc=1 if dq == 1 else 0)
                if half == 1:
                    gpe_l[n] = S.mark("sT")

            def gate_sig(n):
                tt = TORD[n]
                banks = gbanks[n]
                S.wait("scalar", "sT", gpe_l[n])
                for q in range(4):
                    S.op("scalar", lambda s, bb=banks[q], q_=q: s.activation(
                        trA[:, tt * 2048 + q_ * 512: tt * 2048 + (q_ + 1) * 512],
                        bank(bb), AF.Sigmoid), sem="sS", inc=1)
                    if q == 1:
                        sigq2_l[n] = S.mark("sS")
                sig_l[n] = S.mark("sS")

            bc_src = bcwh.rearrange("(i p) m -> p i m", p=128)
            bc_used = [0] * 4
            for b4 in range(4):
                sl = b4 % 2
                wsem = f"sW{sl}"
                if b4 >= 2:
                    S.wait("gpsimd", "sT", bc_used[b4 - 2])
                S.op("gpsimd", lambda g, d=agslot[sl].rearrange("p (c m) -> p c m", c=4),
                     s=bc_src[:, b4 * 4:(b4 + 1) * 4]: g.dma_start(out=d, in_=s), sem=wsem, inc=16)
                blkm = (wsem, S.mark(wsem))
                S.waitm("tensor", blkm)
                if b4 == 0:
                    S.wait("tensor", "sV", btm)
                for c in range(4):
                    i = b4 * 4 + c
                    chunk = agslot[sl][:, c * AG_SL:(c + 1) * AG_SL]
                    S.op("tensor", lambda t, i_=i, ch=chunk: t.matmul(
                        bankrow(6, 2, AG_SL), xtn[:, 2 * i_:2 * i_ + 2], ch,
                        start=(i_ == 0), stop=(i_ == NCH - 1)), sem="sT", inc=1)
                bc_used[b4] = S.mark("sT")
                gate_pe_half(b4 // 2, b4 % 2)
                if b4 % 2 == 1:
                    gate_sig(b4 // 2)
            bc_pe = bc_used[3]
            S.wait("vector", "sT", bc_pe)
            S.op("vector", lambda v: v.tensor_copy(cring[0:2, 0:AG_SL], bankrow(6, 2, AG_SL)), sem="sV", inc=1)
            S.wait("vector", "sV", S.mark("sV"))
            S.op("vector", lambda v: v.tensor_tensor(
                cring[0:2, 0:AG_SL], cring[0:2, 0:AG_SL], bcb2[:], op=ALU.add), sem="sV", inc=1)
            bcm = S.mark("sV")
            S.wait("sync", "sV", bcm)
            S.op("sync", lambda sy: sy.dma_start(out=bc_inh, in_=cring[0:2, 0:AG_SL]), sem="sDC", inc=16)
            S.wait("gpsimd", "sDC", S.mark("sDC"))
            S.op("gpsimd", lambda g: g.collective_compute(
                "AllGather", ALU.bypass, replica_groups=[list(range(NCORES))],
                ins=[bc_inh], outs=[bc_outh]), sem="sCC", inc=1)
            S.wait("sync", "sCC", S.mark("sCC"))
            S.op("sync", lambda sy: sy.dma_start(
                out=cring.ap()[0:2, 0:D].rearrange("b (r s) -> b r s", r=NCORES),
                in_=bc_outh.rearrange("(r b) s -> b r s", b=2)), sem="sDC", inc=16)
            bcrow_m = S.mark("sDC")
            if debug >= 2:
                S.wait("sync", "sDC", bcrow_m)
                S.op("sync", lambda sy: sy.dma_start(out=dbgbc, in_=cring[0:2, 0:D]), sem="sDC", inc=16)
                bcrow_m = S.mark("sDC")
            def emit_sel2_bcbc():
                S.wait("tensor", "sDC", bcrow_m)
                bm = 0
                for p in range(2):
                    if p == 1:
                        S.wait("tensor", "sV", bm)
                    for h in range(2):
                        q = 2 * p + h
                        S.op("tensor", lambda t, b_=6 + h, q_=q: t.matmul(
                            bank(b_), sel2[:], cring[0:2, q_ * 512:(q_ + 1) * 512],
                            start=True, stop=True), sem="sT" if h == 1 else None, inc=1 if h == 1 else 0)
                    pm = S.mark("sT")
                    S.wait("vector", "sT", pm)
                    for h in range(2):
                        q = 2 * p + h
                        S.op("vector", lambda v, b_=6 + h, q_=q: v.tensor_copy(
                            bcbc[:, q_ * 512:(q_ + 1) * 512], bank(b_)), sem="sV", inc=1)
                    bm = S.mark("sV")
                return bm

            S.op("sync", lambda sy: sy.dma_start(out=gamv, in_=gambh), sem="sGB", inc=16)
            S.op("sync", lambda sy: sy.dma_start(out=betv, in_=betbh), sem="sGB", inc=16)
            gb_m = S.mark("sGB")
            S.op("sync", lambda sy: sy.dma_start(
                out=xring[:, 0:D], in_=x1s_tiles[TORD[0]]), sem="sXA", inc=16)
            x1m_l[0] = ("sXA", S.mark("sXA"))
            S.op("sync", lambda sy: sy.dma_start(
                out=xring[:, D:2 * D], in_=x1s_tiles[TORD[1]]), sem="sXB", inc=16)
            x1m_l[1] = ("sXB", S.mark("sXB"))

            def combine_stages(k, bufs=None):
                tt_ = TORD[k]
                sl2 = k % 2
                if bufs is None:
                    xsem_ = "sXA" if sl2 == 0 else "sXB"
                    xt = xring[:, D * sl2: D * (sl2 + 1)]
                    crX = cring[:, 0:D] if sl2 == 0 else cring[:, D:2 * D]
                    scrX = scr2 if sl2 == 0 else scr2b
                    st0 = 0 if sl2 == 0 else 8
                else:
                    crX, scrX, st0, xt, xsem_ = bufs
                gp = trA[:, tt_ * 2048:(tt_ + 1) * 2048]
                mk = {}

                def s_gbc():
                    S.wait("gpsimd", "sS", sig_l[k])
                    if k == 0:
                        S.wait("gpsimd", "sV", bcbc_m)
                    if bufs is None and k >= 2:
                        S.wait("gpsimd", "sV", x2_l[k - 2])
                    S.op("gpsimd", lambda g, g_=gp, c_=crX: g.tensor_tensor(
                        c_, g_, bcbc[:, 0:D], op=ALU.mult), sem="sP", inc=1)
                    mk["gbc"] = S.mark("sP")

                def s_t1():
                    S.waitm("scalar", x1m_l[k])
                    if bufs is not None:
                        pass
                    elif k < 2:
                        S.wait("scalar", "sT", d2m_l[3])
                    else:
                        S.wait("scalar", "sS", ln2_l[k - 2])
                    S.op("scalar", lambda s, x=xt, t_=tt_, d_=scrX: s.activation(
                        d_, x, AF.Identity, bias=stats[:, 40 + t_:41 + t_],
                        scale=stats[:, 32 + t_:33 + t_]), sem="sS", inc=1)
                    mk["t1"] = S.mark("sS")

                def s_outt():
                    S.wait("vector", "sS", mk["t1"])
                    S.wait("vector", "sP", mk["gbc"])
                    S.op("vector", lambda v, c_=crX, d_=scrX: v.tensor_tensor(
                        c_, c_, d_, op=ALU.add), sem="sV", inc=1)
                    mk["outt"] = S.mark("sV")

                def s_stats1():
                    S.wait("vector", "sV", mk["outt"])
                    S.op("vector", lambda v, c_=crX: v.reduce_sum(
                        stats[:, st0:st0 + 1], c_, axis=AX.X), sem="sV", inc=1)
                    mk["red1"] = S.mark("sV")
                    S.wait("scalar", "sV", mk["outt"])
                    S.op("scalar", lambda s, c_=crX, d_=scrX: s.activation(
                        d_, c_, AF.Square, accum_out=stats[:, st0 + 1:st0 + 2]), sem="sS", inc=1)
                    mk["sq1"] = S.mark("sS")

                def s_ln1():
                    S.wait("vector", "sS", mk["sq1"])
                    S.wait("vector", "sV", mk["red1"])
                    mk["nb2"] = ln_stats(EPSF, st0, st0 + 1, st0 + 2, st0 + 3, st0 + 4)

                def s_yh():
                    S.wait("scalar", "sV", mk["nb2"])
                    S.op("scalar", lambda s, c_=crX, d_=scrX: s.activation(
                        d_, c_, AF.Identity, bias=stats[:, st0 + 4:st0 + 5],
                        scale=stats[:, st0 + 3:st0 + 4]), sem="sS", inc=1)
                    mk["yh"] = S.mark("sS")

                def s_gmul():
                    S.wait("vector", "sS", mk["yh"])
                    if k == 0:
                        S.wait("vector", "sGB", gb_m)
                    S.op("vector", lambda v, d_=scrX: v.tensor_tensor(
                        d_, d_, gamv, op=ALU.mult), sem="sV", inc=1)
                    mk["gmul"] = S.mark("sV")

                def s_beta():
                    S.wait("gpsimd", "sV", mk["gmul"])
                    if k == 0:
                        S.wait("gpsimd", "sGB", gb_m)
                    S.op("gpsimd", lambda g, d_=scrX: g.tensor_tensor(
                        d_, d_, betv, op=ALU.add), sem="sP", inc=1)
                    mk["y"] = S.mark("sP")

                def s_stats2():
                    if k < 4:
                        S.wait("vector", "sP", mk["y"])
                        S.op("vector", lambda v, d_=scrX: v.reduce_sum(
                            stats[:, st0:st0 + 1], d_, axis=AX.X), sem="sV", inc=1)
                        mk["red2"] = ("sV", S.mark("sV"))
                    S.wait("scalar", "sP", mk["y"])
                    S.op("scalar", lambda s, c_=crX, d_=scrX: s.activation(
                        c_, d_, AF.Square, accum_out=stats[:, st0 + 1:st0 + 2]), sem="sS", inc=1)
                    if k >= 4:
                        S.op("scalar", lambda s, c_=crX, d_=scrX: s.activation(
                            c_, d_, AF.Identity, accum_out=stats[:, st0:st0 + 1]), sem="sS", inc=1)
                        mk["red2"] = ("sS", S.mark("sS"))
                    mk["sq2"] = S.mark("sS")

                def s_ln2():
                    S.wait("vector", "sS", mk["sq2"])
                    S.waitm("vector", mk["red2"])
                    mk["nb3"] = ln_stats(EPS0, st0, st0 + 1, st0 + 2, st0 + 3, st0 + 4)

                def s_t3():
                    S.wait("scalar", "sV", mk["nb3"])
                    S.op("scalar", lambda s, c_=crX, d_=scrX: s.activation(
                        c_, d_, AF.Identity, bias=stats[:, st0 + 4:st0 + 5],
                        scale=stats[:, st0 + 3:st0 + 4]), sem="sS", inc=1)
                    ln2_l[k] = S.mark("sS")

                def s_x2dma():
                    S.wait("vector", "sS", ln2_l[k])
                    S.op("vector", lambda v, x=xt, c_=crX: v.tensor_tensor(x, x, c_, op=ALU.add), sem="sV", inc=1)
                    x2_l[k] = S.mark("sV")
                    S.wait("sync", "sV", x2_l[k])
                    S.op("sync", lambda sy, x=xt, t_=tt_: sy.dma_start(
                        out=xout_tiles[t_], in_=x), sem=xsem_, inc=16)
                    out_m[k] = (xsem_, S.mark(xsem_))
                    if k + 2 < 6:
                        S.waitm("sync", out_m[k])
                        S.op("sync", lambda sy, sl=sl2, t_=TORD[k + 2]: sy.dma_start(
                            out=xring[:, D * sl: D * (sl + 1)], in_=x1s_tiles[t_]), sem=xsem_, inc=16)
                        x1m_l[k + 2] = (xsem_, S.mark(xsem_))

                return [s_gbc, s_t1, s_outt, s_stats1, s_ln1, s_yh, s_gmul,
                        s_beta, s_stats2, s_ln2, s_t3, s_x2dma]

            def emit_combine_pair(ka, kb):
                for a, b in zip(combine_stages(ka), combine_stages(kb)):
                    a()
                    b()

            def emit_combine_quad():
                crC = trQ.ap()[:, 8192:12288].bitcast(F32)
                scrC = wring.ap()[:, 0:4096].bitcast(F32)
                crD = wring.ap()[:, 4096:8192].bitcast(F32)
                scrD = trV.ap()[:, 0:4096].bitcast(F32)
                xC = trV.ap()[:, 4096:8192].bitcast(F32)
                xD = trV.ap()[:, 8192:12288].bitcast(F32)
                S.wait("sync", "sT", gpe_l[7])
                S.op("sync", lambda sy: sy.dma_start(
                    out=xC, in_=x1s_tiles[TORD[6]]), sem="sDC", inc=16)
                x1m_l[6] = ("sDC", S.mark("sDC"))
                S.op("sync", lambda sy: sy.dma_start(
                    out=xD, in_=x1s_tiles[TORD[7]]), sem="sXC", inc=16)
                x1m_l[7] = ("sXC", S.mark("sXC"))
                stl = [combine_stages(4), combine_stages(5),
                       combine_stages(6, bufs=(crC, scrC, 16, xC, "sXA")),
                       combine_stages(7, bufs=(crD, scrD, 24, xD, "sXB"))]
                ns = len(stl[0])
                for r in range(ns + 2):
                    if r < ns:
                        stl[0][r]()
                        stl[1][r]()
                    if 0 <= r - 2 < ns:
                        stl[2][r - 2]()
                        stl[3][r - 2]()

            gate_pe_half(2, 0)
            gate_pe_half(2, 1)
            gate_sig(2)
            bcbc_m = emit_sel2_bcbc()
            gate_pe_half(3, 0)
            gate_pe_half(3, 1)
            gate_sig(3)
            emit_combine_pair(0, 1)
            for n in (4, 5):
                gate_pe_half(n, 0)
                gate_pe_half(n, 1)
                gate_sig(n)
            emit_combine_pair(2, 3)
            for n in (6, 7):
                gate_pe_half(n, 0)
                gate_pe_half(n, 1)
                gate_sig(n)
            emit_combine_quad()
            S.wait("sync", "sXA", S.mark("sXA"))
            S.wait("sync", "sXB", S.mark("sXB"))
            S.wait("gpsimd", "sXA", S.mark("sXA"))
            S.wait("gpsimd", "sXB", S.mark("sXB"))

        @block.tensor
        def _(t):
            S.emit("tensor", t, sems)

        @block.vector
        def _(v):
            S.emit("vector", v, sems)

        @block.scalar
        def _(s):
            S.emit("scalar", s, sems)

        @block.gpsimd
        def _(g):
            S.emit("gpsimd", g, sems)

        @block.sync
        def _(sy):
            S.emit("sync", sy, sems)

    return nc


_BF = ml_dtypes.bfloat16


def _prep_inputs(inp):
    x = np.asarray(inp["L0_attn_x"], np.float32).reshape(2 * 4096, D)
    cores = []
    iden = np.eye(128, dtype=_BF)
    maskc = np.kron(np.eye(8, dtype=np.float32), np.ones((16, 16), np.float32))
    ones2 = np.ones((1, 2), dtype=_BF)
    ones128 = np.ones((1, 128), dtype=_BF)
    tr = lambda v: np.ascontiguousarray(np.asarray(v, np.float32).reshape(NCH, 128).T)
    dup2 = lambda m: np.ascontiguousarray(np.repeat(m, 2, axis=1))
    pngt = tr(inp["L0_ffn_pn_gamma"])
    pnbt2 = dup2(tr(inp["L0_ffn_pn_beta"]))
    ingt = tr(inp["L0_ffn_in_gamma"])
    inbt2 = dup2(tr(inp["L0_ffn_in_beta"]))
    gamb = np.ascontiguousarray(np.broadcast_to(np.asarray(inp["L0_ffn_out_gamma"], np.float32), (128, D)))
    betb = np.ascontiguousarray(np.broadcast_to(np.asarray(inp["L0_ffn_out_beta"], np.float32), (128, D)))
    d2b_full = np.asarray(inp["L0_ffn_d2_b"], np.float32).reshape(1, D)
    zeros_d2b = np.zeros_like(d2b_full)
    for c in range(NCORES):
        b = c // 4
        sel2 = np.zeros((2, 128), np.float32)
        sel2[b, :] = 1.0
        m = {
            "x0": np.ascontiguousarray(x[c * T:(c + 1) * T]),
            "wq": np.asarray(inp["L0_attn_Wq"], np.float32),
            "wk": np.asarray(inp["L0_attn_Wk"], np.float32),
            "wv": np.asarray(inp["L0_attn_Wv"], np.float32),
            "wo": np.asarray(inp["L0_attn_Wo"], np.float32),
            "wg": np.asarray(inp["L0_ffn_gate_w"], np.float32),
            "d1w": np.ascontiguousarray(np.asarray(inp["L0_ffn_d1_w"], np.float32)[:, c * TH_SL:(c + 1) * TH_SL]),
            "d2w": np.ascontiguousarray(np.asarray(inp["L0_ffn_d2_w"], np.float32)[c * TH_SL:(c + 1) * TH_SL, :]),
            "aggw": np.ascontiguousarray(np.asarray(inp["L0_ffn_agg_w"], np.float32)[:, c * AG_SL:(c + 1) * AG_SL]),
            "bcw": np.ascontiguousarray(np.asarray(inp["L0_ffn_bc_w"], np.float32)[:, c * AG_SL:(c + 1) * AG_SL]),
            "d1b": np.ascontiguousarray(np.asarray(inp["L0_ffn_d1_b"], np.float32)[None, c * TH_SL:(c + 1) * TH_SL]),
            "d2b": d2b_full if c == 0 else zeros_d2b,
            "aggb": np.ascontiguousarray(np.broadcast_to(
                np.asarray(inp["L0_ffn_agg_b"], np.float32)[None, c * AG_SL:(c + 1) * AG_SL], (2, AG_SL))),
            "bcb2": np.ascontiguousarray(np.broadcast_to(
                np.asarray(inp["L0_ffn_bc_b"], np.float32)[None, c * AG_SL:(c + 1) * AG_SL], (2, AG_SL))),
            "gateb": np.asarray(inp["L0_ffn_gate_b"], np.float32).reshape(1, D),
            "gamb": gamb, "betb": betb,
            "pngt": pngt, "pnbt2": pnbt2, "ingt": ingt, "inbt2": inbt2,
            "iden": iden, "maskc": maskc, "ones2": ones2, "ones128": ones128,
            "sel2": sel2,
        }
        cores.append(m)
    return cores


def kernel(**inputs):
    in_maps = _prep_inputs(inputs)
    nc = build(stage=STAGE, debug=DEBUG)
    res = run_bass_kernel_spmd(nc, in_maps, core_ids=list(range(NCORES)))
    out = np.concatenate([np.asarray(r["xout"], np.float32) for r in res.results], axis=0)
    return out.reshape(2, 4096, D)


# version 21 (final)
# speedup vs baseline: 1.1056x; 1.0053x over previous
import numpy as np
import ml_dtypes
import concourse.bass as bass
import concourse.mybir as mybir
from concourse.bass_utils import run_bass_kernel_spmd
from contextlib import ExitStack

F32 = mybir.dt.float32
BF16 = mybir.dt.bfloat16
AF = mybir.ActivationFunctionType
ALU = mybir.AluOpType
AX = mybir.AxisListType

NCORES = 8
T = 1024
NT = 8
D = 2048
NCH = 16
EPS0 = 1e-5
EPSF = 1e-6
TH_SL = 8192 // NCORES
AG_SL = D // NCORES
TORD = [7, 0, 1, 2, 3, 4, 5, 6]

STAGE = 3
DEBUG = 0


class Sched:
    ENGS = ("tensor", "vector", "scalar", "gpsimd", "sync")

    def __init__(self):
        self.items = {e: [] for e in self.ENGS}
        self.cnt = {}

    def op(self, eng, fn, sem=None, inc=0):
        self.items[eng].append(("op", fn, sem, inc))
        if sem is not None and inc:
            self.cnt[sem] = self.cnt.get(sem, 0) + inc
        return self.cnt.get(sem, 0) if sem else 0

    def wait(self, eng, sem, val):
        assert val <= self.cnt.get(sem, 0), f"wait {sem}>={val} unreachable ({self.cnt.get(sem, 0)})"
        self.items[eng].append(("wait", sem, val))

    def waitm(self, eng, mark):
        self.wait(eng, mark[0], mark[1])

    def mark(self, sem):
        return self.cnt.get(sem, 0)

    def emit(self, eng_name, eng, sems):
        for it in self.items[eng_name]:
            if it[0] == "wait":
                eng.wait_ge(sems[it[1]], it[2])
            else:
                inst = it[1](eng)
                if it[2] is not None and it[3]:
                    inst.then_inc(sems[it[2]], it[3])


def build(stage=3, debug=0):
    nc = bass.Bass("TRN2", target_bir_lowering=False)

    I = lambda n, s, d=F32: nc.dram_tensor(n, s, d, kind="ExternalInput").ap()
    x0h = I("x0", [T, D])
    wqh, wkh, wvh = I("wq", [D, D]), I("wk", [D, D]), I("wv", [D, D])
    woh, wgh = I("wo", [D, D]), I("wg", [D, D])
    d1wh, d2wh = I("d1w", [D, TH_SL]), I("d2w", [TH_SL, D])
    aggwh, bcwh = I("aggw", [D, AG_SL]), I("bcw", [D, AG_SL])
    d1bh = I("d1b", [1, TH_SL])
    d2bh = I("d2b", [1, D])
    aggbh, bcbh = I("aggb", [2, AG_SL]), I("bcb2", [2, AG_SL])
    gatebh = I("gateb", [1, D])
    gambh, betbh = I("gamb", [128, D]), I("betb", [128, D])
    pngth, pnbt2h = I("pngt", [128, NCH]), I("pnbt2", [128, 2 * NCH])
    ingth, inbt2h = I("ingt", [128, NCH]), I("inbt2", [128, 2 * NCH])
    idenh = I("iden", [128, 128], BF16)
    maskh = I("maskc", [128, 128])
    ones2h = I("ones2", [1, 2], BF16)
    ones128h = I("ones128", [1, 128], BF16)
    sel2h = I("sel2", [2, 128])

    xouth = nc.dram_tensor("xout", [T, D], F32, kind="ExternalOutput").ap()
    dbgbc = nc.dram_tensor("dbg_bc", [2, D], F32, kind="ExternalOutput").ap() if debug >= 2 else None

    x1scr = nc.dram_tensor("x1scr", [T, D], F32).ap()
    ag1_in = nc.dram_tensor("ag1_in", [1, D], F32).ap()
    ag1_out = nc.dram_tensor("ag1_out", [NCORES, D], F32, addr_space="Shared").ap()
    agg_in = nc.dram_tensor("agg_in", [2, AG_SL], F32).ap()
    agg_out = nc.dram_tensor("agg_out", [2 * NCORES, AG_SL], F32, addr_space="Shared").ap()
    ar_inh = [nc.dram_tensor(f"ar_in{s}", [2, D], F32).ap() for s in range(4)]
    ar_outh = [nc.dram_tensor(f"ar_out{s}", [2, D], F32, addr_space="Shared").ap() for s in range(4)]
    bc_inh = nc.dram_tensor("bc_in", [2, AG_SL], F32).ap()
    bc_outh = nc.dram_tensor("bc_out", [2 * NCORES, AG_SL], F32, addr_space="Shared").ap()

    with ExitStack() as stk:
        e = stk.enter_context
        trA = e(nc.sbuf_tensor("trA", [128, 16384], BF16))
        trQ = e(nc.sbuf_tensor("trQ", [128, 16384], BF16))
        trK = e(nc.sbuf_tensor("trK", [128, 16384], BF16))
        trV = e(nc.sbuf_tensor("trV", [128, 16384], BF16))
        xring = e(nc.sbuf_tensor("xring", [128, 2 * D], F32))
        cring = e(nc.sbuf_tensor("cring", [128, 2 * D], F32))
        wring = e(nc.sbuf_tensor("wring", [128, 8192], BF16))
        chsc = e(nc.sbuf_tensor("chsc", [128, 4096], BF16))
        afbf = e(nc.sbuf_tensor("afbf", [128, 2 * D], BF16))
        iden = e(nc.sbuf_tensor("idn", [128, 128], BF16))
        maskc = e(nc.sbuf_tensor("msk", [128, 128], F32))
        ones2 = e(nc.sbuf_tensor("o2", [1, 2], BF16))
        ones128 = e(nc.sbuf_tensor("o128", [1, 128], BF16))
        sel2 = e(nc.sbuf_tensor("sel2s", [2, 128], F32))
        pngt = e(nc.sbuf_tensor("pngts", [128, NCH], F32))
        pnbt2 = e(nc.sbuf_tensor("pnbt2s", [128, 2 * NCH], BF16))
        ingt = e(nc.sbuf_tensor("ingts", [128, NCH], F32))
        inbt2 = e(nc.sbuf_tensor("inbt2s", [128, 2 * NCH], BF16))
        d1bbf = e(nc.sbuf_tensor("d1bbf", [1, TH_SL], BF16))
        d2brow = e(nc.sbuf_tensor("d2brow", [1, D], BF16))
        aggb2 = e(nc.sbuf_tensor("aggb2s", [2, AG_SL], F32))
        bcb2 = e(nc.sbuf_tensor("bcb2s", [2, AG_SL], F32))
        ebd1 = e(nc.sbuf_tensor("ebd1", [1, TH_SL], BF16))
        xtn = e(nc.sbuf_tensor("xtn", [128, 48], BF16))
        stats = e(nc.sbuf_tensor("stats", [128, 64], F32))
        srow = e(nc.sbuf_tensor("srow", [2, 16], F32))
        wjit2 = e(nc.sbuf_tensor("wjit2", [128, 4 * AG_SL], BF16))
        sqacc = e(nc.sbuf_tensor("sqacc", [128, 8], F32))
        PS = e(nc.psum_tensor("PS", [128, 4096], F32))

        semnames = ["sT", "sV", "sS", "sP", "sCC", "sSM", "sXA", "sXB", "sW0", "sW1", "sW2", "sW3", "sWR", "sWR2", "sD1", "sDC", "sD2", "sGW", "sGB", "sGV", "sGQ", "sXC"]
        sems = {n: e(nc.semaphore(n)) for n in semnames}
        block = e(nc.Block())
        S = Sched()

        def bank(b, off=0, n=512):
            return PS[:, 512 * b + off: 512 * b + off + n]

        def bankbf(b, n=128):
            return PS.ap()[:, 512 * b: 512 * b + (n + 1) // 2].bitcast(BF16)

        def bankrow(b, rows=2, n=512):
            return PS[0:rows, 512 * b: 512 * b + n]

        thought = trK.ap()[:, 10240:14336].bitcast(F32)[0:2, :]
        gamv = trK.ap()[:, 0:4096].bitcast(F32)
        betv = trK.ap()[:, 4096:8192].bitcast(F32)
        tb8 = chsc.ap()[0:2, 0:2048]
        urow = cring.ap()[0:2, D:2 * D]
        trow = chsc.ap()[:, 2048:4096].bitcast(F32)[0:2, :]
        xnbf = chsc.ap()[0:2, 0:2048]
        bcbc = chsc.ap().bitcast(F32)
        scr2 = trQ.ap()[:, 0:4096].bitcast(F32)
        scr2b = trQ.ap()[:, 4096:8192].bitcast(F32)
        gbrow = trK.ap()[0:1, 8192:10240]
        agslot = [wjit2.ap()[:, 0:4 * AG_SL], chsc.ap()[:, 2048:2048 + 4 * AG_SL]]

        x0_tiles = x0h.rearrange("(n p) d -> n p d", p=128)
        xout_tiles = xouth.rearrange("(n p) d -> n p d", p=128)
        x1s_tiles = x1scr.rearrange("(n p) d -> n p d", p=128)

        for dst, src in [(iden[:], idenh), (maskc[:], maskh), (ones2[:], ones2h),
                         (ones128[:], ones128h), (sel2[:], sel2h), (pngt[:], pngth),
                         (pnbt2[:], pnbt2h), (ingt[:], ingth), (inbt2[:], inbt2h),
                         (d1bbf[:], d1bh), (d2brow[:], d2bh), (aggb2[:], aggbh),
                         (bcb2[:], bcbh)]:
            S.op("gpsimd", lambda g, d=dst, s=src: g.dma_start(out=d, in_=s), sem="sSM", inc=16)
        sm_done = ("sSM", S.mark("sSM"))

        def ln_stats(eps, su, sq, m2, rs, nb, rows=stats):
            S.op("vector", lambda v, a=su: v.tensor_scalar(
                rows[:, a:a + 1], rows[:, a:a + 1], 1.0 / D, 0.0, op0=ALU.mult, op1=ALU.add), sem="sV", inc=1)
            S.op("vector", lambda v, a=sq, ee=eps: v.tensor_scalar(
                rows[:, a:a + 1], rows[:, a:a + 1], 1.0 / D, ee, op0=ALU.mult, op1=ALU.add), sem="sV", inc=1)
            S.wait("vector", "sV", S.mark("sV"))
            S.op("vector", lambda v, a=su, b=m2: v.tensor_tensor(
                rows[:, b:b + 1], rows[:, a:a + 1], rows[:, a:a + 1], op=ALU.mult), sem="sV", inc=1)
            S.wait("vector", "sV", S.mark("sV"))
            S.op("vector", lambda v, a=sq, b=m2: v.tensor_tensor(
                rows[:, a:a + 1], rows[:, a:a + 1], rows[:, b:b + 1], op=ALU.subtract), sem="sV", inc=1)
            S.wait("scalar", "sV", S.mark("sV"))
            S.op("scalar", lambda s, a=sq, b=rs: s.activation(
                rows[:, b:b + 1], rows[:, a:a + 1], AF.Ln), sem="sS", inc=1)
            S.wait("scalar", "sS", S.mark("sS"))
            S.op("scalar", lambda s, b=rs: s.activation(
                rows[:, b:b + 1], rows[:, b:b + 1], AF.Exp, scale=-0.5), sem="sS", inc=1)
            S.wait("vector", "sS", S.mark("sS"))
            S.op("vector", lambda v, a=su, b=rs, c=nb: v.tensor_scalar(
                rows[:, c:c + 1], rows[:, a:a + 1], rows[:, b:b + 1], -1.0,
                op0=ALU.mult, op1=ALU.mult), sem="sV", inc=1)
            return S.mark("sV")


        def finish_passthrough(wait_list):
            for sem_, val_ in wait_list:
                S.wait("sync", sem_, val_)
            for n in range(NT):
                S.wait("sync", "sDC", S.mark("sDC"))
                S.op("sync", lambda sy, t_=n: sy.dma_start(
                    out=xring[:, 0:D], in_=x0_tiles[t_]), sem="sDC", inc=16)
                S.wait("sync", "sDC", S.mark("sDC"))
                S.op("sync", lambda sy, t_=n: sy.dma_start(
                    out=xout_tiles[t_], in_=xring[:, 0:D]), sem="sDC", inc=16)
            S.wait("sync", "sDC", S.mark("sDC"))
            S.wait("gpsimd", "sDC", S.mark("sDC"))

        evac_ph0 = []
        cast_ph0 = []
        trdone_ph0 = []
        for idx in range(NT):
            tt = idx
            slot = idx % 2
            xsem = "sXA" if slot == 0 else "sXB"
            if idx >= 2:
                S.wait("sync", "sS", cast_ph0[idx - 2])
            S.op("sync", lambda sy, sl=slot, t=tt: sy.dma_start(
                out=xring[:, D * sl: D * (sl + 1)], in_=x0_tiles[t]), sem=xsem, inc=16)
            xm = (xsem, S.mark(xsem))
            xt = xring[:, D * slot: D * (slot + 1)]
            S.waitm("vector", xm)
            S.op("vector", lambda v, x=xt: v.reduce_sum(stats[:, 0:1], x, axis=AX.X), sem="sV", inc=1)
            sum_m = S.mark("sV")
            S.waitm("scalar", xm)
            if idx >= 1:
                S.wait("scalar", "sV", sq_consumed)
            S.op("scalar", lambda s, x=xt: s.activation(
                cring[:, 0:D], x, AF.Square, accum_out=stats[:, 1:2]), sem="sS", inc=1)
            sq_m = S.mark("sS")
            S.wait("vector", "sS", sq_m)
            S.wait("vector", "sV", sum_m)
            nb_m = ln_stats(EPS0, 0, 1, 2, 3, 4)
            sq_consumed = nb_m
            abf = afbf[:, slot * D:(slot + 1) * D]
            S.wait("scalar", "sV", nb_m)
            if idx >= 2:
                S.wait("scalar", "sT", trdone_ph0[idx - 2])
            S.op("scalar", lambda s, x=xt, dd=abf: s.activation(
                dd, x, AF.Identity, bias=stats[:, 4:5], scale=stats[:, 3:4]), sem="sS", inc=1)
            cast_ph0.append(S.mark("sS"))
            S.wait("tensor", "sS", cast_ph0[-1])
            if idx == 0:
                S.waitm("tensor", sm_done)
            for i in range(NCH):
                b = 6 + (i % 2)
                k = idx * NCH + i
                if k >= 2:
                    S.wait("tensor", "sV", evac_ph0[k - 2])
                S.op("tensor", lambda t, bb=b, aa=abf[:, i * 128:(i + 1) * 128]:
                     t.transpose(bankbf(bb), aa, iden[:]), sem="sT", inc=1)
                tm = S.mark("sT")
                S.wait("vector", "sT", tm)
                dst = trA[:, i * T + tt * 128: i * T + (tt + 1) * 128]
                S.op("vector", lambda v, bb=b, dd=dst: v.tensor_copy(dd, bankbf(bb)), sem="sV", inc=1)
                evac_ph0.append(S.mark("sV"))
            trdone_ph0.append(S.mark("sT"))
        atr_done = S.mark("sV")
        if stage == 0.1:
            finish_passthrough([("sV", atr_done), ("sS", S.mark("sS")), ("sT", S.mark("sT"))])
        if stage < 0.2 and stage != 1 and stage != 3:
            qkv_skip = True
        whs = [wqh, wkh, wvh]
        trX = [trQ, trK, trV]
        NSLOT = 4
        blk_ready, blk_used, evacm = [], [], []
        for j in range(NCH if stage >= 0.2 else 0):
            for w in range(3):
                k = j * 3 + w
                slot = k % NSLOT
                if k >= NSLOT:
                    S.wait("gpsimd", "sT", blk_used[k - NSLOT])
                src = whs[w].rearrange("(i p) m -> p i m", p=128)[:, :, j * 128:(j + 1) * 128]
                dst = wring.ap().rearrange("p (s i m) -> p s i m", s=NSLOT, i=NCH)[:, slot]
                wsem = f"sW{slot}"
                S.op("gpsimd", lambda g, d=dst, s=src: g.dma_start(out=d, in_=s), sem=wsem, inc=16)
                blk_ready.append((wsem, S.mark(wsem)))
                pair = k % 3
                b0, b1 = 2 * pair, 2 * pair + 1
                S.waitm("tensor", blk_ready[-1])
                if k == 0:
                    S.wait("tensor", "sV", atr_done)
                if k >= 3:
                    sem_, val_ = evacm[k - 3]
                    S.wait("tensor", sem_, val_)
                for i in range(NCH):
                    lhs = wring[:, slot * NCH * 128 + i * 128: slot * NCH * 128 + (i + 1) * 128]
                    S.op("tensor", lambda t, l=lhs, r=trA[:, i * T: i * T + 512], bb=b0,
                         st=(i == 0), sp=(i == NCH - 1): t.matmul(bank(bb), l, r, start=st, stop=sp))
                    last = (i == NCH - 1)
                    S.op("tensor", lambda t, l=lhs, r=trA[:, i * T + 512: i * T + 1024], bb=b1,
                         st=(i == 0), sp=(i == NCH - 1): t.matmul(bank(bb), l, r, start=st, stop=sp),
                         sem="sT" if last else None, inc=1 if last else 0)
                pe_m = S.mark("sT")
                blk_used.append(pe_m)
                dst2 = trX[w].ap().rearrange("p (t h) -> p h t", h=NCH)[:, j, :]
                if k % 2 == 0:
                    S.wait("vector", "sT", pe_m)
                    S.op("vector", lambda v, dd=dst2[:, 0:512], bb=b0: v.tensor_copy(dd, bank(bb)), sem="sV", inc=1)
                    S.op("vector", lambda v, dd=dst2[:, 512:1024], bb=b1: v.tensor_copy(dd, bank(bb)), sem="sV", inc=1)
                    evacm.append(("sV", S.mark("sV")))
                else:
                    S.wait("scalar", "sT", pe_m)
                    S.op("scalar", lambda s, dd=dst2[:, 0:512], bb=b0: s.copy(dd, bank(bb)), sem="sS", inc=1)
                    S.op("scalar", lambda s, dd=dst2[:, 512:1024], bb=b1: s.copy(dd, bank(bb)), sem="sS", inc=1)
                    evacm.append(("sS", S.mark("sS")))
        qkv_evV, qkv_evS = S.mark("sV"), S.mark("sS")
        if stage == 0.2:
            finish_passthrough([("sV", qkv_evV), ("sS", qkv_evS), ("sT", S.mark("sT"))])

        NG = T // 8
        scores_pe, eins_done = [], []
        sbd_m, vp_m = [], []
        if stage >= 0.3:
            S.wait("tensor", "sV", qkv_evV)
            S.wait("tensor", "sS", qkv_evS)
        pend = None
        for g in range(NG if stage >= 0.3 else 0):
            c0 = g * 128
            bS, bV, bO = (g % 2), 2 + (g % 2), 4 + (g % 2)
            sbd = afbf.ap()[:, (g % 2) * 128: (g % 2) * 128 + 128]
            vp = afbf.ap()[:, 256 + (g % 2) * 128: 256 + (g % 2) * 128 + 128]
            if g >= 2:
                S.wait("tensor", "sV", sbd_m[g - 2])
                sem_, val_ = vp_m[g - 2]
                S.wait("tensor", sem_, val_)
            if g >= 3:
                S.wait("tensor", "sS", eins_done[g - 3])
            S.op("tensor", lambda t, c=c0, bb=bS: t.matmul(
                bank(bb, 0, 128), trK[:, c: c + 128], trQ[:, c: c + 128],
                start=True, stop=True), sem="sT", inc=1)
            scores_pe.append(S.mark("sT"))
            S.op("tensor", lambda t, c=c0, bb=bV: t.transpose(
                bankbf(bb), trV[:, c: c + 128], iden[:]), sem="sT", inc=1)
            vpt = S.mark("sT")
            S.wait("vector", "sT", scores_pe[-1])
            S.op("vector", lambda v, bb=bS, dd=sbd: v.tensor_tensor(
                dd, bank(bb, 0, 128), maskc[:], op=ALU.mult), sem="sV", inc=1)
            sbd_m.append(S.mark("sV"))
            S.wait("scalar", "sT", vpt)
            S.op("scalar", lambda s, bb=bV, dd=vp: s.copy(dd, bankbf(bb)), sem="sS", inc=1)
            vp_m.append(("sS", S.mark("sS")))
            if pend is not None:
                pg, pvp, psbd, pbO = pend
                S.wait("tensor", "sV", sbd_m[pg])
                S.waitm("tensor", vp_m[pg])
                S.op("tensor", lambda t, vv=pvp, ss_=psbd, bb=pbO: t.matmul(
                    bank(bb, 0, 128), vv, ss_, start=True, stop=True), sem="sT", inc=1)
                om = S.mark("sT")
                S.wait("scalar", "sT", om)
                S.op("scalar", lambda s, bb=pbO, c=pg * 128: s.copy(trA[:, c: c + 128], bank(bb, 0, 128)),
                     sem="sS", inc=1)
                eins_done.append(S.mark("sS"))
            pend = (g, vp, sbd, bO)
        if pend is not None:
            pg, pvp, psbd, pbO = pend
            S.wait("tensor", "sV", sbd_m[pg])
            S.waitm("tensor", vp_m[pg])
            S.op("tensor", lambda t, vv=pvp, ss_=psbd, bb=pbO: t.matmul(
                bank(bb, 0, 128), vv, ss_, start=True, stop=True), sem="sT", inc=1)
            om = S.mark("sT")
            S.wait("scalar", "sT", om)
            S.op("scalar", lambda s, bb=pbO, c=pg * 128: s.copy(trA[:, c: c + 128], bank(bb, 0, 128)),
                 sem="sS", inc=1)
            eins_done.append(S.mark("sS"))

        if stage == 0.3:
            finish_passthrough([("sS", S.mark("sS")), ("sT", S.mark("sT")), ("sV", S.mark("sV"))])
        def wo_dst(i):
            if i < 7:
                return trQ.ap()[:, i * 2048:(i + 1) * 2048]
            if i < 12:
                return trK.ap()[:, (i - 7) * 2048:(i - 6) * 2048]
            return wring.ap()[:, (i - 12) * 2048:(i - 11) * 2048]

        wo_ready = []
        for i in range(NCH if stage >= 0.4 else 0):
            if i < 7:
                S.wait("gpsimd", "sT", scores_pe[16 * i + 15])
            elif i < 12:
                S.wait("gpsimd", "sT", scores_pe[16 * (i - 7) + 15])
            else:
                S.wait("gpsimd", "sT", blk_used[-1])
            src_ = woh.rearrange("(i p) m -> p i m", p=128)[:, i, :]
            wsem_ = "sWR" if i < 8 else "sWR2"
            S.op("gpsimd", lambda g, d=wo_dst(i), s=src_: g.dma_start(out=d, in_=s), sem=wsem_, inc=16)
            wo_ready.append(S.mark(wsem_))
        wo_res = ("sWR", S.mark("sWR"))
        wo_res2 = ("sWR2", S.mark("sWR2"))

        d1src = d1wh.rearrange("(i p) m -> p i m", p=128)
        if stage >= 0.4:
            S.wait("gpsimd", "sT", S.mark("sT"))
        for i4 in range(4 if stage >= 0.4 else 0):
            S.op("gpsimd", lambda g, d=trV.ap().rearrange("p (i m) -> p i m", i=NCH)[:, i4 * 4:(i4 + 1) * 4],
                 s=d1src[:, i4 * 4:(i4 + 1) * 4]: g.dma_start(out=d, in_=s), sem="sD1", inc=16)
        d1_in = ("sD1", S.mark("sD1"))

        x1_built, x1_spilled, x0_ld = [], [], []
        fstat_nb = [0] * NT
        ftr_done = [0] * NT
        wo_banksfree = []
        if stage == 0.4:
            finish_passthrough([("sWR", S.mark("sWR")), ("sD1", S.mark("sD1")), ("sT", S.mark("sT"))])
        fb_marks = [0] * NT
        fsq_l = [0] * NT
        tmark = [0] * NT
        red_done_l = [0] * NT
        stg_done_l = [0] * NT
        red_done = 0
        stg_done = 0

        def emit_ftr(m):
            ttm_ = TORD[m]
            s0m = (m * 4) % 6
            tbanks = [(s0m + 2) % 6, (s0m + 3) % 6]
            fbm_ = fb_marks[m]
            fbuf = afbf[:, (m % 2) * D: (m % 2 + 1) * D]
            S.wait("tensor", "sS", fbm_)
            S.wait("tensor", "sV", red_done_l[m])
            ev = []
            for i in range(NCH):
                b_ = tbanks[i % 2]
                if i >= 2:
                    S.wait("tensor", "sV", ev[i - 2])
                S.op("tensor", lambda t, bb=b_, aa=fbuf[:, i * 128:(i + 1) * 128]:
                     t.transpose(bankbf(bb), aa, iden[:]), sem="sT", inc=1)
                tm_ = S.mark("sT")
                S.wait("vector", "sT", tm_)
                dst = trA[:, ttm_ * 2048 + i * 128: ttm_ * 2048 + (i + 1) * 128]
                S.op("vector", lambda v, bb=b_, dd=dst: v.tensor_copy(dd, bankbf(bb)), sem="sV", inc=1)
                ev.append(S.mark("sV"))
            tmark[m] = S.mark("sT")
            ftr_done[TORD[m]] = S.mark("sV")

        for n, tt in enumerate(TORD if stage >= 0.5 else []):
            slot = n % 2
            xsem = "sXA" if slot == 0 else "sXB"
            pb = 8 * (n % 2)
            if n >= 2:
                S.waitm("sync", x1_spilled[n - 2])
                S.wait("sync", "sS", fb_marks[n - 2])
            else:
                S.wait("sync", "sS", cast_ph0[6 + n])
            S.op("sync", lambda sy, sl=slot, t=tt: sy.dma_start(
                out=xring[:, D * sl: D * (sl + 1)], in_=x0_tiles[t]), sem=xsem, inc=16)
            x0_ld.append((xsem, S.mark(xsem)))

            s0 = (n * 4) % 6
            banks = [(s0 + q) % 6 for q in range(4)]
            if n == 0:
                S.waitm("tensor", wo_res)
            S.wait("tensor", "sS", eins_done[16 * tt + 15])
            if n >= 1:
                S.wait("tensor", "sV", red_done_l[n - 1])
            if n >= 2:
                S.wait("tensor", "sV", ftr_done[TORD[n - 2]])
            for i in range(NCH):
                if n == 0 and i == 8:
                    S.waitm("tensor", wo_res2)
                lhs = trA.ap().rearrange("p (t h) -> p h t", h=NCH)[:, i, tt * 128:(tt + 1) * 128]
                wsrc = wo_dst(i)
                for q in range(4):
                    last = (i == NCH - 1 and q == 3)
                    S.op("tensor", lambda t, l=lhs, r=wsrc[:, q * 512:(q + 1) * 512], bb=banks[q],
                         st=(i == 0), sp=(i == NCH - 1): t.matmul(bank(bb), l, r, start=st, stop=sp),
                         sem="sT" if last else None, inc=1 if last else 0)
            wo_pe = S.mark("sT")
            if n >= 1 and stage >= 1:
                emit_ftr(n - 1)
            S.wait("vector", "sT", wo_pe)
            if n >= 1:
                S.wait("vector", "sS", fsq_l[n - 1])
            for q in range(4):
                S.op("vector", lambda v, bb=banks[q], q_=q: v.tensor_copy(
                    cring[:, q_ * 512:(q_ + 1) * 512], bank(bb)), sem="sV", inc=1)
            red_done = S.mark("sV")
            red_done_l[n] = red_done
            S.wait("vector", "sV", red_done)
            S.op("vector", lambda v, a=pb: v.reduce_sum(stats[:, a:a + 1], cring[:, 0:D], axis=AX.X), sem="sV", inc=1)
            S.wait("scalar", "sV", red_done)
            if n >= 1:
                S.wait("scalar", "sV", x1_built[-1])
            S.op("scalar", lambda s, a=pb: s.activation(
                cring[:, D:2 * D], cring[:, 0:D], AF.Square, accum_out=stats[:, a + 1:a + 2]), sem="sS", inc=1)
            S.wait("vector", "sS", S.mark("sS"))
            S.wait("vector", "sV", S.mark("sV"))
            nb_m = ln_stats(EPS0, pb, pb + 1, pb + 2, pb + 3, pb + 4)
            S.wait("scalar", "sV", nb_m)
            S.op("scalar", lambda s, a=pb: s.activation(
                cring[:, 0:D], cring[:, 0:D], AF.Identity,
                bias=stats[:, a + 4:a + 5], scale=stats[:, a + 3:a + 4]), sem="sS", inc=1)
            stg_done = S.mark("sS")
            stg_done_l[n] = stg_done
            S.wait("vector", "sS", stg_done)
            S.waitm("vector", x0_ld[n])
            xt = xring[:, D * slot: D * (slot + 1)]
            S.op("vector", lambda v, x=xt: v.tensor_tensor(x, x, cring[:, 0:D], op=ALU.add), sem="sV", inc=1)
            x1_built.append(S.mark("sV"))
            x1t = xt
            S.wait("vector", "sV", x1_built[-1])
            S.op("vector", lambda v, x=x1t, a=pb: v.reduce_sum(stats[:, a:a + 1], x, axis=AX.X), sem="sV", inc=1)
            S.wait("scalar", "sV", x1_built[-1])
            S.op("scalar", lambda s, x=x1t, a=pb: s.activation(
                cring[:, 0:D], x, AF.Square, accum_out=stats[:, a + 1:a + 2]), sem="sS", inc=1)
            fsq_l[n] = S.mark("sS")
            S.wait("vector", "sS", S.mark("sS"))
            S.wait("vector", "sV", S.mark("sV"))
            nbf = ln_stats(EPS0, pb, pb + 1, pb + 2, 32 + tt, 40 + tt)
            fstat_nb[tt] = nbf
            fb = afbf[:, (n % 2) * D: (n % 2 + 1) * D]
            S.wait("scalar", "sV", nbf)
            if n >= 2 and stage >= 1:
                S.wait("scalar", "sT", tmark[n - 2])
            S.op("scalar", lambda s, x=x1t, dd=fb, t_=tt: s.activation(
                dd, x, AF.Identity, bias=stats[:, 40 + t_:41 + t_], scale=stats[:, 32 + t_:33 + t_]),
                sem="sS", inc=1)
            fb_marks[n] = S.mark("sS")
            if stage < 1:
                ftr_done[tt] = S.mark("sV")
            S.wait("sync", "sV", x1_built[-1])
            S.op("sync", lambda sy, sl=slot, t_=tt: sy.dma_start(
                out=x1s_tiles[t_], in_=xring[:, D * sl: D * (sl + 1)]), sem=xsem, inc=16)
            x1_spilled.append((xsem, S.mark(xsem)))
        if stage >= 1:
            emit_ftr(NT - 1)
        wo_all_pe = S.mark("sT")
        if stage in (0.5, 0.51, 0.52, 0.53, 0.55, 0.57, 0.58, 0.6):
            finish_passthrough([("sT", wo_all_pe), ("sS", S.mark("sS")), ("sV", S.mark("sV")),
                                ("sXA", S.mark("sXA")), ("sXB", S.mark("sXB"))])
        if stage < 1:
            pass
        elif stage == 1:
            S.wait("sync", "sXA", S.mark("sXA"))
            S.wait("sync", "sXB", S.mark("sXB"))
            for n in range(NT):
                S.wait("sync", "sDC", S.mark("sDC"))
                S.op("sync", lambda sy, t_=n: sy.dma_start(
                    out=xring[:, 0:D], in_=x1s_tiles[t_]), sem="sDC", inc=16)
                S.wait("sync", "sDC", S.mark("sDC"))
                S.op("sync", lambda sy, t_=n: sy.dma_start(
                    out=xout_tiles[t_], in_=xring[:, 0:D]), sem="sDC", inc=16)
            S.wait("sync", "sDC", S.mark("sDC"))
            S.wait("gpsimd", "sDC", S.mark("sDC"))
        else:
            wg_src = wgh.rearrange("(i p) m -> p i m", p=128)
            S.wait("gpsimd", "sT", wo_all_pe)
            d2src = d2wh.rearrange("(c p) m -> p c m", p=128)
            for c4 in range(4):
                S.op("gpsimd", lambda g, d=trQ.ap().rearrange("p (c m) -> p c m", c=8)[:, c4 * 2:(c4 + 1) * 2],
                     s=d2src[:, c4 * 2:(c4 + 1) * 2]: g.dma_start(out=d, in_=s), sem="sD2", inc=16)
            d2s_in = S.mark("sD2")
            def gw_dst(c):
                if c < 8:
                    return trV.ap()[:, c * 2048:(c + 1) * 2048]
                if c < 12:
                    return wring.ap()[:, (c - 8) * 2048:(c - 7) * 2048]
                if c == 12:
                    return trK.ap()[:, 14336:16384]
                if c < 15:
                    return afbf.ap()[:, (c - 13) * 2048:(c - 12) * 2048]
                return trQ.ap()[:, 12288:14336]
            S.wait("gpsimd", "sV", max(ftr_done))
            for c in range(8, 15):
                S.op("gpsimd", lambda g, d=gw_dst(c), s=wg_src[:, c, :]: g.dma_start(out=d, in_=s),
                     sem="sGW", inc=16)
            gw_early = S.mark("sGW")
            S.waitm("tensor", d1_in)
            for i in range(NCH):
                for hh in range(2):
                    S.op("tensor", lambda t, i_=i, h_=hh: t.matmul(
                        bankrow(6 + h_, 2, 512), pnbt2[:, 2 * i_:2 * i_ + 2],
                        trV[:, i_ * 1024 + h_ * 512: i_ * 1024 + (h_ + 1) * 512],
                        start=(i_ == 0), stop=False), sem=None, inc=0)
            for hh in range(2):
                S.op("tensor", lambda t, h_=hh: t.matmul(
                    bankrow(6 + h_, 2, 512), ones2[:], d1bbf[0:1, h_ * 512:(h_ + 1) * 512],
                    start=False, stop=True), sem="sT" if hh == 1 else None, inc=1 if hh == 1 else 0)
            ebmm = S.mark("sT")
            S.wait("scalar", "sT", ebmm)
            S.op("scalar", lambda s: s.copy(ebd1[0:1, 0:512], bankrow(6, 1, 512)), sem="sS", inc=1)
            S.op("scalar", lambda s: s.copy(ebd1[0:1, 512:1024], bankrow(7, 1, 512)), sem="sS", inc=1)
            ebd1_m = S.mark("sS")
            S.wait("vector", "sT", ebmm)
            for i in range(NCH):
                S.op("vector", lambda v, i_=i: v.tensor_scalar(
                    trV[:, i_ * 1024:(i_ + 1) * 1024], trV[:, i_ * 1024:(i_ + 1) * 1024],
                    pngt[:, i_:i_ + 1], 0.0, op0=ALU.mult, op1=ALU.add), sem="sV", inc=1)
            d1s_ready = S.mark("sV")

            S.wait("vector", "sS", S.mark("sS"))
            S.op("vector", lambda v: v.memset(urow[:], 0.0), sem="sV", inc=1)
            um0 = S.mark("sV")
            S.wait("sync", "sV", um0)
            S.wait("sync", "sXA", S.mark("sXA"))
            S.wait("sync", "sXB", S.mark("sXB"))
            S.op("sync", lambda sy: sy.dma_start(out=urow[0:1, :], in_=x1scr[1023:1024, :]),
                 sem="sDC", inc=16)
            fl_m = ("sDC", S.mark("sDC"))

            def row_ln(src_row, dst_row, eps, sc=0):
                S.op("vector", lambda v: v.reduce_sum(srow[0:2, sc:sc + 1], src_row, axis=AX.X),
                     sem="sV", inc=1)
                sm = S.mark("sV")
                S.op("scalar", lambda s: s.activation(
                    cring[0:2, 0:D], src_row, AF.Square,
                    accum_out=srow[0:2, sc + 1:sc + 2]), sem="sS", inc=1)
                qm = S.mark("sS")
                S.wait("vector", "sV", sm)
                S.wait("vector", "sS", qm)
                S.op("vector", lambda v: v.tensor_scalar(
                    srow[0:2, sc:sc + 1], srow[0:2, sc:sc + 1], 1.0 / D, 0.0, op0=ALU.mult, op1=ALU.add), sem="sV", inc=1)
                S.op("vector", lambda v, ee=eps: v.tensor_scalar(
                    srow[0:2, sc + 1:sc + 2], srow[0:2, sc + 1:sc + 2], 1.0 / D, ee, op0=ALU.mult, op1=ALU.add), sem="sV", inc=1)
                S.wait("vector", "sV", S.mark("sV"))
                S.op("vector", lambda v: v.tensor_tensor(
                    srow[0:2, sc + 2:sc + 3], srow[0:2, sc:sc + 1], srow[0:2, sc:sc + 1], op=ALU.mult), sem="sV", inc=1)
                S.wait("vector", "sV", S.mark("sV"))
                S.op("vector", lambda v: v.tensor_tensor(
                    srow[0:2, sc + 1:sc + 2], srow[0:2, sc + 1:sc + 2], srow[0:2, sc + 2:sc + 3], op=ALU.subtract), sem="sV", inc=1)
                S.wait("scalar", "sV", S.mark("sV"))
                S.op("scalar", lambda s: s.activation(
                    srow[0:2, sc + 3:sc + 4], srow[0:2, sc + 1:sc + 2], AF.Ln), sem="sS", inc=1)
                S.wait("scalar", "sS", S.mark("sS"))
                S.op("scalar", lambda s: s.activation(
                    srow[0:2, sc + 3:sc + 4], srow[0:2, sc + 3:sc + 4], AF.Exp, scale=-0.5), sem="sS", inc=1)
                S.wait("vector", "sS", S.mark("sS"))
                S.op("vector", lambda v: v.tensor_scalar(
                    srow[0:2, sc + 4:sc + 5], srow[0:2, sc:sc + 1], srow[0:2, sc + 3:sc + 4], -1.0,
                    op0=ALU.mult, op1=ALU.mult), sem="sV", inc=1)
                S.wait("scalar", "sV", S.mark("sV"))
                S.op("scalar", lambda s: s.activation(
                    dst_row, src_row, AF.Identity, bias=srow[0:2, sc + 4:sc + 5],
                    scale=srow[0:2, sc + 3:sc + 4]), sem="sS", inc=1)
                return S.mark("sS")

            S.waitm("vector", fl_m)
            S.waitm("scalar", fl_m)
            m1 = row_ln(urow[:], urow[:], EPS0, sc=0)
            S.wait("vector", "sS", m1)
            S.wait("scalar", "sS", m1)
            m2 = row_ln(urow[:], cring[0:2, 0:D], EPSF, sc=5)
            S.wait("sync", "sS", m2)
            S.op("sync", lambda sy: sy.dma_start(out=ag1_in, in_=cring[0:1, 0:D]), sem="sDC", inc=16)
            S.wait("gpsimd", "sDC", S.mark("sDC"))
            S.op("gpsimd", lambda g: g.collective_compute(
                "AllGather", ALU.bypass, replica_groups=[list(range(NCORES))],
                ins=[ag1_in], outs=[ag1_out]), sem="sCC", inc=1)
            ag1_m = S.mark("sCC")
            S.wait("sync", "sCC", ag1_m)
            S.op("sync", lambda sy: sy.dma_start(out=cring[0:1, 0:D], in_=ag1_out[3:4, :]), sem="sDC", inc=16)
            S.op("sync", lambda sy: sy.dma_start(out=cring[1:2, 0:D], in_=ag1_out[7:8, :]), sem="sDC", inc=16)
            xn_m = ("sDC", S.mark("sDC"))
            S.waitm("vector", xn_m)
            S.op("vector", lambda v: v.tensor_copy(xnbf, cring[0:2, 0:D]), sem="sV", inc=1)
            xnbf_m = S.mark("sV")
            S.wait("tensor", "sV", xnbf_m)
            xe = []
            for i in range(NCH):
                bb = 6 + (i % 2)
                if i >= 2:
                    S.wait("tensor", "sV", xe[i - 2])
                S.op("tensor", lambda t, b_=bb, aa=xnbf[:, i * 128:(i + 1) * 128]:
                     t.transpose(PS.ap()[:, 512 * b_:512 * b_ + 1].bitcast(BF16), aa, iden[0:2, 0:2]),
                     sem="sT", inc=1)
                tm = S.mark("sT")
                S.wait("vector", "sT", tm)
                S.op("vector", lambda v, b_=bb, i_=i: v.tensor_copy(
                    xtn[:, 2 * i_:2 * i_ + 2], PS.ap()[:, 512 * b_:512 * b_ + 1].bitcast(BF16)), sem="sV", inc=1)
                xe.append(S.mark("sV"))
            xtn_m = S.mark("sV")
            agg_src = aggwh.rearrange("(i p) m -> p i m", p=128)
            agg_used = [0] * 4
            for b4 in range(4):
                sl = b4 % 2
                wsem = f"sW{sl}"
                if b4 >= 2:
                    S.wait("gpsimd", "sT", agg_used[b4 - 2])
                S.op("gpsimd", lambda g, d=agslot[sl].rearrange("p (c m) -> p c m", c=4),
                     s=agg_src[:, b4 * 4:(b4 + 1) * 4]: g.dma_start(out=d, in_=s), sem=wsem, inc=16)
                blkm = (wsem, S.mark(wsem))
                S.waitm("tensor", blkm)
                if b4 == 0:
                    S.wait("tensor", "sV", xtn_m)
                S.waitm("vector", blkm)
                for c in range(4):
                    i = b4 * 4 + c
                    chunk = agslot[sl][:, c * AG_SL:(c + 1) * AG_SL]
                    S.op("tensor", lambda t, i_=i, ch=chunk: t.matmul(
                        bankrow(6, 2, AG_SL), inbt2[:, 2 * i_:2 * i_ + 2], ch,
                        start=(i_ == 0), stop=(i_ == NCH - 1)), sem="sT", inc=1)
                    ebm = S.mark("sT")
                    S.wait("vector", "sT", ebm)
                    S.op("vector", lambda v, i_=i, ch=chunk: v.tensor_scalar(
                        ch, ch, ingt[:, i_:i_ + 1], 0.0, op0=ALU.mult, op1=ALU.add), sem="sV", inc=1)
                    fm = S.mark("sV")
                    S.wait("tensor", "sV", fm)
                    S.op("tensor", lambda t, i_=i, ch=chunk: t.matmul(
                        bankrow(7, 2, AG_SL), xtn[:, 2 * i_:2 * i_ + 2], ch,
                        start=(i_ == 0), stop=(i_ == NCH - 1)), sem="sT", inc=1)
                agg_used[b4] = S.mark("sT")
            agg_pe = S.mark("sT")
            S.wait("vector", "sT", agg_pe)
            S.op("vector", lambda v: v.tensor_copy(cring[0:2, 0:AG_SL], bankrow(7, 2, AG_SL)), sem="sV", inc=1)
            S.wait("vector", "sV", S.mark("sV"))
            S.op("vector", lambda v: v.tensor_tensor(
                cring[0:2, 0:AG_SL], cring[0:2, 0:AG_SL], bankrow(6, 2, AG_SL), op=ALU.add), sem="sV", inc=1)
            S.wait("vector", "sV", S.mark("sV"))
            S.op("vector", lambda v: v.tensor_tensor(
                cring[0:2, 0:AG_SL], cring[0:2, 0:AG_SL], aggb2[:], op=ALU.add), sem="sV", inc=1)
            th0_m = S.mark("sV")
            S.wait("sync", "sV", th0_m)
            S.op("sync", lambda sy: sy.dma_start(out=agg_in, in_=cring[0:2, 0:AG_SL]), sem="sDC", inc=16)
            S.wait("gpsimd", "sDC", S.mark("sDC"))
            S.op("gpsimd", lambda g: g.collective_compute(
                "AllGather", ALU.bypass, replica_groups=[list(range(NCORES))],
                ins=[agg_in], outs=[agg_out]), sem="sCC", inc=1)
            S.wait("sync", "sCC", S.mark("sCC"))
            S.op("sync", lambda sy: sy.dma_start(
                out=thought.rearrange("b (r s) -> b r s", r=NCORES),
                in_=agg_out.rearrange("(r b) s -> b r s", b=2)), sem="sDC", inc=16)
            th_m = ("sDC", S.mark("sDC"))

            d1m_l = [0] * 4
            d2m_l = [0] * 4
            prev = th_m
            for st in range(4):
                S.waitm("vector", prev)
                S.waitm("scalar", prev)
                mln = row_ln(thought, tb8, EPSF, sc=8)
                S.wait("tensor", "sS", mln)
                xe2 = []
                for i in range(NCH):
                    bb = 6 + (i % 2)
                    if i >= 2:
                        S.wait("tensor", "sV", xe2[i - 2])
                    S.op("tensor", lambda t, b_=bb, aa=tb8[:, i * 128:(i + 1) * 128]:
                         t.transpose(PS.ap()[:, 512 * b_:512 * b_ + 1].bitcast(BF16), aa, iden[0:2, 0:2]),
                         sem="sT", inc=1)
                    tm = S.mark("sT")
                    S.wait("vector", "sT", tm)
                    S.op("vector", lambda v, b_=bb, i_=i: v.tensor_copy(
                        xtn[:, 2 * i_:2 * i_ + 2], PS.ap()[:, 512 * b_:512 * b_ + 1].bitcast(BF16)), sem="sV", inc=1)
                    xe2.append(S.mark("sV"))
                ttm = S.mark("sV")
                S.wait("tensor", "sV", ttm)
                if st == 0:
                    S.wait("tensor", "sV", d1s_ready)
                    S.wait("tensor", "sS", ebd1_m)
                for i in range(NCH):
                    for hh in range(2):
                        S.op("tensor", lambda t, i_=i, h_=hh: t.matmul(
                            bankrow(6 + h_, 2, 512), xtn[:, 2 * i_:2 * i_ + 2],
                            trV[:, i_ * 1024 + h_ * 512: i_ * 1024 + (h_ + 1) * 512],
                            start=(i_ == 0), stop=False), sem=None, inc=0)
                for hh in range(2):
                    S.op("tensor", lambda t, h_=hh: t.matmul(
                        bankrow(6 + h_, 2, 512), ones2[:], ebd1[0:1, h_ * 512:(h_ + 1) * 512],
                        start=False, stop=True), sem="sT" if hh == 1 else None, inc=1 if hh == 1 else 0)
                d1m = S.mark("sT")
                d1m_l[st] = d1m
                if st == 3:
                    S.wait("gpsimd", "sT", d1m)
                    for c4 in range(4):
                        S.op("gpsimd", lambda g, d=trV.ap().rearrange("p (c m) -> p c m", c=8)[:, c4 * 2:(c4 + 1) * 2],
                             s=wg_src[:, c4 * 2:(c4 + 1) * 2, :]: g.dma_start(out=d, in_=s), sem="sGV", inc=16)
                    gv_m = S.mark("sGV")
                S.wait("scalar", "sT", d1m)
                S.op("scalar", lambda s: s.activation(tb8[:, 0:512], bankrow(6, 2, 512), AF.Gelu), sem="sS", inc=1)
                S.op("scalar", lambda s: s.activation(tb8[:, 512:1024], bankrow(7, 2, 512), AF.Gelu), sem="sS", inc=1)
                gm = S.mark("sS")
                S.wait("tensor", "sS", gm)
                xe3 = []
                for ii in range(8):
                    bb = 6 + (ii % 2)
                    if ii >= 2:
                        S.wait("tensor", "sV", xe3[ii - 2])
                    S.op("tensor", lambda t, b_=bb, i_=ii:
                         t.transpose(PS.ap()[:, 512 * b_:512 * b_ + 1].bitcast(BF16),
                                     tb8[:, i_ * 128:(i_ + 1) * 128], iden[0:2, 0:2]), sem="sT", inc=1)
                    tm = S.mark("sT")
                    S.wait("vector", "sT", tm)
                    S.op("vector", lambda v, b_=bb, i_=ii: v.tensor_copy(
                        xtn[:, 32 + 2 * i_:32 + 2 * i_ + 2], PS.ap()[:, 512 * b_:512 * b_ + 1].bitcast(BF16)),
                        sem="sV", inc=1)
                    xe3.append(S.mark("sV"))
                tgm = S.mark("sV")
                S.wait("tensor", "sV", tgm)
                if st == 0:
                    S.wait("tensor", "sD2", d2s_in)
                for c in range(8):
                    for q in range(4):
                        S.op("tensor", lambda t, c_=c, q_=q: t.matmul(
                            bankrow(q_, 2, 512), xtn[:, 32 + 2 * c_:32 + 2 * c_ + 2],
                            trQ[:, c_ * 2048 + q_ * 512: c_ * 2048 + (q_ + 1) * 512],
                            start=(c_ == 0), stop=False), sem=None, inc=0)
                for q in range(4):
                    S.op("tensor", lambda t, q_=q: t.matmul(
                        bankrow(q_, 2, 512), ones2[:], d2brow[0:1, q_ * 512:(q_ + 1) * 512],
                        start=False, stop=True), sem="sT" if q == 3 else None, inc=1 if q == 3 else 0)
                d2m = S.mark("sT")
                d2m_l[st] = d2m
                S.wait("vector", "sT", d2m)
                S.op("vector", lambda v: v.tensor_copy(urow[:], PS[0:2, 0:2048]), sem="sV", inc=1)
                um = S.mark("sV")
                if st == 3:
                    um3 = um
                S.wait("sync", "sV", um)
                S.op("sync", lambda sy, s_=st: sy.dma_start(out=ar_inh[s_], in_=urow[:]), sem="sDC", inc=16)
                S.wait("gpsimd", "sDC", S.mark("sDC"))
                S.op("gpsimd", lambda g, s_=st: g.collective_compute(
                    "AllReduce", ALU.add, replica_groups=[list(range(NCORES))],
                    ins=[ar_inh[s_]], outs=[ar_outh[s_]]), sem="sCC", inc=1)
                S.wait("sync", "sCC", S.mark("sCC"))
                S.op("sync", lambda sy, s_=st: sy.dma_start(out=urow[:], in_=ar_outh[s_]), sem="sDC", inc=16)
                arm = S.mark("sDC")
                S.wait("vector", "sDC", arm)
                S.op("vector", lambda v: v.tensor_tensor(thought, thought, urow[:], op=ALU.add), sem="sV", inc=1)
                prev = ("sV", S.mark("sV"))
                if st == 3:
                    S.wait("gpsimd", "sT", d2m)
                    S.wait("gpsimd", "sV", prev[1])
                    S.op("gpsimd", lambda g: g.dma_start(out=gw_dst(15), in_=wg_src[:, 15, :]), sem="sGQ", inc=16)
                    S.op("gpsimd", lambda g: g.dma_start(out=gbrow, in_=gatebh), sem="sGQ", inc=16)
                    gq_m = S.mark("sGQ")

            sig_l = [0] * NT
            sigq2_l = [0] * NT
            gpe_l = [0] * NT
            x1m_l = [None] * NT
            ln2_l = [0] * NT
            x2_l = [0] * NT
            out_m = [None] * NT
            gbanks = [[((n * 4) % 6 + q) % 6 for q in range(4)] for n in range(NT)]

            def gw_chunkap(c, q):
                return gw_dst(c)[:, q * 512:(q + 1) * 512]

            def gate_pe_half(n, half):
                tt = TORD[n]
                banks = gbanks[n]
                q0 = 2 * half
                if half == 0:
                    if n == 0:
                        S.wait("tensor", "sGV", gv_m)
                        S.wait("tensor", "sGW", gw_early)
                        S.wait("tensor", "sV", max(ftr_done))
                        S.wait("tensor", "sV", um3)
                    if n >= 2:
                        S.wait("tensor", "sS", sig_l[n - 2])
                else:
                    if n >= 1:
                        S.wait("tensor", "sS", sigq2_l[n - 1])
                for i in range(NCH - 1):
                    for dq in range(2):
                        S.op("tensor", lambda t, l=trA[:, tt * 2048 + i * 128: tt * 2048 + (i + 1) * 128],
                             r=gw_chunkap(i, q0 + dq), bb=banks[q0 + dq], st=(i == 0):
                             t.matmul(bank(bb), l, r, start=st, stop=False), sem=None, inc=0)
                if n == 0 and half == 0:
                    S.wait("tensor", "sGQ", gq_m)
                for dq in range(2):
                    S.op("tensor", lambda t, bb=banks[q0 + dq], q_=q0 + dq: t.matmul(
                        bank(bb), ones128[:], gbrow[0:1, q_ * 512:(q_ + 1) * 512],
                        start=False, stop=False), sem=None, inc=0)
                for dq in range(2):
                    S.op("tensor", lambda t, l=trA[:, tt * 2048 + 15 * 128: tt * 2048 + 16 * 128],
                         r=gw_chunkap(15, q0 + dq), bb=banks[q0 + dq]:
                         t.matmul(bank(bb), l, r, start=False, stop=True),
                         sem="sT" if dq == 1 else None, inc=1 if dq == 1 else 0)
                if half == 1:
                    gpe_l[n] = S.mark("sT")

            def gate_sig(n):
                tt = TORD[n]
                banks = gbanks[n]
                S.wait("scalar", "sT", gpe_l[n])
                for q in range(4):
                    S.op("scalar", lambda s, bb=banks[q], q_=q: s.activation(
                        trA[:, tt * 2048 + q_ * 512: tt * 2048 + (q_ + 1) * 512],
                        bank(bb), AF.Sigmoid), sem="sS", inc=1)
                    if q == 1:
                        sigq2_l[n] = S.mark("sS")
                sig_l[n] = S.mark("sS")
            gate_pe_half(0, 0)

            S.wait("vector", prev[0], prev[1])
            S.op("vector", lambda v: v.tensor_copy(tb8[:, 0:D], thought), sem="sV", inc=1)
            tbm = S.mark("sV")
            S.wait("tensor", "sV", tbm)
            xe4 = []
            for i in range(NCH):
                bb = 6 + (i % 2)
                if i >= 2:
                    S.wait("tensor", "sV", xe4[i - 2])
                S.op("tensor", lambda t, b_=bb, i_=i: t.transpose(
                    PS.ap()[:, 512 * b_:512 * b_ + 1].bitcast(BF16),
                    tb8[:, i_ * 128:(i_ + 1) * 128], iden[0:2, 0:2]), sem="sT", inc=1)
                tm = S.mark("sT")
                S.wait("vector", "sT", tm)
                S.op("vector", lambda v, b_=bb, i_=i: v.tensor_copy(
                    xtn[:, 2 * i_:2 * i_ + 2], PS.ap()[:, 512 * b_:512 * b_ + 1].bitcast(BF16)), sem="sV", inc=1)
                xe4.append(S.mark("sV"))
            btm = S.mark("sV")

            bc_src = bcwh.rearrange("(i p) m -> p i m", p=128)
            bc_used = [0] * 4
            for b4 in range(4):
                sl = b4 % 2
                wsem = f"sW{sl}"
                if b4 >= 2:
                    S.wait("gpsimd", "sT", bc_used[b4 - 2])
                S.op("gpsimd", lambda g, d=agslot[sl].rearrange("p (c m) -> p c m", c=4),
                     s=bc_src[:, b4 * 4:(b4 + 1) * 4]: g.dma_start(out=d, in_=s), sem=wsem, inc=16)
                blkm = (wsem, S.mark(wsem))
                S.waitm("tensor", blkm)
                if b4 == 0:
                    S.wait("tensor", "sV", btm)
                for c in range(4):
                    i = b4 * 4 + c
                    chunk = agslot[sl][:, c * AG_SL:(c + 1) * AG_SL]
                    S.op("tensor", lambda t, i_=i, ch=chunk: t.matmul(
                        bankrow(6, 2, AG_SL), xtn[:, 2 * i_:2 * i_ + 2], ch,
                        start=(i_ == 0), stop=(i_ == NCH - 1)), sem="sT", inc=1)
                bc_used[b4] = S.mark("sT")
                n_, h_ = [(0, 1), (1, 0), (1, 1), (2, 0)][b4]
                gate_pe_half(n_, h_)
                if h_ == 1:
                    gate_sig(n_)
            bc_pe = bc_used[3]
            S.wait("vector", "sT", bc_pe)
            S.op("vector", lambda v: v.tensor_copy(cring[0:2, 0:AG_SL], bankrow(6, 2, AG_SL)), sem="sV", inc=1)
            S.wait("vector", "sV", S.mark("sV"))
            S.op("vector", lambda v: v.tensor_tensor(
                cring[0:2, 0:AG_SL], cring[0:2, 0:AG_SL], bcb2[:], op=ALU.add), sem="sV", inc=1)
            bcm = S.mark("sV")
            S.wait("sync", "sV", bcm)
            S.op("sync", lambda sy: sy.dma_start(out=bc_inh, in_=cring[0:2, 0:AG_SL]), sem="sDC", inc=16)
            S.wait("gpsimd", "sDC", S.mark("sDC"))
            S.op("gpsimd", lambda g: g.collective_compute(
                "AllGather", ALU.bypass, replica_groups=[list(range(NCORES))],
                ins=[bc_inh], outs=[bc_outh]), sem="sCC", inc=1)
            S.wait("sync", "sCC", S.mark("sCC"))
            S.op("sync", lambda sy: sy.dma_start(
                out=cring.ap()[0:2, 0:D].rearrange("b (r s) -> b r s", r=NCORES),
                in_=bc_outh.rearrange("(r b) s -> b r s", b=2)), sem="sDC", inc=16)
            bcrow_m = S.mark("sDC")
            if debug >= 2:
                S.wait("sync", "sDC", bcrow_m)
                S.op("sync", lambda sy: sy.dma_start(out=dbgbc, in_=cring[0:2, 0:D]), sem="sDC", inc=16)
                bcrow_m = S.mark("sDC")
            def emit_sel2_bcbc():
                S.wait("tensor", "sDC", bcrow_m)
                bm = 0
                for p in range(2):
                    if p == 1:
                        S.wait("tensor", "sV", bm)
                    for h in range(2):
                        q = 2 * p + h
                        S.op("tensor", lambda t, b_=6 + h, q_=q: t.matmul(
                            bank(b_), sel2[:], cring[0:2, q_ * 512:(q_ + 1) * 512],
                            start=True, stop=True), sem="sT" if h == 1 else None, inc=1 if h == 1 else 0)
                    pm = S.mark("sT")
                    S.wait("vector", "sT", pm)
                    for h in range(2):
                        q = 2 * p + h
                        S.op("vector", lambda v, b_=6 + h, q_=q: v.tensor_copy(
                            bcbc[:, q_ * 512:(q_ + 1) * 512], bank(b_)), sem="sV", inc=1)
                    bm = S.mark("sV")
                return bm

            S.op("sync", lambda sy: sy.dma_start(out=gamv, in_=gambh), sem="sGB", inc=16)
            S.op("sync", lambda sy: sy.dma_start(out=betv, in_=betbh), sem="sGB", inc=16)
            gb_m = S.mark("sGB")
            S.op("sync", lambda sy: sy.dma_start(
                out=xring[:, 0:D], in_=x1s_tiles[TORD[0]]), sem="sXA", inc=16)
            x1m_l[0] = ("sXA", S.mark("sXA"))
            S.op("sync", lambda sy: sy.dma_start(
                out=xring[:, D:2 * D], in_=x1s_tiles[TORD[1]]), sem="sXB", inc=16)
            x1m_l[1] = ("sXB", S.mark("sXB"))

            def combine_stages(k, bufs=None):
                tt_ = TORD[k]
                sl2 = k % 2
                if bufs is None:
                    xsem_ = "sXA" if sl2 == 0 else "sXB"
                    xt = xring[:, D * sl2: D * (sl2 + 1)]
                    crX = cring[:, 0:D] if sl2 == 0 else cring[:, D:2 * D]
                    scrX = scr2 if sl2 == 0 else scr2b
                    st0 = 0 if sl2 == 0 else 8
                else:
                    crX, scrX, st0, xt, xsem_ = bufs
                gp = trA[:, tt_ * 2048:(tt_ + 1) * 2048]
                mk = {}

                def s_gbc():
                    S.wait("gpsimd", "sS", sig_l[k])
                    if k == 0:
                        S.wait("gpsimd", "sV", bcbc_m)
                    if bufs is None and k >= 2:
                        S.wait("gpsimd", "sV", x2_l[k - 2])
                    S.op("gpsimd", lambda g, g_=gp, c_=crX: g.tensor_tensor(
                        c_, g_, bcbc[:, 0:D], op=ALU.mult), sem="sP", inc=1)
                    mk["gbc"] = S.mark("sP")

                def s_t1():
                    S.waitm("scalar", x1m_l[k])
                    if bufs is not None:
                        pass
                    elif k < 2:
                        S.wait("scalar", "sT", d2m_l[3])
                    else:
                        S.wait("scalar", "sS", ln2_l[k - 2])
                    S.op("scalar", lambda s, x=xt, t_=tt_, d_=scrX: s.activation(
                        d_, x, AF.Identity, bias=stats[:, 40 + t_:41 + t_],
                        scale=stats[:, 32 + t_:33 + t_]), sem="sS", inc=1)
                    mk["t1"] = S.mark("sS")

                def s_outt():
                    S.wait("vector", "sS", mk["t1"])
                    S.wait("vector", "sP", mk["gbc"])
                    S.op("vector", lambda v, c_=crX, d_=scrX: v.tensor_tensor(
                        c_, c_, d_, op=ALU.add), sem="sV", inc=1)
                    mk["outt"] = S.mark("sV")

                def s_stats1():
                    S.wait("vector", "sV", mk["outt"])
                    S.op("vector", lambda v, c_=crX: v.reduce_sum(
                        stats[:, st0:st0 + 1], c_, axis=AX.X), sem="sV", inc=1)
                    mk["red1"] = S.mark("sV")
                    S.wait("scalar", "sV", mk["outt"])
                    S.op("scalar", lambda s, c_=crX, d_=scrX: s.activation(
                        d_, c_, AF.Square, accum_out=stats[:, st0 + 1:st0 + 2]), sem="sS", inc=1)
                    mk["sq1"] = S.mark("sS")

                def s_ln1():
                    S.wait("vector", "sS", mk["sq1"])
                    S.wait("vector", "sV", mk["red1"])
                    mk["nb2"] = ln_stats(EPSF, st0, st0 + 1, st0 + 2, st0 + 3, st0 + 4)

                def s_yh():
                    S.wait("scalar", "sV", mk["nb2"])
                    S.op("scalar", lambda s, c_=crX, d_=scrX: s.activation(
                        d_, c_, AF.Identity, bias=stats[:, st0 + 4:st0 + 5],
                        scale=stats[:, st0 + 3:st0 + 4]), sem="sS", inc=1)
                    mk["yh"] = S.mark("sS")

                def s_gmul():
                    S.wait("vector", "sS", mk["yh"])
                    if k == 0:
                        S.wait("vector", "sGB", gb_m)
                    S.op("vector", lambda v, d_=scrX: v.tensor_tensor(
                        d_, d_, gamv, op=ALU.mult), sem="sV", inc=1)
                    mk["gmul"] = S.mark("sV")

                def s_beta():
                    S.wait("gpsimd", "sV", mk["gmul"])
                    if k == 0:
                        S.wait("gpsimd", "sGB", gb_m)
                    S.op("gpsimd", lambda g, d_=scrX: g.tensor_tensor(
                        d_, d_, betv, op=ALU.add), sem="sP", inc=1)
                    mk["y"] = S.mark("sP")

                def s_stats2():
                    if k < 4:
                        S.wait("vector", "sP", mk["y"])
                        S.op("vector", lambda v, d_=scrX: v.reduce_sum(
                            stats[:, st0:st0 + 1], d_, axis=AX.X), sem="sV", inc=1)
                        mk["red2"] = ("sV", S.mark("sV"))
                    S.wait("scalar", "sP", mk["y"])
                    S.op("scalar", lambda s, c_=crX, d_=scrX: s.activation(
                        c_, d_, AF.Square, accum_out=stats[:, st0 + 1:st0 + 2]), sem="sS", inc=1)
                    if k >= 4:
                        S.op("scalar", lambda s, c_=crX, d_=scrX: s.activation(
                            c_, d_, AF.Identity, accum_out=stats[:, st0:st0 + 1]), sem="sS", inc=1)
                        mk["red2"] = ("sS", S.mark("sS"))
                    mk["sq2"] = S.mark("sS")

                def s_ln2():
                    S.wait("vector", "sS", mk["sq2"])
                    S.waitm("vector", mk["red2"])
                    mk["nb3"] = ln_stats(EPS0, st0, st0 + 1, st0 + 2, st0 + 3, st0 + 4)

                def s_t3():
                    S.wait("scalar", "sV", mk["nb3"])
                    S.op("scalar", lambda s, c_=crX, d_=scrX: s.activation(
                        c_, d_, AF.Identity, bias=stats[:, st0 + 4:st0 + 5],
                        scale=stats[:, st0 + 3:st0 + 4]), sem="sS", inc=1)
                    ln2_l[k] = S.mark("sS")

                def s_x2dma():
                    S.wait("vector", "sS", ln2_l[k])
                    S.op("vector", lambda v, x=xt, c_=crX: v.tensor_tensor(x, x, c_, op=ALU.add), sem="sV", inc=1)
                    x2_l[k] = S.mark("sV")
                    S.wait("sync", "sV", x2_l[k])
                    S.op("sync", lambda sy, x=xt, t_=tt_: sy.dma_start(
                        out=xout_tiles[t_], in_=x), sem=xsem_, inc=16)
                    out_m[k] = (xsem_, S.mark(xsem_))
                    if k + 2 < 6:
                        S.waitm("sync", out_m[k])
                        S.op("sync", lambda sy, sl=sl2, t_=TORD[k + 2]: sy.dma_start(
                            out=xring[:, D * sl: D * (sl + 1)], in_=x1s_tiles[t_]), sem=xsem_, inc=16)
                        x1m_l[k + 2] = (xsem_, S.mark(xsem_))

                return [s_gbc, s_t1, s_outt, s_stats1, s_ln1, s_yh, s_gmul,
                        s_beta, s_stats2, s_ln2, s_t3, s_x2dma]

            def emit_combine_pair(ka, kb):
                for a, b in zip(combine_stages(ka), combine_stages(kb)):
                    a()
                    b()

            def emit_combine_quad():
                crC = trQ.ap()[:, 8192:12288].bitcast(F32)
                scrC = wring.ap()[:, 0:4096].bitcast(F32)
                crD = wring.ap()[:, 4096:8192].bitcast(F32)
                scrD = trV.ap()[:, 0:4096].bitcast(F32)
                xC = trV.ap()[:, 4096:8192].bitcast(F32)
                xD = trV.ap()[:, 8192:12288].bitcast(F32)
                S.wait("sync", "sT", gpe_l[7])
                S.op("sync", lambda sy: sy.dma_start(
                    out=xC, in_=x1s_tiles[TORD[6]]), sem="sDC", inc=16)
                x1m_l[6] = ("sDC", S.mark("sDC"))
                S.op("sync", lambda sy: sy.dma_start(
                    out=xD, in_=x1s_tiles[TORD[7]]), sem="sXC", inc=16)
                x1m_l[7] = ("sXC", S.mark("sXC"))
                stl = [combine_stages(4), combine_stages(5),
                       combine_stages(6, bufs=(crC, scrC, 16, xC, "sXA")),
                       combine_stages(7, bufs=(crD, scrD, 24, xD, "sXB"))]
                for stage in range(len(stl[0])):
                    for t4 in range(4):
                        stl[t4][stage]()

            gate_pe_half(2, 1)
            gate_sig(2)
            bcbc_m = emit_sel2_bcbc()
            gate_pe_half(3, 0)
            gate_pe_half(3, 1)
            gate_sig(3)
            emit_combine_pair(0, 1)
            for n in (4, 5):
                gate_pe_half(n, 0)
                gate_pe_half(n, 1)
                gate_sig(n)
            emit_combine_pair(2, 3)
            for n in (6, 7):
                gate_pe_half(n, 0)
                gate_pe_half(n, 1)
                gate_sig(n)
            emit_combine_quad()
            S.wait("sync", "sXA", S.mark("sXA"))
            S.wait("sync", "sXB", S.mark("sXB"))
            S.wait("gpsimd", "sXA", S.mark("sXA"))
            S.wait("gpsimd", "sXB", S.mark("sXB"))

        @block.tensor
        def _(t):
            S.emit("tensor", t, sems)

        @block.vector
        def _(v):
            S.emit("vector", v, sems)

        @block.scalar
        def _(s):
            S.emit("scalar", s, sems)

        @block.gpsimd
        def _(g):
            S.emit("gpsimd", g, sems)

        @block.sync
        def _(sy):
            S.emit("sync", sy, sems)

    return nc


_BF = ml_dtypes.bfloat16


def _prep_inputs(inp):
    x = np.asarray(inp["L0_attn_x"], np.float32).reshape(2 * 4096, D)
    cores = []
    iden = np.eye(128, dtype=_BF)
    maskc = np.kron(np.eye(8, dtype=np.float32), np.ones((16, 16), np.float32))
    ones2 = np.ones((1, 2), dtype=_BF)
    ones128 = np.ones((1, 128), dtype=_BF)
    tr = lambda v: np.ascontiguousarray(np.asarray(v, np.float32).reshape(NCH, 128).T)
    dup2 = lambda m: np.ascontiguousarray(np.repeat(m, 2, axis=1))
    pngt = tr(inp["L0_ffn_pn_gamma"])
    pnbt2 = dup2(tr(inp["L0_ffn_pn_beta"]))
    ingt = tr(inp["L0_ffn_in_gamma"])
    inbt2 = dup2(tr(inp["L0_ffn_in_beta"]))
    gamb = np.ascontiguousarray(np.broadcast_to(np.asarray(inp["L0_ffn_out_gamma"], np.float32), (128, D)))
    betb = np.ascontiguousarray(np.broadcast_to(np.asarray(inp["L0_ffn_out_beta"], np.float32), (128, D)))
    d2b_full = np.asarray(inp["L0_ffn_d2_b"], np.float32).reshape(1, D)
    zeros_d2b = np.zeros_like(d2b_full)
    for c in range(NCORES):
        b = c // 4
        sel2 = np.zeros((2, 128), np.float32)
        sel2[b, :] = 1.0
        m = {
            "x0": np.ascontiguousarray(x[c * T:(c + 1) * T]),
            "wq": np.asarray(inp["L0_attn_Wq"], np.float32),
            "wk": np.asarray(inp["L0_attn_Wk"], np.float32),
            "wv": np.asarray(inp["L0_attn_Wv"], np.float32),
            "wo": np.asarray(inp["L0_attn_Wo"], np.float32),
            "wg": np.asarray(inp["L0_ffn_gate_w"], np.float32),
            "d1w": np.ascontiguousarray(np.asarray(inp["L0_ffn_d1_w"], np.float32)[:, c * TH_SL:(c + 1) * TH_SL]),
            "d2w": np.ascontiguousarray(np.asarray(inp["L0_ffn_d2_w"], np.float32)[c * TH_SL:(c + 1) * TH_SL, :]),
            "aggw": np.ascontiguousarray(np.asarray(inp["L0_ffn_agg_w"], np.float32)[:, c * AG_SL:(c + 1) * AG_SL]),
            "bcw": np.ascontiguousarray(np.asarray(inp["L0_ffn_bc_w"], np.float32)[:, c * AG_SL:(c + 1) * AG_SL]),
            "d1b": np.ascontiguousarray(np.asarray(inp["L0_ffn_d1_b"], np.float32)[None, c * TH_SL:(c + 1) * TH_SL]),
            "d2b": d2b_full if c == 0 else zeros_d2b,
            "aggb": np.ascontiguousarray(np.broadcast_to(
                np.asarray(inp["L0_ffn_agg_b"], np.float32)[None, c * AG_SL:(c + 1) * AG_SL], (2, AG_SL))),
            "bcb2": np.ascontiguousarray(np.broadcast_to(
                np.asarray(inp["L0_ffn_bc_b"], np.float32)[None, c * AG_SL:(c + 1) * AG_SL], (2, AG_SL))),
            "gateb": np.asarray(inp["L0_ffn_gate_b"], np.float32).reshape(1, D),
            "gamb": gamb, "betb": betb,
            "pngt": pngt, "pnbt2": pnbt2, "ingt": ingt, "inbt2": inbt2,
            "iden": iden, "maskc": maskc, "ones2": ones2, "ones128": ones128,
            "sel2": sel2,
        }
        cores.append(m)
    return cores


def kernel(**inputs):
    in_maps = _prep_inputs(inputs)
    nc = build(stage=STAGE, debug=DEBUG)
    res = run_bass_kernel_spmd(nc, in_maps, core_ids=list(range(NCORES)))
    out = np.concatenate([np.asarray(r["xout"], np.float32) for r in res.results], axis=0)
    return out.reshape(2, 4096, D)
